# version 21
# baseline (speedup 1.0000x reference)
import contextlib
import numpy as np
import ml_dtypes
import concourse.bass as bass
import concourse.mybir as mybir
from concourse.bass_utils import run_bass_kernel_spmd

F32 = mybir.dt.float32
BF16 = mybir.dt.bfloat16
AF = mybir.ActivationFunctionType
ALU = mybir.AluOpType
AX = mybir.AxisListType

FULL_CFG = dict(T=4096, C=2048, D=4096, H=5120, TB=2048, CH1=256, CH2=512)
EPS = 1e-5


def _w(eng, s, v):
    if v > 0:
        eng.wait_ge(s, v)


def rinc(instr, ring, i):
    instr.then_inc(ring[i % len(ring)], 16)


def rwait(eng, ring, i):
    if i >= 0:
        n = len(ring)
        eng.wait_ge(ring[i % n], 16 * (i // n + 1))


def rwait_all(eng, ring, n):
    r = len(ring)
    for s in range(r):
        c = (n - s + r - 1) // r
        if c > 0:
            eng.wait_ge(ring[s], 16 * c)


def _bn(vec, st6, dst_mv, src, width, self_sem, ctr):
    nb = max(1, width // 512)
    step = width // nb
    assert step * nb == width and step <= 512 and step % 2 == 0
    for j in range(nb):
        ins = vec.bn_stats(st6[:, 6 * j:6 * j + 6], src[:, j * step:(j + 1) * step])
    ins.then_inc(self_sem, 1)
    ctr += 1
    vec.wait_ge(self_sem, ctr)
    vec.bn_aggr(dst_mv, st6[:, 0:6 * nb]).then_inc(self_sem, 1)
    return ctr + 1


def build_program(cfg):
    T, C, D, H = cfg["T"], cfg["C"], cfg["D"], cfg["H"]
    TB, CH1, CH2 = cfg["TB"], cfg["CH1"], cfg["CH2"]

    NC = C // 128
    ND = D // 128
    NG = NC
    HALF = NG // 2
    NH = H // 128
    NHB = NH // 2
    NCH = T // CH1
    NQ = T // CH2
    TT1 = CH1 // 128
    TT2 = CH2 // 128
    RESET = TB // CH1
    GPC = 2 * HALF
    NT128 = T // 128

    nc = bass.Bass("TRN2", target_bir_lowering=False)

    x_d = nc.dram_tensor("x", [T, C], F32, kind="ExternalInput").ap()
    w_d = nc.dram_tensor("W", [D, C], F32, kind="ExternalInput").ap()
    bT_d = nc.dram_tensor("bT", [128, ND], F32, kind="ExternalInput").ap()
    wfc_d = nc.dram_tensor("wfc", [H, C], F32, kind="ExternalInput").ap()
    wpj_d = nc.dram_tensor("wpj", [C, H], F32, kind="ExternalInput").ap()
    id32_d = nc.dram_tensor("id32", [128, 128], F32, kind="ExternalInput").ap()
    idbf_d = nc.dram_tensor("idbf", [128, 128], BF16, kind="ExternalInput").ap()
    out_d = nc.dram_tensor("out", [T, C], F32, kind="ExternalOutput").ap()
    wbf_d = nc.dram_tensor("wbf_s", [D, C], BF16).ap()
    xn_d = nc.dram_tensor("xn_s", [T, C], BF16).ap()
    h_d = None
    x1_d = nc.dram_tensor("x1_s", [T, C], F32).ap()
    xn1_d = nc.dram_tensor("xn1_s", [T, C], BF16).ap()
    h_t = nc.dram_tensor("h_t", [T // CH1, 128, C // 128, CH1], F32).ap()
    sfcT_t = nc.dram_tensor("sfcT_t", [H // 256, 128, C // 128, 256], BF16).ap()
    spjT_t = nc.dram_tensor("spjT_t", [C // 128, 128, H // 128, 128], BF16).ap()

    x_ch1 = x_d.rearrange("(k tt p) c -> k p tt c", p=128, tt=TT1)
    xn_ch1 = xn_d.rearrange("(k tt p) c -> k p tt c", p=128, tt=TT1)
    x1_ch1 = x1_d.rearrange("(k tt p) c -> k p tt c", p=128, tt=TT1)
    x1_ch2 = x1_d.rearrange("(q tt p) c -> q p tt c", p=128, tt=TT2)
    xn1_ch2 = xn1_d.rearrange("(q tt p) c -> q p tt c", p=128, tt=TT2)
    out_rows = out_d.rearrange("(n p) c -> n p c", p=128)

    est = contextlib.ExitStack()
    sbuf = lambda n, s, d: est.enter_context(nc.sbuf_tensor(n, s, d))
    sem = lambda n: est.enter_context(nc.semaphore(n))

    with est:
        bT_sb = sbuf("bT_sb", [128, ND], F32)
        id32 = sbuf("id32_sb", [128, 128], F32)
        idbf = sbuf("idbf_sb", [128, 128], BF16)
        ones_row = sbuf("ones_row", [1, 128], F32)
        am_fc = sbuf("am_fc", [128, 1], F32)
        thr_fc = sbuf("thr_fc", [128, 1], F32)
        nthr_fc = sbuf("nthr_fc", [128, 1], F32)
        am_pj = sbuf("am_pj", [128, 1], F32)
        thr_pj = sbuf("thr_pj", [128, 1], F32)
        nthr_pj = sbuf("nthr_pj", [128, 1], F32)
        amsq_pj = sbuf("amsq_pj", [128, 1], F32)
        tot1 = sbuf("tot1", [1, 2], F32)
        tot2 = sbuf("tot2", [1, 2], F32)
        red1 = sbuf("red1", [128, 1], F32)
        carry = sbuf("carry", [128, NG], F32)

        ring = lambda n, r=2: [sem(n + str(i)) for i in range(r)]
        d_prep = sem("d_prep"); d_cast = sem("d_cast"); v_prep = sem("v_prep")
        d_wf = ring("d_wf"); v_abs = sem("v_abs"); fz = sem("fz")
        d_sf = ring("d_sf"); v_sg = sem("v_sg"); g_sg = sem("g_sg")
        p_tr2 = sem("p_tr2"); s_ev2 = sem("s_ev2"); d_sw = ring("d_sw")
        d_wt3 = sem("d_wt3"); d_wt5 = sem("d_wt5")
        d_x4 = ring("d_x4"); vs4 = sem("vs4"); s_sq4 = sem("s_sq4")
        v_lna = sem("v_lna"); v_lnb = sem("v_lnb"); s_nm4 = sem("s_nm4")
        d_xnw4 = ring("d_xnw4")
        d_xnT4 = ring("d_xnT4", 3); p_mm4 = sem("p_mm4"); s_ev4 = sem("s_ev4")
        v_sc4 = sem("v_sc4"); v_cr4 = sem("v_cr4"); d_hw4 = ring("d_hw4")
        d_xnT6 = ring("d_xnT6"); p_mm6 = sem("p_mm6"); s_ev6 = sem("s_ev6")
        v_sc6 = sem("v_sc6"); v_cr6 = sem("v_cr6"); d_hw6 = ring("d_hw6")
        d_hr7 = ring("d_hr7"); d_xr7 = ring("d_xr7"); p_tr7 = sem("p_tr7")
        vs7 = sem("vs7"); s_sq7 = sem("s_sq7"); v_nm7 = sem("v_nm7")
        g_xp7 = sem("g_xp7"); v_out7 = sem("v_out7"); d_x1w7 = ring("d_x1w7")
        d_x1a = ring("d_x1a"); d_x1b = ring("d_x1b")
        vs8 = sem("vs8"); s_sq8 = sem("s_sq8")
        v_st8 = sem("v_st8"); v_nm3 = sem("v_nm3"); s_nm8 = sem("s_nm8")
        d_xn1w = ring("d_xn1w"); d_xn1T = ring("d_xn1T")
        d_sfc = ring("d_sfc", 3); d_spr = ring("d_spr")
        p_fc = sem("p_fc"); s_rev = sem("s_rev"); s_sqq = sem("s_sqq")
        p_pj = sem("p_pj"); s_yev = sem("s_yev"); p_yt = sem("p_yt")
        s_sq8b = sem("s_sq8b"); v_fold8 = sem("v_fold8")
        g_xp8 = sem("g_xp8"); v_o8 = sem("v_o8"); d_out8 = ring("d_out8", 2)

        with nc.Block() as block:
            @block.sync
            def _(sync):
                sync.dma_start(out=bT_sb[:, :], in_=bT_d[:, :]).then_inc(d_prep, 16)
                sync.dma_start(out=id32[:, :], in_=id32_d[:, :]).then_inc(d_prep, 16)
                sync.dma_start(out=idbf[:, :], in_=idbf_d[:, :]).then_inc(d_prep, 16)

            @block.gpsimd
            def _(gpsimd):
                q4 = D // 4
                for i in range(4):
                    gpsimd.dma_start(out=wbf_d[i * q4:(i + 1) * q4, :],
                                     in_=w_d[i * q4:(i + 1) * q4, :]).then_inc(d_cast, 16)

            @block.vector
            def _(vector):
                vector.memset(ones_row[:, :], 1.0).then_inc(v_prep, 1)

            with contextlib.ExitStack() as s12:
                sb12 = lambda n, s, d: s12.enter_context(nc.sbuf_tensor(n, s, d))
                ps12 = lambda n, s, d: s12.enter_context(nc.psum_tensor(n, s, d))
                MW = max(C, H)
                wslab = sb12("wslab", [128, 2, MW], F32)
                acc_fc = sb12("acc_fc", [128, NH], F32)
                acc_pj = sb12("acc_pj", [128, NC], F32)
                nbs = sb12("nbs", [128, 2, MW], BF16)
                ssl = sb12("ssl", [128, 2, MW], BF16)
                stage_fc = sb12("stage_fc", [128, 2, NC, 256], BF16)
                stage_pj = sb12("stage_pj", [128, 2, NH, 128], BF16)
                ps_misc = ps12("ps_misc", [128, 512], F32)
                ps_bf = [ps12("ps_bf0", [128, 1024], BF16),
                         ps12("ps_bf1", [128, 1024], BF16)]

                @block.sync
                def _(sync):
                    for i in range(NH):
                        _w(sync, v_abs, i - 1)
                        rinc(sync.dma_start(out=wslab[:, i % 2, 0:C],
                                            in_=wfc_d[i * 128:(i + 1) * 128, :]), d_wf, i)
                    for j in range(NC):
                        _w(sync, v_abs, NH + j - 1)
                        rinc(sync.dma_start(out=wslab[:, j % 2, 0:H],
                                            in_=wpj_d[j * 128:(j + 1) * 128, :]), d_wf, NH + j)

                @block.vector
                def _(vector):
                    for i in range(NH):
                        rwait(vector, d_wf, i)
                        vector.tensor_reduce(out=acc_fc[:, i:i + 1], in_=wslab[:, i % 2, 0:C],
                                             axis=AX.X, op=ALU.add,
                                             apply_absolute_value=True).then_inc(v_abs, 1)
                    for j in range(NC):
                        rwait(vector, d_wf, NH + j)
                        vector.tensor_reduce(out=acc_pj[:, j:j + 1], in_=wslab[:, j % 2, 0:H],
                                             axis=AX.X, op=ALU.add,
                                             apply_absolute_value=True).then_inc(v_abs, 1)

                fzc = [0]

                def vst(vec, f):
                    _w(vec, fz, fzc[0])
                    f().then_inc(fz, 1)
                    fzc[0] += 1

                def tst(tensor, f):
                    _w(tensor, fz, fzc[0])
                    f().then_inc(fz, 1)
                    fzc[0] += 1

                @block.vector
                def _(vector):
                    vector.wait_ge(v_abs, NH + NC)
                    vst(vector, lambda: vector.tensor_reduce(
                        out=red1[:, 0:1], in_=acc_fc[:, 0:NH], axis=AX.X, op=ALU.add))

                @block.tensor
                def _(tensor):
                    tensor.wait_ge(d_prep, 48)
                    tensor.wait_ge(v_prep, 1)
                    tst(tensor, lambda: tensor.transpose(
                        ps_misc[0:1, 0:128], red1[:, 0:1], id32[:, :]))

                @block.vector
                def _(vector):
                    vst(vector, lambda: vector.tensor_reduce(
                        out=tot1[0:1, 0:1], in_=ps_misc[0:1, 0:128], axis=AX.X, op=ALU.add))
                    vst(vector, lambda: vector.tensor_scalar(
                        out=tot2[0:1, 0:1], in0=tot1[0:1, 0:1],
                        scalar1=1.0 / (H * C), scalar2=EPS, op0=ALU.mult, op1=ALU.max))

                @block.tensor
                def _(tensor):
                    tst(tensor, lambda: tensor.matmul(
                        ps_misc[0:128, 256:257], ones_row[0:1, 0:128],
                        tot2[0:1, 0:1], start=True, stop=True))

                @block.vector
                def _(vector):
                    vst(vector, lambda: vector.tensor_copy(
                        out=am_fc[:, 0:1], in_=ps_misc[0:128, 256:257]))
                    vst(vector, lambda: vector.tensor_scalar(
                        out=thr_fc[:, 0:1], in0=am_fc[:, 0:1], scalar1=0.7,
                        scalar2=None, op0=ALU.mult))
                    vst(vector, lambda: vector.tensor_scalar(
                        out=nthr_fc[:, 0:1], in0=thr_fc[:, 0:1], scalar1=-1.0,
                        scalar2=None, op0=ALU.mult))
                    vst(vector, lambda: vector.tensor_reduce(
                        out=red1[:, 0:1], in_=acc_pj[:, 0:NC], axis=AX.X, op=ALU.add))

                @block.tensor
                def _(tensor):
                    tst(tensor, lambda: tensor.transpose(
                        ps_misc[0:1, 0:128], red1[:, 0:1], id32[:, :]))

                @block.vector
                def _(vector):
                    vst(vector, lambda: vector.tensor_reduce(
                        out=tot1[0:1, 1:2], in_=ps_misc[0:1, 0:128], axis=AX.X, op=ALU.add))
                    vst(vector, lambda: vector.tensor_scalar(
                        out=tot2[0:1, 1:2], in0=tot1[0:1, 1:2],
                        scalar1=1.0 / (H * C), scalar2=EPS, op0=ALU.mult, op1=ALU.max))

                @block.tensor
                def _(tensor):
                    tst(tensor, lambda: tensor.matmul(
                        ps_misc[0:128, 256:257], ones_row[0:1, 0:128],
                        tot2[0:1, 1:2], start=True, stop=True))

                @block.vector
                def _(vector):
                    vst(vector, lambda: vector.tensor_copy(
                        out=am_pj[:, 0:1], in_=ps_misc[0:128, 256:257]))
                    vst(vector, lambda: vector.tensor_scalar(
                        out=thr_pj[:, 0:1], in0=am_pj[:, 0:1], scalar1=0.7,
                        scalar2=None, op0=ALU.mult))
                    vst(vector, lambda: vector.tensor_scalar(
                        out=nthr_pj[:, 0:1], in0=thr_pj[:, 0:1], scalar1=-1.0,
                        scalar2=None, op0=ALU.mult))
                    vst(vector, lambda: vector.tensor_scalar(
                        out=amsq_pj[:, 0:1], in0=am_pj[:, 0:1], scalar1=am_pj[:, 0:1],
                        scalar2=None, op0=ALU.mult))

                FZ_END = fzc[0]

                @block.sync
                def _(sync):
                    for i in range(NH):
                        if i < 2:
                            sync.wait_ge(v_abs, NH + NC)
                        else:
                            sync.wait_ge(v_sg, 2 * (i - 2) + 2)
                        rinc(sync.dma_start(out=wslab[:, i % 2, 0:C],
                                            in_=wfc_d[i * 128:(i + 1) * 128, :]), d_sf, i)
                    for j in range(NC):
                        if j < 2:
                            sync.wait_ge(v_sg, 2 * (NH - 2 + j) + 2)
                        else:
                            sync.wait_ge(g_sg, 2 * (j - 2) + 2)
                        rinc(sync.dma_start(out=wslab[:, j % 2, 0:H],
                                            in_=wpj_d[j * 128:(j + 1) * 128, :]), d_sf, NH + j)

                @block.vector
                def _(vector):
                    vector.wait_ge(fz, FZ_END)
                    for i in range(NH):
                        rwait(vector, d_sf, i)
                        _w(vector, p_tr2, NC * (i - 1))
                        vector.tensor_scalar(
                            out=nbs[:, i % 2, 0:C], in0=wslab[:, i % 2, 0:C],
                            scalar1=nthr_fc[:, 0:1], scalar2=-1.0,
                            op0=ALU.is_le, op1=ALU.mult).then_inc(v_sg, 1)
                        vector.wait_ge(v_sg, 2 * i + 1)
                        vector.scalar_tensor_tensor(
                            out=ssl[:, i % 2, 0:C], in0=wslab[:, i % 2, 0:C],
                            scalar=thr_fc[:, 0:1], in1=nbs[:, i % 2, 0:C],
                            op0=ALU.is_ge, op1=ALU.add).then_inc(v_sg, 1)
                    for j in range(NC):
                        rwait(vector, d_sf, NH + j)
                        if j < 2:
                            vector.wait_ge(p_tr2, NC * (NH - 1 + j))
                        else:
                            vector.wait_ge(p_tr2, NC * NH + NH * (j - 1))
                        vector.tensor_scalar(
                            out=nbs[:, j % 2, 0:H], in0=wslab[:, j % 2, 0:H],
                            scalar1=nthr_pj[:, 0:1], scalar2=-1.0,
                            op0=ALU.is_le, op1=ALU.mult).then_inc(g_sg, 1)
                        vector.wait_ge(g_sg, 2 * j + 1)
                        vector.scalar_tensor_tensor(
                            out=ssl[:, j % 2, 0:H], in0=wslab[:, j % 2, 0:H],
                            scalar=thr_pj[:, 0:1], in1=nbs[:, j % 2, 0:H],
                            op0=ALU.is_ge, op1=ALU.add).then_inc(g_sg, 1)


                @block.tensor
                def _(tensor):
                    tcnt = 0
                    for i in range(NH):
                        tensor.wait_ge(v_sg, 2 * i + 2)
                        for ct in range(NC):
                            _w(tensor, s_ev2, tcnt - 1)
                            tensor.transpose(ps_bf[tcnt % 2][0:128, 0:128],
                                             ssl[:, i % 2, ct * 128:(ct + 1) * 128],
                                             idbf[:, :]).then_inc(p_tr2, 1)
                            tcnt += 1
                    for j in range(NC):
                        tensor.wait_ge(g_sg, 2 * j + 2)
                        for ht in range(NH):
                            _w(tensor, s_ev2, tcnt - 1)
                            tensor.transpose(ps_bf[tcnt % 2][0:128, 0:128],
                                             ssl[:, j % 2, ht * 128:(ht + 1) * 128],
                                             idbf[:, :]).then_inc(p_tr2, 1)
                            tcnt += 1

                @block.scalar
                def _(scalar):
                    tcnt = 0
                    for i in range(NH):
                        hb, hoff = i // 2, (i % 2) * 128
                        if i % 2 == 0:
                            rwait(scalar, d_sw, hb - 2)
                        for ct in range(NC):
                            scalar.wait_ge(p_tr2, tcnt + 1)
                            scalar.copy(out=stage_fc[:, hb % 2, ct, hoff:hoff + 128],
                                        in_=ps_bf[tcnt % 2][0:128, 0:128]).then_inc(s_ev2, 1)
                            tcnt += 1
                    for j in range(NC):
                        rwait(scalar, d_sw, NHB + j - 2)
                        for ht in range(NH):
                            scalar.wait_ge(p_tr2, tcnt + 1)
                            scalar.copy(out=stage_pj[:, j % 2, ht, :],
                                        in_=ps_bf[tcnt % 2][0:128, 0:128]).then_inc(s_ev2, 1)
                            tcnt += 1

                @block.gpsimd
                def _(gpsimd):
                    for hb in range(NHB):
                        gpsimd.wait_ge(s_ev2, NC * (2 * hb + 2))
                        rinc(gpsimd.dma_start(out=sfcT_t[hb],
                                         in_=stage_fc[:, hb % 2, :, :]), d_sw, hb)
                    for j in range(NC):
                        gpsimd.wait_ge(s_ev2, NC * NH + NH * (j + 1))
                        rinc(gpsimd.dma_start(out=spjT_t[j],
                                         in_=stage_pj[:, j % 2, :, :]), d_sw, NHB + j)

            S2_DONE = 16 * (NH + NC)

            with contextlib.ExitStack() as sp1:
                sb1 = lambda n, s, d: sp1.enter_context(nc.sbuf_tensor(n, s, d))
                wt_sb = sb1("wt_sb", [128, NC, 2 * NG * 128], BF16)
                x_sb = sb1("x_sb", [128, 1, TT1, C], F32)
                xnb_sb = sb1("xnb_sb", [128, 1, TT1, C], BF16)
                xnT_sb = sb1("xnT_sb", [128, 3, NC, CH1], BF16)
                g_sb = sb1("g_sb", [128, 2, CH1], F32)
                v_sb = sb1("v_sb", [128, 2, CH1], F32)
                h_sb = sb1("h_sb", [128, 1, NG, CH1], F32)
                st6a = sb1("st6a", [128, TT1, 24], F32)
                mv1 = sb1("mv1", [128, TT1, 2], F32)
                veps1 = sb1("veps1", [128, TT1], F32)
                sd1 = sb1("sd1", [128, TT1], F32)
                rs1 = sb1("rs1", [128, TT1], F32)
                nmrs1 = sb1("nmrs1", [128, TT1], F32)
                ps_par = [sp1.enter_context(nc.psum_tensor(f"ps_par{i}", [128, 512], F32))
                          for i in range(4)]
                GP1 = 2 * NG

                @block.sync
                def _(sync):
                    sync.wait_ge(d_cast, 64)
                    rwait_all(sync, d_sw, NHB + NC)
                    for ct in range(NC):
                        sync.dma_start(out=wt_sb[:, ct, 0:NG * 128],
                                       in_=wbf_d[0:C, ct * 128:(ct + 1) * 128],
                                       transpose=True).then_inc(d_wt3, 16)
                        sync.dma_start(out=wt_sb[:, ct, NG * 128:2 * NG * 128],
                                       in_=wbf_d[C:D, ct * 128:(ct + 1) * 128],
                                       transpose=True).then_inc(d_wt3, 16)

                @block.sync
                def _(sync):
                    for k in range(NCH):
                        _w(sync, s_nm4, TT1 * k)
                        rinc(sync.dma_start(out=x_sb[:, 0, :, :],
                                       in_=x_ch1[k]), d_x4, k)
                        rwait(sync, d_xnw4, k)
                        _w(sync, p_mm4, GP1 * (k - 2))
                        for ct in range(NC):
                            sync.dma_start(
                                out=xnT_sb[:, k % 3, ct, :],
                                in_=xn_d[k * CH1:(k + 1) * CH1, ct * 128:(ct + 1) * 128],
                                transpose=True).then_inc(d_xnT4[k % 3], 16)

                @block.vector
                def _(vector):
                    vcnt = 0

                    def stats_seg(vector, k, vcnt):
                        rwait(vector, d_x4, k)
                        for tt in range(TT1):
                            vcnt = _bn(vector, st6a[:, tt, :], mv1[:, tt, :],
                                       x_sb[:, 0, tt, :], C, vs4, vcnt)
                        vector.wait_ge(vs4, vcnt)
                        _w(vector, s_sq4, k)
                        for tt in range(TT1):
                            ins = vector.tensor_scalar(
                                out=veps1[:, tt:tt + 1], in0=mv1[:, tt, 1:2],
                                scalar1=EPS, scalar2=None, op0=ALU.add)
                        ins.then_inc(v_lna, 1)
                        vector.wait_ge(s_sq4, k + 1)
                        _w(vector, s_nm4, TT1 * k)
                        vector.reciprocal(out=rs1[:, 0:TT1],
                                          in_=sd1[:, 0:TT1]).then_inc(vs4, 1)
                        vcnt += 1
                        vector.wait_ge(vs4, vcnt)
                        for tt in range(TT1):
                            ins = vector.scalar_tensor_tensor(
                                out=nmrs1[:, tt:tt + 1], in0=mv1[:, tt, 0:1],
                                scalar=-1.0, in1=rs1[:, tt:tt + 1],
                                op0=ALU.mult, op1=ALU.mult)
                        ins.then_inc(v_lnb, 1)
                        return vcnt

                    def scan_seg(vector, k):
                        for pair in range(NG):
                            gp = k * NG + pair
                            vector.wait_ge(s_ev4, k * GP1 + 2 * pair + 2)
                            rwait(vector, d_hw4, k - 1)
                            if k % RESET == 0:
                                init = 0.0
                            else:
                                vector.wait_ge(v_cr4, (k - 1) * NG + pair + 1)
                                init = carry[:, pair:pair + 1]
                            vector.tensor_tensor_scan(
                                out=h_sb[:, 0, pair, :],
                                data0=g_sb[:, gp % 2, :], data1=v_sb[:, gp % 2, :],
                                initial=init, op0=ALU.mult,
                                op1=ALU.add).then_inc(v_sc4, 1)
                            vector.wait_ge(v_sc4, gp + 1)
                            vector.tensor_copy(
                                out=carry[:, pair:pair + 1],
                                in_=h_sb[:, 0, pair, CH1 - 1:CH1]).then_inc(v_cr4, 1)

                    vcnt = stats_seg(vector, 0, vcnt)
                    for k in range(1, NCH):
                        vcnt = stats_seg(vector, k, vcnt)
                        scan_seg(vector, k - 1)
                    scan_seg(vector, NCH - 1)

                @block.scalar
                def _(scalar):
                    scalar.wait_ge(d_prep, 48)

                    def ln1_seg(scalar, k):
                        scalar.wait_ge(v_lna, k + 1)
                        scalar.sqrt(sd1[:, 0:TT1],
                                    veps1[:, 0:TT1]).then_inc(s_sq4, 1)
                        scalar.wait_ge(v_lnb, k + 1)
                        rwait(scalar, d_xnw4, k - 1)
                        for tt in range(TT1):
                            scalar.activation(
                                out=xnb_sb[:, 0, tt, :],
                                in_=x_sb[:, 0, tt, :], func=AF.Identity,
                                bias=nmrs1[:, tt:tt + 1],
                                scale=rs1[:, tt:tt + 1]).then_inc(s_nm4, 1)
                        scalar.wait_ge(s_nm4, TT1 * (k + 1))
                        rinc(scalar.dma_start(out=xn_ch1[k],
                                         in_=xnb_sb[:, 0, :, :]), d_xnw4, k)

                    def evac_seg(scalar, k):
                        for pair in range(NG):
                            for role in range(2):
                                gi = k * GP1 + 2 * pair + role
                                gp = k * NG + pair
                                scalar.wait_ge(p_mm4, gi + 1)
                                _w(scalar, v_sc4, gp - 1)
                                if role == 0:
                                    scalar.activation(
                                        out=g_sb[:, gp % 2, :],
                                        in_=ps_par[gi % 4][:, 0:CH1],
                                        func=AF.Sigmoid,
                                        bias=bT_sb[:, pair:pair + 1]).then_inc(s_ev4, 1)
                                else:
                                    scalar.activation(
                                        out=v_sb[:, gp % 2, :],
                                        in_=ps_par[gi % 4][:, 0:CH1],
                                        func=AF.Tanh,
                                        bias=bT_sb[:, NG + pair:NG + pair + 1]).then_inc(s_ev4, 1)

                    ln1_seg(scalar, 0)
                    for k in range(1, NCH):
                        ln1_seg(scalar, k)
                        evac_seg(scalar, k - 1)
                    evac_seg(scalar, NCH - 1)

                @block.tensor
                def _(tensor):
                    tensor.wait_ge(d_wt3, 32 * NC)
                    for k in range(NCH):
                        tensor.wait_ge(d_xnT4[k % 3], 16 * NC * (k // 3 + 1))
                        for pair in range(NG):
                            for role in range(2):
                                gi = k * GP1 + 2 * pair + role
                                _w(tensor, s_ev4, gi - 3)
                                col0 = (role * NG + pair) * 128
                                pp = ps_par[gi % 4]
                                for ct in range(NC):
                                    ins = tensor.matmul(
                                        pp[:, 0:CH1],
                                        wt_sb[:, ct, col0:col0 + 128],
                                        xnT_sb[:, k % 3, ct, :],
                                        start=(ct == 0), stop=(ct == NC - 1))
                                ins.then_inc(p_mm4, 1)

                @block.gpsimd
                def _(gpsimd):
                    for k in range(NCH):
                        gpsimd.wait_ge(v_sc4, NG * (k + 1))
                        rinc(gpsimd.dma_start(
                            out=h_t[k][:, 0:NG, :],
                            in_=h_sb[:, 0, :, :]), d_hw4, k)

            with contextlib.ExitStack() as sp7:
                sb7 = lambda n, s, d: sp7.enter_context(nc.sbuf_tensor(n, s, d))
                htin_sb = sb7("htin_sb", [128, 2, NG, CH1], F32)
                x2_sb = sb7("x2_sb", [128, 2, TT1, C], F32)
                xp_sb = sb7("xp_sb", [128, 2, C], F32)
                x1o_sb = sb7("x1o_sb", [128, 2, TT1, C], F32)
                st6b = sb7("st6b", [128, 24], F32)
                mv2 = sb7("mv2", [128, 2], F32)
                veps2 = sb7("veps2", [128, 1], F32)
                sd2 = sb7("sd2", [128, 1], F32)
                rs2 = sb7("rs2", [128, 1], F32)
                nmrs2 = sb7("nmrs2", [128, 1], F32)
                ps_hT = [sp7.enter_context(nc.psum_tensor(f"ps_hT{i}", [128, 2048], F32))
                         for i in range(2)]

                @block.sync
                def _(sync):
                    for k in range(NCH):
                        if k == 0:
                            rwait_all(sync, d_hw4, NCH)
                        _w(sync, p_tr7, NG * TT1 * (k - 1))
                        rinc(sync.dma_start(out=htin_sb[:, k % 2, :, :],
                                       in_=h_t[k]), d_hr7, k)

                @block.scalar
                def _(scalar):
                    for k in range(NCH):
                        if k == 0:
                            rwait_all(scalar, d_hw4, NCH)
                        _w(scalar, g_xp7, TT1 * (k - 1))
                        rinc(scalar.dma_start(out=x2_sb[:, k % 2, :, :],
                                         in_=x_ch1[k]), d_xr7, k)
                        for tt in range(TT1):
                            gt = k * TT1 + tt
                            scalar.wait_ge(vs7, 4 * gt + 3)
                            scalar.sqrt(sd2[:, 0:1], veps2[:, 0:1]).then_inc(s_sq7, 1)

                @block.tensor
                def _(tensor):
                    for k in range(NCH):
                        rwait(tensor, d_hr7, k)
                        for tt in range(TT1):
                            gt = k * TT1 + tt
                            _w(tensor, v_out7, gt - 1)
                            for ct in range(NG):
                                tensor.transpose(
                                    ps_hT[gt % 2][:, ct * 128:(ct + 1) * 128],
                                    htin_sb[:, k % 2, ct, tt * 128:(tt + 1) * 128],
                                    id32[:, :]).then_inc(p_tr7, 1)

                @block.vector
                def _(vector):
                    vcnt = 0
                    for k in range(NCH):
                        rwait(vector, d_x1w7, k - 2)
                        for tt in range(TT1):
                            gt = k * TT1 + tt
                            vector.wait_ge(p_tr7, NG * (gt + 1))
                            vcnt = _bn(vector, st6b[:, :], mv2[:, :],
                                       ps_hT[gt % 2][:, 0:C], C, vs7, vcnt)
                            vector.wait_ge(vs7, vcnt)
                            vector.tensor_scalar(
                                out=veps2[:, 0:1], in0=mv2[:, 1:2], scalar1=EPS,
                                scalar2=None, op0=ALU.add).then_inc(vs7, 1)
                            vcnt += 1
                            vector.wait_ge(s_sq7, gt + 1)
                            vector.reciprocal(out=rs2[:, 0:1],
                                              in_=sd2[:, 0:1]).then_inc(vs7, 1)
                            vcnt += 1
                            vector.wait_ge(vs7, vcnt)
                            vector.scalar_tensor_tensor(
                                out=nmrs2[:, 0:1], in0=mv2[:, 0:1], scalar=-1.0,
                                in1=rs2[:, 0:1], op0=ALU.mult,
                                op1=ALU.mult).then_inc(v_nm7, 1)
                            rwait(vector, d_xr7, k)
                            vector.wait_ge(v_nm7, gt + 1)
                            vector.tensor_scalar(
                                out=xp_sb[:, gt % 2, :], in0=x2_sb[:, k % 2, tt, :],
                                scalar1=nmrs2[:, 0:1], scalar2=None,
                                op0=ALU.add).then_inc(g_xp7, 1)
                            vector.wait_ge(g_xp7, gt + 1)
                            vector.scalar_tensor_tensor(
                                out=x1o_sb[:, k % 2, tt, :], in0=ps_hT[gt % 2][:, 0:C],
                                scalar=rs2[:, 0:1], in1=xp_sb[:, gt % 2, :],
                                op0=ALU.mult, op1=ALU.add).then_inc(v_out7, 1)

                @block.gpsimd
                def _(gpsimd):
                    for k in range(NCH):
                        gpsimd.wait_ge(v_out7, TT1 * (k + 1))
                        rinc(gpsimd.dma_start(out=x1_ch1[k],
                                         in_=x1o_sb[:, k % 2, :, :]), d_x1w7, k)

            with contextlib.ExitStack() as sp8:
                sb8 = lambda n, s, d: sp8.enter_context(nc.sbuf_tensor(n, s, d))
                x1a_sb = sb8("x1a_sb", [128, 2, C], F32)
                x1b_sb = sb8("x1b_sb", [128, 2, C], F32)
                xn1b_sb = sb8("xn1b_sb", [128, 2, C], BF16)
                xn1T_sb = sb8("xn1T_sb", [128, 2, NC, CH2], BF16)
                sfc_sb = sb8("sfc_sb", [128, 3, NC, 128], BF16)
                spj_sb = sb8("spj_sb", [128, 2, NH, 128], BF16)
                hfc_sb = sb8("hfc_sb", [128, NH, CH2], BF16)
                r_sb = sb8("r_sb", [128, 1, CH2], F32)
                y_sb = sb8("y_sb", [128, NC, CH2], F32)
                outb_sb = sb8("outb_sb", [128, 1, C], F32)
                out2_sb = sb8("out2_sb", [128, 2, C], F32)
                st6c = sb8("st6c", [128, 24], F32)
                mv3 = sb8("mv3", [128, 2], F32)
                veps3 = sb8("veps3", [128, 2], F32)
                sd3 = sb8("sd3", [128, 2], F32)
                rs3 = sb8("rs3", [128, 2], F32)
                nmrs3 = sb8("nmrs3", [128, 2], F32)
                mv4 = sb8("mv4", [128, 2], F32)
                veps4 = sb8("veps4", [128, 1], F32)
                sd4 = sb8("sd4", [128, 1], F32)
                rstd4 = sb8("rstd4", [128, 1], F32)
                sceff = sb8("sceff", [128, 1], F32)
                bseff = sb8("bseff", [128, 1], F32)
                ps_fc = [sp8.enter_context(nc.psum_tensor(f"ps_fc{i}", [128, 512], F32))
                         for i in range(2)]
                ps_pj = [sp8.enter_context(nc.psum_tensor(f"ps_pj{i}", [128, 512], F32))
                         for i in range(2)]
                ps_yT = sp8.enter_context(nc.psum_tensor("ps_yT", [128, 2048], F32))
                xn1_rows = xn1_d.rearrange("(n p) c -> n p c", p=128)
                x1_rows128 = x1_d.rearrange("(n p) c -> n p c", p=128)
                NGQ = NQ * TT2
                VS8_CHUNK = 3 * TT2 + 5 * TT2

                @block.gpsimd
                def _(gpsimd):
                    def x1b_seg(gpsimd, q):
                        for tt in range(TT2):
                            gq = q * TT2 + tt
                            _w(gpsimd, v_o8, gq - 1)
                            rinc(gpsimd.dma_start(out=x1b_sb[:, gq % 2, :],
                                           in_=x1_rows128[gq]), d_x1b, gq)

                    for q in range(NQ):
                        for tt in range(TT2):
                            gq = q * TT2 + tt
                            if gq == 0:
                                rwait_all(gpsimd, d_x1w7, NCH)
                            _w(gpsimd, s_nm8, gq - 1)
                            rinc(gpsimd.dma_start(out=x1a_sb[:, gq % 2, :],
                                           in_=x1_rows128[gq]), d_x1a, gq)
                        if q >= 1:
                            x1b_seg(gpsimd, q - 1)
                    x1b_seg(gpsimd, NQ - 1)

                @block.vector
                def _(vector):
                    vcnt = [0]

                    def ln3_seg(vector, q):
                        for tt in range(TT2):
                            gq = q * TT2 + tt
                            rwait(vector, d_x1a, gq)
                            _w(vector, s_sq8, gq - 1)
                            vcnt[0] = _bn(vector, st6c[:, :], mv3[:, :],
                                          x1a_sb[:, gq % 2, :], C, vs8, vcnt[0])
                            vector.wait_ge(vs8, vcnt[0])
                            vector.tensor_scalar(
                                out=veps3[:, gq % 2:gq % 2 + 1], in0=mv3[:, 1:2],
                                scalar1=EPS, scalar2=None,
                                op0=ALU.add).then_inc(v_st8, 1)
                            vector.wait_ge(s_sq8, gq + 1)
                            _w(vector, s_nm8, gq - 1)
                            vector.reciprocal(
                                out=rs3[:, gq % 2:gq % 2 + 1],
                                in_=sd3[:, gq % 2:gq % 2 + 1]).then_inc(vs8, 1)
                            vcnt[0] += 1
                            vector.wait_ge(vs8, vcnt[0])
                            vector.scalar_tensor_tensor(
                                out=nmrs3[:, gq % 2:gq % 2 + 1], in0=mv3[:, 0:1],
                                scalar=-1.0, in1=rs3[:, gq % 2:gq % 2 + 1],
                                op0=ALU.mult, op1=ALU.mult).then_inc(v_nm3, 1)

                    def ln4_seg(vector, q):
                        for tt in range(TT2):
                            gq = q * TT2 + tt
                            vector.wait_ge(p_yt, NC * (gq + 1))
                            vcnt[0] = _bn(vector, st6c[:, :], mv4[:, :],
                                          ps_yT[:, 0:C], C, vs8, vcnt[0])
                            vector.wait_ge(vs8, vcnt[0])
                            vector.tensor_scalar(
                                out=veps4[:, 0:1], in0=mv4[:, 1:2],
                                scalar1=amsq_pj[:, 0:1], scalar2=EPS,
                                op0=ALU.mult, op1=ALU.add).then_inc(vs8, 1)
                            vcnt[0] += 1
                            vector.wait_ge(s_sq8b, gq + 1)
                            vector.reciprocal(out=rstd4[:, 0:1],
                                              in_=sd4[:, 0:1]).then_inc(vs8, 1)
                            vcnt[0] += 1
                            vector.wait_ge(vs8, vcnt[0])
                            _w(vector, g_xp8, gq)
                            vector.tensor_scalar(
                                out=sceff[:, 0:1], in0=am_pj[:, 0:1],
                                scalar1=rstd4[:, 0:1], scalar2=None,
                                op0=ALU.mult).then_inc(vs8, 1)
                            vcnt[0] += 1
                            vector.wait_ge(vs8, vcnt[0])
                            vector.scalar_tensor_tensor(
                                out=bseff[:, 0:1], in0=mv4[:, 0:1], scalar=-1.0,
                                in1=sceff[:, 0:1], op0=ALU.mult,
                                op1=ALU.mult).then_inc(v_fold8, 1)
                            vector.wait_ge(g_xp8, gq + 1)
                            rwait(vector, d_x1b, gq)
                            rwait(vector, d_out8, gq - 2)
                            vector.scalar_tensor_tensor(
                                out=out2_sb[:, gq % 2, :], in0=outb_sb[:, 0, :],
                                scalar=1.0, in1=x1b_sb[:, gq % 2, :],
                                op0=ALU.mult, op1=ALU.add).then_inc(v_o8, 1)

                    ln3_seg(vector, 0)
                    for q in range(1, NQ):
                        ln3_seg(vector, q)
                        ln4_seg(vector, q - 1)
                    ln4_seg(vector, NQ - 1)

                @block.scalar
                def _(scalar):
                    LN3P = 3 * TT2
                    LN4P = 5 * TT2

                    def ln3s_seg(scalar, q):
                        for tt in range(TT2):
                            gq = q * TT2 + tt
                            scalar.wait_ge(v_st8, gq + 1)
                            scalar.sqrt(sd3[:, gq % 2:gq % 2 + 1],
                                        veps3[:, gq % 2:gq % 2 + 1]).then_inc(s_sq8, 1)
                            scalar.wait_ge(v_nm3, gq + 1)
                            rwait(scalar, d_xn1w, gq - 2)
                            scalar.activation(
                                out=xn1b_sb[:, gq % 2, :], in_=x1a_sb[:, gq % 2, :],
                                func=AF.Identity, bias=nmrs3[:, gq % 2:gq % 2 + 1],
                                scale=rs3[:, gq % 2:gq % 2 + 1]).then_inc(s_nm8, 1)
                            scalar.wait_ge(s_nm8, gq + 1)
                            rinc(scalar.dma_start(out=xn1_rows[gq],
                                             in_=xn1b_sb[:, gq % 2, :]), d_xn1w, gq)

                    def xn1T_seg(scalar, q):
                        if q == 0:
                            rwait_all(scalar, d_sw, NHB + NC)
                        rwait_all(scalar, d_xn1w, TT2 * (q + 1))
                        _w(scalar, p_fc, NH * (q - 1))
                        for ct in range(NC):
                            scalar.dma_start(
                                out=xn1T_sb[:, q % 2, ct, :],
                                in_=xn1_d[q * CH2:(q + 1) * CH2, ct * 128:(ct + 1) * 128],
                                transpose=True).then_inc(d_xn1T[q % 2], 16)

                    def comp_seg(scalar, q):
                        if q + 1 < NQ:
                            xn1T_seg(scalar, q + 1)
                        for ht in range(NH):
                            gh = q * NH + ht
                            scalar.wait_ge(p_fc, gh + 1)
                            _w(scalar, s_sqq, gh)
                            scalar.activation(
                                out=r_sb[:, 0, :], in_=ps_fc[gh % 2][:, 0:CH2],
                                func=AF.Relu, scale=am_fc[:, 0:1]).then_inc(s_rev, 1)
                            scalar.wait_ge(s_rev, gh + 1)
                            scalar.activation(
                                out=hfc_sb[:, ht, :], in_=r_sb[:, 0, :],
                                func=AF.Square).then_inc(s_sqq, 1)
                        for ct in range(NC):
                            gc = q * NC + ct
                            scalar.wait_ge(p_pj, gc + 1)
                            if ct == 0:
                                _w(scalar, p_yt, NC * TT2 * q)
                            scalar.copy(out=y_sb[:, ct, :],
                                        in_=ps_pj[gc % 2][:, 0:CH2]).then_inc(s_yev, 1)
                        for tt in range(TT2):
                            gq = q * TT2 + tt
                            target = LN3P * min(q + 2, NQ) + LN4P * q + 5 * tt + 3
                            scalar.wait_ge(vs8, target)
                            scalar.sqrt(sd4[:, 0:1], veps4[:, 0:1]).then_inc(s_sq8b, 1)
                            scalar.wait_ge(v_fold8, gq + 1)
                            _w(scalar, v_o8, gq)
                            scalar.activation(
                                out=outb_sb[:, 0, :], in_=ps_yT[:, 0:C],
                                func=AF.Identity, bias=bseff[:, 0:1],
                                scale=sceff[:, 0:1]).then_inc(g_xp8, 1)
                            scalar.wait_ge(v_o8, gq + 1)
                            rinc(scalar.dma_start(out=out_rows[gq],
                                             in_=out2_sb[:, gq % 2, :]), d_out8, gq)

                    ln3s_seg(scalar, 0)
                    xn1T_seg(scalar, 0)
                    for q in range(1, NQ):
                        ln3s_seg(scalar, q)
                        comp_seg(scalar, q - 1)
                    comp_seg(scalar, NQ - 1)
                    rwait_all(scalar, d_out8, NT128)

                @block.sync
                def _(sync):
                    for q in range(NQ):
                        if q == 0:
                            rwait_all(sync, d_sw, NHB + NC)
                            rwait_all(sync, d_x1w7, NCH)
                        for ht in range(NH):
                            ghx = q * NH + ht
                            _w(sync, p_fc, ghx - 2)
                            rinc(sync.dma_start(
                                out=sfc_sb[:, ghx % 3, :, :],
                                in_=sfcT_t[ht // 2][:, :, (ht % 2) * 128:(ht % 2 + 1) * 128]),
                                d_sfc, ghx)
                        for ct in range(NC):
                            gc = q * NC + ct
                            _w(sync, p_pj, gc - 1)
                            rinc(sync.dma_start(
                                out=spj_sb[:, gc % 2, :, :],
                                in_=spjT_t[ct]), d_spr, gc)

                @block.tensor
                def _(tensor):
                    for q in range(NQ):
                        tensor.wait_ge(d_xn1T[q % 2], 16 * NC * (q // 2 + 1))
                        for ht in range(NH):
                            gh = q * NH + ht
                            rwait(tensor, d_sfc, gh)
                            _w(tensor, s_rev, gh - 1)
                            for ct in range(NC):
                                ins = tensor.matmul(
                                    ps_fc[gh % 2][:, 0:CH2],
                                    sfc_sb[:, gh % 3, ct, :],
                                    xn1T_sb[:, q % 2, ct, :],
                                    start=(ct == 0), stop=(ct == NC - 1))
                            ins.then_inc(p_fc, 1)
                        tensor.wait_ge(s_sqq, NH * (q + 1))
                        for ct in range(NC):
                            gc = q * NC + ct
                            rwait(tensor, d_spr, gc)
                            _w(tensor, s_yev, gc - 1)
                            for ht in range(NH):
                                ins = tensor.matmul(
                                    ps_pj[gc % 2][:, 0:CH2],
                                    spj_sb[:, gc % 2, ht, :],
                                    hfc_sb[:, ht, :],
                                    start=(ht == 0), stop=(ht == NH - 1))
                            ins.then_inc(p_pj, 1)
                        for tt in range(TT2):
                            gq = q * TT2 + tt
                            _w(tensor, g_xp8, gq)
                            for ct in range(NC):
                                tensor.wait_ge(s_yev, q * NC + ct + 1)
                                tensor.transpose(
                                    ps_yT[:, ct * 128:(ct + 1) * 128],
                                    y_sb[:, ct, tt * 128:(tt + 1) * 128],
                                    id32[:, :]).then_inc(p_yt, 1)

    return nc


def kernel(L0_attn_x, L0_attn_W, L0_attn_b, L0_ffn_w_fc, L0_ffn_w_proj):
    x = np.ascontiguousarray(np.asarray(L0_attn_x, dtype=np.float32))
    W = np.ascontiguousarray(np.asarray(L0_attn_W, dtype=np.float32))
    b = np.ascontiguousarray(np.asarray(L0_attn_b, dtype=np.float32))
    wfc = np.ascontiguousarray(np.asarray(L0_ffn_w_fc, dtype=np.float32))
    wpj = np.ascontiguousarray(np.asarray(L0_ffn_w_proj, dtype=np.float32))

    nc = build_program(FULL_CFG)
    ND = FULL_CFG["D"] // 128
    bT = np.ascontiguousarray(b.reshape(ND, 128).T)
    id32 = np.eye(128, dtype=np.float32)
    idbf = np.eye(128, dtype=ml_dtypes.bfloat16)
    in_maps = []
    for i in range(8):
        shard = np.ascontiguousarray(x[2 * i:2 * i + 2].reshape(4096, 2048))
        in_maps.append(dict(x=shard, W=W, bT=bT, wfc=wfc, wpj=wpj,
                            id32=id32, idbf=idbf))
    res = run_bass_kernel_spmd(nc, in_maps, core_ids=list(range(8)))
    outs = [np.asarray(r["out"]).reshape(2, 2048, 2048) for r in res.results]
    return np.concatenate(outs, axis=0)


# version 22
# speedup vs baseline: 1.1734x; 1.1734x over previous
import contextlib
import numpy as np
import ml_dtypes
import concourse.bass as bass
import concourse.mybir as mybir
from concourse.bass_utils import run_bass_kernel_spmd

F32 = mybir.dt.float32
BF16 = mybir.dt.bfloat16
AF = mybir.ActivationFunctionType
ALU = mybir.AluOpType
AX = mybir.AxisListType

FULL_CFG = dict(T=4096, C=2048, D=4096, H=5120, TB=2048, CH1=256, CH2=512)
EPS = 1e-5


def _w(eng, s, v):
    if v > 0:
        eng.wait_ge(s, v)


def rinc(instr, ring, i):
    instr.then_inc(ring[i % len(ring)], 16)


def rwait(eng, ring, i):
    if i >= 0:
        n = len(ring)
        eng.wait_ge(ring[i % n], 16 * (i // n + 1))


def rwait_all(eng, ring, n):
    r = len(ring)
    for s in range(r):
        c = (n - s + r - 1) // r
        if c > 0:
            eng.wait_ge(ring[s], 16 * c)


def _bn(vec, st6, dst_mv, src, width, self_sem, ctr):
    nb = max(1, width // 512)
    step = width // nb
    assert step * nb == width and step <= 512 and step % 2 == 0
    for j in range(nb):
        ins = vec.bn_stats(st6[:, 6 * j:6 * j + 6], src[:, j * step:(j + 1) * step])
    ins.then_inc(self_sem, 1)
    ctr += 1
    vec.wait_ge(self_sem, ctr)
    vec.bn_aggr(dst_mv, st6[:, 0:6 * nb]).then_inc(self_sem, 1)
    return ctr + 1


def build_program(cfg):
    T, C, D, H = cfg["T"], cfg["C"], cfg["D"], cfg["H"]
    TB, CH1, CH2 = cfg["TB"], cfg["CH1"], cfg["CH2"]

    NC = C // 128
    ND = D // 128
    NG = NC
    HALF = NG // 2
    NH = H // 128
    NHB = NH // 2
    NCH = T // CH1
    NQ = T // CH2
    TT1 = CH1 // 128
    TT2 = CH2 // 128
    RESET = TB // CH1
    GPC = 2 * HALF
    NT128 = T // 128

    nc = bass.Bass("TRN2", target_bir_lowering=False)

    x_d = nc.dram_tensor("x", [T, C], F32, kind="ExternalInput").ap()
    w_d = nc.dram_tensor("W", [D, C], F32, kind="ExternalInput").ap()
    bT_d = nc.dram_tensor("bT", [128, ND], F32, kind="ExternalInput").ap()
    wfc_d = nc.dram_tensor("wfc", [H, C], F32, kind="ExternalInput").ap()
    wpj_d = nc.dram_tensor("wpj", [C, H], F32, kind="ExternalInput").ap()
    id32_d = nc.dram_tensor("id32", [128, 128], F32, kind="ExternalInput").ap()
    idbf_d = nc.dram_tensor("idbf", [128, 128], BF16, kind="ExternalInput").ap()
    out_d = nc.dram_tensor("out", [T, C], F32, kind="ExternalOutput").ap()
    wbf_d = nc.dram_tensor("wbf_s", [D, C], BF16).ap()
    xn_d = nc.dram_tensor("xn_s", [T, C], BF16).ap()
    h_d = None
    x1_d = nc.dram_tensor("x1_s", [T, C], F32).ap()
    xn1_d = nc.dram_tensor("xn1_s", [T, C], BF16).ap()
    h_t = nc.dram_tensor("h_t", [T // CH1, 128, C // 128, CH1], F32).ap()
    sfcT_t = nc.dram_tensor("sfcT_t", [H // 256, 128, C // 128, 256], BF16).ap()
    spjT_t = nc.dram_tensor("spjT_t", [C // 128, 128, H // 128, 128], BF16).ap()

    x_ch1 = x_d.rearrange("(k tt p) c -> k p tt c", p=128, tt=TT1)
    xn_ch1 = xn_d.rearrange("(k tt p) c -> k p tt c", p=128, tt=TT1)
    x1_ch1 = x1_d.rearrange("(k tt p) c -> k p tt c", p=128, tt=TT1)
    x1_ch2 = x1_d.rearrange("(q tt p) c -> q p tt c", p=128, tt=TT2)
    xn1_ch2 = xn1_d.rearrange("(q tt p) c -> q p tt c", p=128, tt=TT2)
    out_rows = out_d.rearrange("(n p) c -> n p c", p=128)

    est = contextlib.ExitStack()
    sbuf = lambda n, s, d: est.enter_context(nc.sbuf_tensor(n, s, d))
    sem = lambda n: est.enter_context(nc.semaphore(n))

    with est:
        bT_sb = sbuf("bT_sb", [128, ND], F32)
        id32 = sbuf("id32_sb", [128, 128], F32)
        idbf = sbuf("idbf_sb", [128, 128], BF16)
        ones_row = sbuf("ones_row", [1, 128], F32)
        am_fc = sbuf("am_fc", [128, 1], F32)
        thr_fc = sbuf("thr_fc", [128, 1], F32)
        nthr_fc = sbuf("nthr_fc", [128, 1], F32)
        am_pj = sbuf("am_pj", [128, 1], F32)
        thr_pj = sbuf("thr_pj", [128, 1], F32)
        nthr_pj = sbuf("nthr_pj", [128, 1], F32)
        amsq_pj = sbuf("amsq_pj", [128, 1], F32)
        tot1 = sbuf("tot1", [1, 2], F32)
        tot2 = sbuf("tot2", [1, 2], F32)
        red1 = sbuf("red1", [128, 1], F32)
        carry = sbuf("carry", [128, NG], F32)

        ring = lambda n, r=2: [sem(n + str(i)) for i in range(r)]
        d_prep = sem("d_prep"); d_cast = sem("d_cast"); v_prep = sem("v_prep")
        d_wf = ring("d_wf"); v_abs = sem("v_abs"); fz = sem("fz")
        d_sf = ring("d_sf"); v_sg = sem("v_sg"); g_sg = sem("g_sg")
        p_tr2 = sem("p_tr2"); s_ev2 = sem("s_ev2"); d_sw = ring("d_sw")
        d_wt3 = sem("d_wt3"); d_wt5 = sem("d_wt5")
        d_x4 = ring("d_x4"); vs4 = sem("vs4"); s_sq4 = sem("s_sq4")
        v_lna = sem("v_lna"); v_lnb = sem("v_lnb"); s_nm4 = sem("s_nm4")
        d_xnw4 = ring("d_xnw4")
        d_xnT4 = ring("d_xnT4", 3); p_mm4 = sem("p_mm4"); s_ev4 = sem("s_ev4")
        v_sc4 = sem("v_sc4"); v_cr4 = sem("v_cr4"); d_hw4 = ring("d_hw4")
        d_xnT6 = ring("d_xnT6"); p_mm6 = sem("p_mm6"); s_ev6 = sem("s_ev6")
        v_sc6 = sem("v_sc6"); v_cr6 = sem("v_cr6"); d_hw6 = ring("d_hw6")
        d_hr7 = ring("d_hr7"); d_xr7 = ring("d_xr7"); p_tr7 = sem("p_tr7")
        vs7 = sem("vs7"); s_sq7 = sem("s_sq7"); v_nm7 = sem("v_nm7")
        g_xp7 = sem("g_xp7"); v_out7 = sem("v_out7"); d_x1w7 = ring("d_x1w7")
        d_x1a = ring("d_x1a"); d_x1b = ring("d_x1b")
        vs8 = sem("vs8"); s_sq8 = sem("s_sq8")
        v_st8 = sem("v_st8"); v_nm3 = sem("v_nm3"); s_nm8 = sem("s_nm8")
        d_xn1w = ring("d_xn1w"); d_xn1T = ring("d_xn1T")
        d_sfc = ring("d_sfc", 2); d_spr = ring("d_spr")
        p_fc = sem("p_fc"); s_rev = sem("s_rev"); s_sqq = sem("s_sqq")
        p_pj = sem("p_pj"); s_yev = sem("s_yev"); p_yt = sem("p_yt")
        s_sq8b = sem("s_sq8b"); v_fold8 = sem("v_fold8")
        g_xp8 = sem("g_xp8"); v_o8 = sem("v_o8"); d_out8 = ring("d_out8", 2)

        with nc.Block() as block:
            @block.sync
            def _(sync):
                sync.dma_start(out=bT_sb[:, :], in_=bT_d[:, :]).then_inc(d_prep, 16)
                sync.dma_start(out=id32[:, :], in_=id32_d[:, :]).then_inc(d_prep, 16)
                sync.dma_start(out=idbf[:, :], in_=idbf_d[:, :]).then_inc(d_prep, 16)

            @block.gpsimd
            def _(gpsimd):
                q4 = D // 4
                for i in range(4):
                    gpsimd.dma_start(out=wbf_d[i * q4:(i + 1) * q4, :],
                                     in_=w_d[i * q4:(i + 1) * q4, :]).then_inc(d_cast, 16)

            @block.vector
            def _(vector):
                vector.memset(ones_row[:, :], 1.0).then_inc(v_prep, 1)

            with contextlib.ExitStack() as s12:
                sb12 = lambda n, s, d: s12.enter_context(nc.sbuf_tensor(n, s, d))
                ps12 = lambda n, s, d: s12.enter_context(nc.psum_tensor(n, s, d))
                MW = max(C, H)
                wslab = sb12("wslab", [128, 2, MW], F32)
                acc_fc = sb12("acc_fc", [128, NH], F32)
                acc_pj = sb12("acc_pj", [128, NC], F32)
                nbs = sb12("nbs", [128, 2, MW], BF16)
                ssl = sb12("ssl", [128, 2, MW], BF16)
                stage_fc = sb12("stage_fc", [128, 2, NC, 256], BF16)
                stage_pj = sb12("stage_pj", [128, 2, NH, 128], BF16)
                ps_misc = ps12("ps_misc", [128, 512], F32)
                ps_bf = [ps12("ps_bf0", [128, 1024], BF16),
                         ps12("ps_bf1", [128, 1024], BF16)]

                @block.sync
                def _(sync):
                    for i in range(NH):
                        _w(sync, v_abs, i - 1)
                        rinc(sync.dma_start(out=wslab[:, i % 2, 0:C],
                                            in_=wfc_d[i * 128:(i + 1) * 128, :]), d_wf, i)
                    for j in range(NC):
                        _w(sync, v_abs, NH + j - 1)
                        rinc(sync.dma_start(out=wslab[:, j % 2, 0:H],
                                            in_=wpj_d[j * 128:(j + 1) * 128, :]), d_wf, NH + j)

                @block.vector
                def _(vector):
                    for i in range(NH):
                        rwait(vector, d_wf, i)
                        vector.tensor_reduce(out=acc_fc[:, i:i + 1], in_=wslab[:, i % 2, 0:C],
                                             axis=AX.X, op=ALU.add,
                                             apply_absolute_value=True).then_inc(v_abs, 1)
                    for j in range(NC):
                        rwait(vector, d_wf, NH + j)
                        vector.tensor_reduce(out=acc_pj[:, j:j + 1], in_=wslab[:, j % 2, 0:H],
                                             axis=AX.X, op=ALU.add,
                                             apply_absolute_value=True).then_inc(v_abs, 1)

                fzc = [0]

                def vst(vec, f):
                    _w(vec, fz, fzc[0])
                    f().then_inc(fz, 1)
                    fzc[0] += 1

                def tst(tensor, f):
                    _w(tensor, fz, fzc[0])
                    f().then_inc(fz, 1)
                    fzc[0] += 1

                @block.vector
                def _(vector):
                    vector.wait_ge(v_abs, NH + NC)
                    vst(vector, lambda: vector.tensor_reduce(
                        out=red1[:, 0:1], in_=acc_fc[:, 0:NH], axis=AX.X, op=ALU.add))

                @block.tensor
                def _(tensor):
                    tensor.wait_ge(d_prep, 48)
                    tensor.wait_ge(v_prep, 1)
                    tst(tensor, lambda: tensor.transpose(
                        ps_misc[0:1, 0:128], red1[:, 0:1], id32[:, :]))

                @block.vector
                def _(vector):
                    vst(vector, lambda: vector.tensor_reduce(
                        out=tot1[0:1, 0:1], in_=ps_misc[0:1, 0:128], axis=AX.X, op=ALU.add))
                    vst(vector, lambda: vector.tensor_scalar(
                        out=tot2[0:1, 0:1], in0=tot1[0:1, 0:1],
                        scalar1=1.0 / (H * C), scalar2=EPS, op0=ALU.mult, op1=ALU.max))

                @block.tensor
                def _(tensor):
                    tst(tensor, lambda: tensor.matmul(
                        ps_misc[0:128, 256:257], ones_row[0:1, 0:128],
                        tot2[0:1, 0:1], start=True, stop=True))

                @block.vector
                def _(vector):
                    vst(vector, lambda: vector.tensor_copy(
                        out=am_fc[:, 0:1], in_=ps_misc[0:128, 256:257]))
                    vst(vector, lambda: vector.tensor_scalar(
                        out=thr_fc[:, 0:1], in0=am_fc[:, 0:1], scalar1=0.7,
                        scalar2=None, op0=ALU.mult))
                    vst(vector, lambda: vector.tensor_scalar(
                        out=nthr_fc[:, 0:1], in0=thr_fc[:, 0:1], scalar1=-1.0,
                        scalar2=None, op0=ALU.mult))
                    vst(vector, lambda: vector.tensor_reduce(
                        out=red1[:, 0:1], in_=acc_pj[:, 0:NC], axis=AX.X, op=ALU.add))

                @block.tensor
                def _(tensor):
                    tst(tensor, lambda: tensor.transpose(
                        ps_misc[0:1, 0:128], red1[:, 0:1], id32[:, :]))

                @block.vector
                def _(vector):
                    vst(vector, lambda: vector.tensor_reduce(
                        out=tot1[0:1, 1:2], in_=ps_misc[0:1, 0:128], axis=AX.X, op=ALU.add))
                    vst(vector, lambda: vector.tensor_scalar(
                        out=tot2[0:1, 1:2], in0=tot1[0:1, 1:2],
                        scalar1=1.0 / (H * C), scalar2=EPS, op0=ALU.mult, op1=ALU.max))

                @block.tensor
                def _(tensor):
                    tst(tensor, lambda: tensor.matmul(
                        ps_misc[0:128, 256:257], ones_row[0:1, 0:128],
                        tot2[0:1, 1:2], start=True, stop=True))

                @block.vector
                def _(vector):
                    vst(vector, lambda: vector.tensor_copy(
                        out=am_pj[:, 0:1], in_=ps_misc[0:128, 256:257]))
                    vst(vector, lambda: vector.tensor_scalar(
                        out=thr_pj[:, 0:1], in0=am_pj[:, 0:1], scalar1=0.7,
                        scalar2=None, op0=ALU.mult))
                    vst(vector, lambda: vector.tensor_scalar(
                        out=nthr_pj[:, 0:1], in0=thr_pj[:, 0:1], scalar1=-1.0,
                        scalar2=None, op0=ALU.mult))
                    vst(vector, lambda: vector.tensor_scalar(
                        out=amsq_pj[:, 0:1], in0=am_pj[:, 0:1], scalar1=am_pj[:, 0:1],
                        scalar2=None, op0=ALU.mult))

                FZ_END = fzc[0]

                @block.sync
                def _(sync):
                    for i in range(NH):
                        if i < 2:
                            sync.wait_ge(v_abs, NH + NC)
                        else:
                            sync.wait_ge(v_sg, 2 * (i - 2) + 2)
                        rinc(sync.dma_start(out=wslab[:, i % 2, 0:C],
                                            in_=wfc_d[i * 128:(i + 1) * 128, :]), d_sf, i)
                    for j in range(NC):
                        if j < 2:
                            sync.wait_ge(v_sg, 2 * (NH - 2 + j) + 2)
                        else:
                            sync.wait_ge(g_sg, 2 * (j - 2) + 2)
                        rinc(sync.dma_start(out=wslab[:, j % 2, 0:H],
                                            in_=wpj_d[j * 128:(j + 1) * 128, :]), d_sf, NH + j)

                @block.vector
                def _(vector):
                    vector.wait_ge(fz, FZ_END)
                    for i in range(NH):
                        rwait(vector, d_sf, i)
                        _w(vector, p_tr2, NC * (i - 1))
                        vector.tensor_scalar(
                            out=nbs[:, i % 2, 0:C], in0=wslab[:, i % 2, 0:C],
                            scalar1=nthr_fc[:, 0:1], scalar2=-1.0,
                            op0=ALU.is_le, op1=ALU.mult).then_inc(v_sg, 1)
                        vector.wait_ge(v_sg, 2 * i + 1)
                        vector.scalar_tensor_tensor(
                            out=ssl[:, i % 2, 0:C], in0=wslab[:, i % 2, 0:C],
                            scalar=thr_fc[:, 0:1], in1=nbs[:, i % 2, 0:C],
                            op0=ALU.is_ge, op1=ALU.add).then_inc(v_sg, 1)
                    for j in range(NC):
                        rwait(vector, d_sf, NH + j)
                        if j < 2:
                            vector.wait_ge(p_tr2, NC * (NH - 1 + j))
                        else:
                            vector.wait_ge(p_tr2, NC * NH + NH * (j - 1))
                        vector.tensor_scalar(
                            out=nbs[:, j % 2, 0:H], in0=wslab[:, j % 2, 0:H],
                            scalar1=nthr_pj[:, 0:1], scalar2=-1.0,
                            op0=ALU.is_le, op1=ALU.mult).then_inc(g_sg, 1)
                        vector.wait_ge(g_sg, 2 * j + 1)
                        vector.scalar_tensor_tensor(
                            out=ssl[:, j % 2, 0:H], in0=wslab[:, j % 2, 0:H],
                            scalar=thr_pj[:, 0:1], in1=nbs[:, j % 2, 0:H],
                            op0=ALU.is_ge, op1=ALU.add).then_inc(g_sg, 1)


                @block.tensor
                def _(tensor):
                    tcnt = 0
                    for i in range(NH):
                        tensor.wait_ge(v_sg, 2 * i + 2)
                        for ct in range(NC):
                            _w(tensor, s_ev2, tcnt - 1)
                            tensor.transpose(ps_bf[tcnt % 2][0:128, 0:128],
                                             ssl[:, i % 2, ct * 128:(ct + 1) * 128],
                                             idbf[:, :]).then_inc(p_tr2, 1)
                            tcnt += 1
                    for j in range(NC):
                        tensor.wait_ge(g_sg, 2 * j + 2)
                        for ht in range(NH):
                            _w(tensor, s_ev2, tcnt - 1)
                            tensor.transpose(ps_bf[tcnt % 2][0:128, 0:128],
                                             ssl[:, j % 2, ht * 128:(ht + 1) * 128],
                                             idbf[:, :]).then_inc(p_tr2, 1)
                            tcnt += 1

                @block.scalar
                def _(scalar):
                    tcnt = 0
                    for i in range(NH):
                        hb, hoff = i // 2, (i % 2) * 128
                        if i % 2 == 0:
                            rwait(scalar, d_sw, hb - 2)
                        for ct in range(NC):
                            scalar.wait_ge(p_tr2, tcnt + 1)
                            scalar.copy(out=stage_fc[:, hb % 2, ct, hoff:hoff + 128],
                                        in_=ps_bf[tcnt % 2][0:128, 0:128]).then_inc(s_ev2, 1)
                            tcnt += 1
                    for j in range(NC):
                        rwait(scalar, d_sw, NHB + j - 2)
                        for ht in range(NH):
                            scalar.wait_ge(p_tr2, tcnt + 1)
                            scalar.copy(out=stage_pj[:, j % 2, ht, :],
                                        in_=ps_bf[tcnt % 2][0:128, 0:128]).then_inc(s_ev2, 1)
                            tcnt += 1

                @block.gpsimd
                def _(gpsimd):
                    for hb in range(NHB):
                        gpsimd.wait_ge(s_ev2, NC * (2 * hb + 2))
                        rinc(gpsimd.dma_start(out=sfcT_t[hb],
                                         in_=stage_fc[:, hb % 2, :, :]), d_sw, hb)
                    for j in range(NC):
                        gpsimd.wait_ge(s_ev2, NC * NH + NH * (j + 1))
                        rinc(gpsimd.dma_start(out=spjT_t[j],
                                         in_=stage_pj[:, j % 2, :, :]), d_sw, NHB + j)

            S2_DONE = 16 * (NH + NC)

            with contextlib.ExitStack() as sp1:
                sb1 = lambda n, s, d: sp1.enter_context(nc.sbuf_tensor(n, s, d))
                wt_sb = sb1("wt_sb", [128, NC, 2 * NG * 128], BF16)
                x_sb = sb1("x_sb", [128, 1, TT1, C], F32)
                xnb_sb = sb1("xnb_sb", [128, 1, TT1, C], BF16)
                xnT_sb = sb1("xnT_sb", [128, 3, NC, CH1], BF16)
                g_sb = sb1("g_sb", [128, 2, CH1], F32)
                v_sb = sb1("v_sb", [128, 2, CH1], F32)
                h_sb = sb1("h_sb", [128, 1, NG, CH1], F32)
                st6a = sb1("st6a", [128, TT1, 24], F32)
                mv1 = sb1("mv1", [128, TT1, 2], F32)
                veps1 = sb1("veps1", [128, TT1], F32)
                sd1 = sb1("sd1", [128, TT1], F32)
                rs1 = sb1("rs1", [128, TT1], F32)
                nmrs1 = sb1("nmrs1", [128, TT1], F32)
                ps_par = [sp1.enter_context(nc.psum_tensor(f"ps_par{i}", [128, 512], F32))
                          for i in range(4)]
                GP1 = 2 * NG

                @block.sync
                def _(sync):
                    sync.wait_ge(d_cast, 64)
                    rwait_all(sync, d_sw, NHB + NC)
                    for ct in range(NC):
                        sync.dma_start(out=wt_sb[:, ct, 0:NG * 128],
                                       in_=wbf_d[0:C, ct * 128:(ct + 1) * 128],
                                       transpose=True).then_inc(d_wt3, 16)
                        sync.dma_start(out=wt_sb[:, ct, NG * 128:2 * NG * 128],
                                       in_=wbf_d[C:D, ct * 128:(ct + 1) * 128],
                                       transpose=True).then_inc(d_wt3, 16)

                @block.sync
                def _(sync):
                    for k in range(NCH):
                        _w(sync, s_nm4, TT1 * k)
                        rinc(sync.dma_start(out=x_sb[:, 0, :, :],
                                       in_=x_ch1[k]), d_x4, k)
                        rwait(sync, d_xnw4, k)
                        _w(sync, p_mm4, GP1 * (k - 2))
                        for ct in range(NC):
                            sync.dma_start(
                                out=xnT_sb[:, k % 3, ct, :],
                                in_=xn_d[k * CH1:(k + 1) * CH1, ct * 128:(ct + 1) * 128],
                                transpose=True).then_inc(d_xnT4[k % 3], 16)

                @block.vector
                def _(vector):
                    vcnt = 0

                    def stats_seg(vector, k, vcnt):
                        rwait(vector, d_x4, k)
                        for tt in range(TT1):
                            vcnt = _bn(vector, st6a[:, tt, :], mv1[:, tt, :],
                                       x_sb[:, 0, tt, :], C, vs4, vcnt)
                        vector.wait_ge(vs4, vcnt)
                        _w(vector, s_sq4, k)
                        for tt in range(TT1):
                            ins = vector.tensor_scalar(
                                out=veps1[:, tt:tt + 1], in0=mv1[:, tt, 1:2],
                                scalar1=EPS, scalar2=None, op0=ALU.add)
                        ins.then_inc(v_lna, 1)
                        vector.wait_ge(s_sq4, k + 1)
                        _w(vector, s_nm4, TT1 * k)
                        vector.reciprocal(out=rs1[:, 0:TT1],
                                          in_=sd1[:, 0:TT1]).then_inc(vs4, 1)
                        vcnt += 1
                        vector.wait_ge(vs4, vcnt)
                        for tt in range(TT1):
                            ins = vector.scalar_tensor_tensor(
                                out=nmrs1[:, tt:tt + 1], in0=mv1[:, tt, 0:1],
                                scalar=-1.0, in1=rs1[:, tt:tt + 1],
                                op0=ALU.mult, op1=ALU.mult)
                        ins.then_inc(v_lnb, 1)
                        return vcnt

                    def scan_seg(vector, k):
                        for pair in range(NG):
                            gp = k * NG + pair
                            vector.wait_ge(s_ev4, k * GP1 + 2 * pair + 2)
                            rwait(vector, d_hw4, k - 1)
                            if k % RESET == 0:
                                init = 0.0
                            else:
                                vector.wait_ge(v_cr4, (k - 1) * NG + pair + 1)
                                init = carry[:, pair:pair + 1]
                            vector.tensor_tensor_scan(
                                out=h_sb[:, 0, pair, :],
                                data0=g_sb[:, gp % 2, :], data1=v_sb[:, gp % 2, :],
                                initial=init, op0=ALU.mult,
                                op1=ALU.add).then_inc(v_sc4, 1)
                            vector.wait_ge(v_sc4, gp + 1)
                            vector.tensor_copy(
                                out=carry[:, pair:pair + 1],
                                in_=h_sb[:, 0, pair, CH1 - 1:CH1]).then_inc(v_cr4, 1)

                    vcnt = stats_seg(vector, 0, vcnt)
                    for k in range(1, NCH):
                        vcnt = stats_seg(vector, k, vcnt)
                        scan_seg(vector, k - 1)
                    scan_seg(vector, NCH - 1)

                @block.scalar
                def _(scalar):
                    scalar.wait_ge(d_prep, 48)

                    def ln1_seg(scalar, k):
                        scalar.wait_ge(v_lna, k + 1)
                        scalar.sqrt(sd1[:, 0:TT1],
                                    veps1[:, 0:TT1]).then_inc(s_sq4, 1)
                        scalar.wait_ge(v_lnb, k + 1)
                        rwait(scalar, d_xnw4, k - 1)
                        for tt in range(TT1):
                            scalar.activation(
                                out=xnb_sb[:, 0, tt, :],
                                in_=x_sb[:, 0, tt, :], func=AF.Identity,
                                bias=nmrs1[:, tt:tt + 1],
                                scale=rs1[:, tt:tt + 1]).then_inc(s_nm4, 1)
                        scalar.wait_ge(s_nm4, TT1 * (k + 1))
                        rinc(scalar.dma_start(out=xn_ch1[k],
                                         in_=xnb_sb[:, 0, :, :]), d_xnw4, k)

                    def evac_seg(scalar, k):
                        for pair in range(NG):
                            for role in range(2):
                                gi = k * GP1 + 2 * pair + role
                                gp = k * NG + pair
                                scalar.wait_ge(p_mm4, gi + 1)
                                _w(scalar, v_sc4, gp - 1)
                                if role == 0:
                                    scalar.activation(
                                        out=g_sb[:, gp % 2, :],
                                        in_=ps_par[gi % 4][:, 0:CH1],
                                        func=AF.Sigmoid,
                                        bias=bT_sb[:, pair:pair + 1]).then_inc(s_ev4, 1)
                                else:
                                    scalar.activation(
                                        out=v_sb[:, gp % 2, :],
                                        in_=ps_par[gi % 4][:, 0:CH1],
                                        func=AF.Tanh,
                                        bias=bT_sb[:, NG + pair:NG + pair + 1]).then_inc(s_ev4, 1)

                    ln1_seg(scalar, 0)
                    for k in range(1, NCH):
                        ln1_seg(scalar, k)
                        evac_seg(scalar, k - 1)
                    evac_seg(scalar, NCH - 1)

                @block.tensor
                def _(tensor):
                    tensor.wait_ge(d_wt3, 32 * NC)
                    for k in range(NCH):
                        tensor.wait_ge(d_xnT4[k % 3], 16 * NC * (k // 3 + 1))
                        for pair in range(NG):
                            for role in range(2):
                                gi = k * GP1 + 2 * pair + role
                                _w(tensor, s_ev4, gi - 3)
                                col0 = (role * NG + pair) * 128
                                pp = ps_par[gi % 4]
                                for ct in range(NC):
                                    ins = tensor.matmul(
                                        pp[:, 0:CH1],
                                        wt_sb[:, ct, col0:col0 + 128],
                                        xnT_sb[:, k % 3, ct, :],
                                        start=(ct == 0), stop=(ct == NC - 1))
                                ins.then_inc(p_mm4, 1)

                @block.gpsimd
                def _(gpsimd):
                    for k in range(NCH):
                        gpsimd.wait_ge(v_sc4, NG * (k + 1))
                        rinc(gpsimd.dma_start(
                            out=h_t[k][:, 0:NG, :],
                            in_=h_sb[:, 0, :, :]), d_hw4, k)

            with contextlib.ExitStack() as sp7:
                sb7 = lambda n, s, d: sp7.enter_context(nc.sbuf_tensor(n, s, d))
                htin_sb = sb7("htin_sb", [128, 2, NG, CH1], F32)
                x2_sb = sb7("x2_sb", [128, 2, TT1, C], F32)
                xp_sb = sb7("xp_sb", [128, 2, C], F32)
                x1o_sb = sb7("x1o_sb", [128, 2, TT1, C], F32)
                st6b = sb7("st6b", [128, 24], F32)
                mv2 = sb7("mv2", [128, 2], F32)
                veps2 = sb7("veps2", [128, 1], F32)
                sd2 = sb7("sd2", [128, 1], F32)
                rs2 = sb7("rs2", [128, 1], F32)
                nmrs2 = sb7("nmrs2", [128, 1], F32)
                ps_hT = [sp7.enter_context(nc.psum_tensor(f"ps_hT{i}", [128, 2048], F32))
                         for i in range(2)]

                @block.sync
                def _(sync):
                    for k in range(NCH):
                        if k == 0:
                            rwait_all(sync, d_hw4, NCH)
                        _w(sync, p_tr7, NG * TT1 * (k - 1))
                        rinc(sync.dma_start(out=htin_sb[:, k % 2, :, :],
                                       in_=h_t[k]), d_hr7, k)

                @block.scalar
                def _(scalar):
                    for k in range(NCH):
                        if k == 0:
                            rwait_all(scalar, d_hw4, NCH)
                        _w(scalar, g_xp7, TT1 * (k - 1))
                        rinc(scalar.dma_start(out=x2_sb[:, k % 2, :, :],
                                         in_=x_ch1[k]), d_xr7, k)
                        for tt in range(TT1):
                            gt = k * TT1 + tt
                            scalar.wait_ge(vs7, 4 * gt + 3)
                            scalar.sqrt(sd2[:, 0:1], veps2[:, 0:1]).then_inc(s_sq7, 1)

                @block.tensor
                def _(tensor):
                    for k in range(NCH):
                        rwait(tensor, d_hr7, k)
                        for tt in range(TT1):
                            gt = k * TT1 + tt
                            _w(tensor, v_out7, gt - 1)
                            for ct in range(NG):
                                tensor.transpose(
                                    ps_hT[gt % 2][:, ct * 128:(ct + 1) * 128],
                                    htin_sb[:, k % 2, ct, tt * 128:(tt + 1) * 128],
                                    id32[:, :]).then_inc(p_tr7, 1)

                @block.vector
                def _(vector):
                    vcnt = 0
                    for k in range(NCH):
                        rwait(vector, d_x1w7, k - 2)
                        for tt in range(TT1):
                            gt = k * TT1 + tt
                            vector.wait_ge(p_tr7, NG * (gt + 1))
                            vcnt = _bn(vector, st6b[:, :], mv2[:, :],
                                       ps_hT[gt % 2][:, 0:C], C, vs7, vcnt)
                            vector.wait_ge(vs7, vcnt)
                            vector.tensor_scalar(
                                out=veps2[:, 0:1], in0=mv2[:, 1:2], scalar1=EPS,
                                scalar2=None, op0=ALU.add).then_inc(vs7, 1)
                            vcnt += 1
                            vector.wait_ge(s_sq7, gt + 1)
                            vector.reciprocal(out=rs2[:, 0:1],
                                              in_=sd2[:, 0:1]).then_inc(vs7, 1)
                            vcnt += 1
                            vector.wait_ge(vs7, vcnt)
                            vector.scalar_tensor_tensor(
                                out=nmrs2[:, 0:1], in0=mv2[:, 0:1], scalar=-1.0,
                                in1=rs2[:, 0:1], op0=ALU.mult,
                                op1=ALU.mult).then_inc(v_nm7, 1)
                            rwait(vector, d_xr7, k)
                            vector.wait_ge(v_nm7, gt + 1)
                            vector.tensor_scalar(
                                out=xp_sb[:, gt % 2, :], in0=x2_sb[:, k % 2, tt, :],
                                scalar1=nmrs2[:, 0:1], scalar2=None,
                                op0=ALU.add).then_inc(g_xp7, 1)
                            vector.wait_ge(g_xp7, gt + 1)
                            vector.scalar_tensor_tensor(
                                out=x1o_sb[:, k % 2, tt, :], in0=ps_hT[gt % 2][:, 0:C],
                                scalar=rs2[:, 0:1], in1=xp_sb[:, gt % 2, :],
                                op0=ALU.mult, op1=ALU.add).then_inc(v_out7, 1)

                @block.gpsimd
                def _(gpsimd):
                    for k in range(NCH):
                        gpsimd.wait_ge(v_out7, TT1 * (k + 1))
                        rinc(gpsimd.dma_start(out=x1_ch1[k],
                                         in_=x1o_sb[:, k % 2, :, :]), d_x1w7, k)

            with contextlib.ExitStack() as sp8:
                sb8 = lambda n, s, d: sp8.enter_context(nc.sbuf_tensor(n, s, d))
                x1a_sb = sb8("x1a_sb", [128, 2, C], F32)
                x1b_sb = sb8("x1b_sb", [128, 2, C], F32)
                xn1b_sb = sb8("xn1b_sb", [128, 2, C], BF16)
                xn1T_sb = sb8("xn1T_sb", [128, 2, NC, CH2], BF16)
                sfc_sb = sb8("sfc_sb", [128, 2, NC, 256], BF16)
                spj_sb = sb8("spj_sb", [128, 2, NH, 128], BF16)
                hfc_sb = sb8("hfc_sb", [128, NH, CH2], BF16)
                r_sb = sb8("r_sb", [128, 1, CH2], F32)
                y_sb = sb8("y_sb", [128, NC, CH2], BF16)
                outb_sb = sb8("outb_sb", [128, 1, C], F32)
                out2_sb = sb8("out2_sb", [128, 2, C], F32)
                st6c = sb8("st6c", [128, 24], F32)
                mv3 = sb8("mv3", [128, 2], F32)
                veps3 = sb8("veps3", [128, 2], F32)
                sd3 = sb8("sd3", [128, 2], F32)
                rs3 = sb8("rs3", [128, 2], F32)
                nmrs3 = sb8("nmrs3", [128, 2], F32)
                mv4 = sb8("mv4", [128, 2], F32)
                veps4 = sb8("veps4", [128, 1], F32)
                sd4 = sb8("sd4", [128, 1], F32)
                rstd4 = sb8("rstd4", [128, 1], F32)
                sceff = sb8("sceff", [128, 1], F32)
                bseff = sb8("bseff", [128, 1], F32)
                ps_fc = [sp8.enter_context(nc.psum_tensor(f"ps_fc{i}", [128, 512], F32))
                         for i in range(2)]
                ps_pj = [sp8.enter_context(nc.psum_tensor(f"ps_pj{i}", [128, 512], F32))
                         for i in range(2)]
                ps_yT = sp8.enter_context(nc.psum_tensor("ps_yT", [128, 2048], BF16))
                xn1_rows = xn1_d.rearrange("(n p) c -> n p c", p=128)
                x1_rows128 = x1_d.rearrange("(n p) c -> n p c", p=128)
                NGQ = NQ * TT2
                VS8_CHUNK = 3 * TT2 + 5 * TT2

                @block.gpsimd
                def _(gpsimd):
                    def x1b_seg(gpsimd, q):
                        for tt in range(TT2):
                            gq = q * TT2 + tt
                            _w(gpsimd, v_o8, gq - 1)
                            rinc(gpsimd.dma_start(out=x1b_sb[:, gq % 2, :],
                                           in_=x1_rows128[gq]), d_x1b, gq)

                    for q in range(NQ):
                        for tt in range(TT2):
                            gq = q * TT2 + tt
                            if gq == 0:
                                rwait_all(gpsimd, d_x1w7, NCH)
                            _w(gpsimd, s_nm8, gq - 1)
                            rinc(gpsimd.dma_start(out=x1a_sb[:, gq % 2, :],
                                           in_=x1_rows128[gq]), d_x1a, gq)
                        if q >= 1:
                            x1b_seg(gpsimd, q - 1)
                    x1b_seg(gpsimd, NQ - 1)

                @block.vector
                def _(vector):
                    vcnt = [0]

                    def ln3_seg(vector, q):
                        for tt in range(TT2):
                            gq = q * TT2 + tt
                            rwait(vector, d_x1a, gq)
                            _w(vector, s_sq8, gq - 1)
                            vcnt[0] = _bn(vector, st6c[:, :], mv3[:, :],
                                          x1a_sb[:, gq % 2, :], C, vs8, vcnt[0])
                            vector.wait_ge(vs8, vcnt[0])
                            vector.tensor_scalar(
                                out=veps3[:, gq % 2:gq % 2 + 1], in0=mv3[:, 1:2],
                                scalar1=EPS, scalar2=None,
                                op0=ALU.add).then_inc(v_st8, 1)
                            vector.wait_ge(s_sq8, gq + 1)
                            _w(vector, s_nm8, gq - 1)
                            vector.reciprocal(
                                out=rs3[:, gq % 2:gq % 2 + 1],
                                in_=sd3[:, gq % 2:gq % 2 + 1]).then_inc(vs8, 1)
                            vcnt[0] += 1
                            vector.wait_ge(vs8, vcnt[0])
                            vector.scalar_tensor_tensor(
                                out=nmrs3[:, gq % 2:gq % 2 + 1], in0=mv3[:, 0:1],
                                scalar=-1.0, in1=rs3[:, gq % 2:gq % 2 + 1],
                                op0=ALU.mult, op1=ALU.mult).then_inc(v_nm3, 1)

                    def ln4_seg(vector, q):
                        for tt in range(TT2):
                            gq = q * TT2 + tt
                            vector.wait_ge(p_yt, NC * (gq + 1))
                            vcnt[0] = _bn(vector, st6c[:, :], mv4[:, :],
                                          ps_yT[:, 0:C], C, vs8, vcnt[0])
                            vector.wait_ge(vs8, vcnt[0])
                            vector.tensor_scalar(
                                out=veps4[:, 0:1], in0=mv4[:, 1:2],
                                scalar1=amsq_pj[:, 0:1], scalar2=EPS,
                                op0=ALU.mult, op1=ALU.add).then_inc(vs8, 1)
                            vcnt[0] += 1
                            vector.wait_ge(s_sq8b, gq + 1)
                            vector.reciprocal(out=rstd4[:, 0:1],
                                              in_=sd4[:, 0:1]).then_inc(vs8, 1)
                            vcnt[0] += 1
                            vector.wait_ge(vs8, vcnt[0])
                            _w(vector, g_xp8, gq)
                            vector.tensor_scalar(
                                out=sceff[:, 0:1], in0=am_pj[:, 0:1],
                                scalar1=rstd4[:, 0:1], scalar2=None,
                                op0=ALU.mult).then_inc(vs8, 1)
                            vcnt[0] += 1
                            vector.wait_ge(vs8, vcnt[0])
                            vector.scalar_tensor_tensor(
                                out=bseff[:, 0:1], in0=mv4[:, 0:1], scalar=-1.0,
                                in1=sceff[:, 0:1], op0=ALU.mult,
                                op1=ALU.mult).then_inc(v_fold8, 1)
                            vector.wait_ge(g_xp8, gq + 1)
                            rwait(vector, d_x1b, gq)
                            rwait(vector, d_out8, gq - 2)
                            vector.scalar_tensor_tensor(
                                out=out2_sb[:, gq % 2, :], in0=outb_sb[:, 0, :],
                                scalar=1.0, in1=x1b_sb[:, gq % 2, :],
                                op0=ALU.mult, op1=ALU.add).then_inc(v_o8, 1)

                    ln3_seg(vector, 0)
                    for q in range(1, NQ):
                        ln3_seg(vector, q)
                        ln4_seg(vector, q - 1)
                    ln4_seg(vector, NQ - 1)

                @block.scalar
                def _(scalar):
                    LN3P = 3 * TT2
                    LN4P = 5 * TT2

                    def ln3s_seg(scalar, q):
                        for tt in range(TT2):
                            gq = q * TT2 + tt
                            scalar.wait_ge(v_st8, gq + 1)
                            scalar.sqrt(sd3[:, gq % 2:gq % 2 + 1],
                                        veps3[:, gq % 2:gq % 2 + 1]).then_inc(s_sq8, 1)
                            scalar.wait_ge(v_nm3, gq + 1)
                            rwait(scalar, d_xn1w, gq - 2)
                            scalar.activation(
                                out=xn1b_sb[:, gq % 2, :], in_=x1a_sb[:, gq % 2, :],
                                func=AF.Identity, bias=nmrs3[:, gq % 2:gq % 2 + 1],
                                scale=rs3[:, gq % 2:gq % 2 + 1]).then_inc(s_nm8, 1)
                            scalar.wait_ge(s_nm8, gq + 1)
                            rinc(scalar.dma_start(out=xn1_rows[gq],
                                             in_=xn1b_sb[:, gq % 2, :]), d_xn1w, gq)

                    def xn1T_seg(scalar, q):
                        if q == 0:
                            rwait_all(scalar, d_sw, NHB + NC)
                        rwait_all(scalar, d_xn1w, TT2 * (q + 1))
                        _w(scalar, p_fc, NH * (q - 1))
                        for ct in range(NC):
                            scalar.dma_start(
                                out=xn1T_sb[:, q % 2, ct, :],
                                in_=xn1_d[q * CH2:(q + 1) * CH2, ct * 128:(ct + 1) * 128],
                                transpose=True).then_inc(d_xn1T[q % 2], 16)

                    def comp_seg(scalar, q):
                        if q + 1 < NQ:
                            xn1T_seg(scalar, q + 1)
                        for ht in range(NH):
                            gh = q * NH + ht
                            scalar.wait_ge(p_fc, gh + 1)
                            _w(scalar, s_sqq, gh)
                            scalar.activation(
                                out=r_sb[:, 0, :], in_=ps_fc[gh % 2][:, 0:CH2],
                                func=AF.Relu, scale=am_fc[:, 0:1]).then_inc(s_rev, 1)
                            scalar.wait_ge(s_rev, gh + 1)
                            scalar.activation(
                                out=hfc_sb[:, ht, :], in_=r_sb[:, 0, :],
                                func=AF.Square).then_inc(s_sqq, 1)
                        for ct in range(NC):
                            gc = q * NC + ct
                            scalar.wait_ge(p_pj, gc + 1)
                            if ct == 0:
                                _w(scalar, p_yt, NC * TT2 * q)
                            scalar.copy(out=y_sb[:, ct, :],
                                        in_=ps_pj[gc % 2][:, 0:CH2]).then_inc(s_yev, 1)
                        for tt in range(TT2):
                            gq = q * TT2 + tt
                            target = LN3P * min(q + 2, NQ) + LN4P * q + 5 * tt + 3
                            scalar.wait_ge(vs8, target)
                            scalar.sqrt(sd4[:, 0:1], veps4[:, 0:1]).then_inc(s_sq8b, 1)
                            scalar.wait_ge(v_fold8, gq + 1)
                            _w(scalar, v_o8, gq)
                            scalar.activation(
                                out=outb_sb[:, 0, :], in_=ps_yT[:, 0:C],
                                func=AF.Identity, bias=bseff[:, 0:1],
                                scale=sceff[:, 0:1]).then_inc(g_xp8, 1)
                            scalar.wait_ge(v_o8, gq + 1)
                            rinc(scalar.dma_start(out=out_rows[gq],
                                             in_=out2_sb[:, gq % 2, :]), d_out8, gq)

                    ln3s_seg(scalar, 0)
                    xn1T_seg(scalar, 0)
                    for q in range(1, NQ):
                        ln3s_seg(scalar, q)
                        comp_seg(scalar, q - 1)
                    comp_seg(scalar, NQ - 1)
                    rwait_all(scalar, d_out8, NT128)

                @block.sync
                def _(sync):
                    for q in range(NQ):
                        if q == 0:
                            rwait_all(sync, d_sw, NHB + NC)
                            rwait_all(sync, d_x1w7, NCH)
                        for hb in range(NHB):
                            ghb = q * NHB + hb
                            _w(sync, p_fc, 2 * ghb - 2)
                            rinc(sync.dma_start(
                                out=sfc_sb[:, ghb % 2, :, :],
                                in_=sfcT_t[hb]), d_sfc, ghb)
                        for ct in range(NC):
                            gc = q * NC + ct
                            _w(sync, p_pj, gc - 1)
                            rinc(sync.dma_start(
                                out=spj_sb[:, gc % 2, :, :],
                                in_=spjT_t[ct]), d_spr, gc)

                @block.tensor
                def _(tensor):
                    for q in range(NQ):
                        tensor.wait_ge(d_xn1T[q % 2], 16 * NC * (q // 2 + 1))
                        for ht in range(NH):
                            gh = q * NH + ht
                            ghb = q * NHB + ht // 2
                            rwait(tensor, d_sfc, ghb)
                            _w(tensor, s_rev, gh - 1)
                            for ct in range(NC):
                                ins = tensor.matmul(
                                    ps_fc[gh % 2][:, 0:CH2],
                                    sfc_sb[:, ghb % 2, ct,
                                           (ht % 2) * 128:(ht % 2 + 1) * 128],
                                    xn1T_sb[:, q % 2, ct, :],
                                    start=(ct == 0), stop=(ct == NC - 1))
                            ins.then_inc(p_fc, 1)
                        tensor.wait_ge(s_sqq, NH * (q + 1))
                        for ct in range(NC):
                            gc = q * NC + ct
                            rwait(tensor, d_spr, gc)
                            _w(tensor, s_yev, gc - 1)
                            for ht in range(NH):
                                ins = tensor.matmul(
                                    ps_pj[gc % 2][:, 0:CH2],
                                    spj_sb[:, gc % 2, ht, :],
                                    hfc_sb[:, ht, :],
                                    start=(ht == 0), stop=(ht == NH - 1))
                            ins.then_inc(p_pj, 1)
                        for tt in range(TT2):
                            gq = q * TT2 + tt
                            _w(tensor, g_xp8, gq)
                            for ct in range(NC):
                                tensor.wait_ge(s_yev, q * NC + ct + 1)
                                tensor.transpose(
                                    ps_yT[:, ct * 128:(ct + 1) * 128],
                                    y_sb[:, ct, tt * 128:(tt + 1) * 128],
                                    idbf[:, :]).then_inc(p_yt, 1)

    return nc


def kernel(L0_attn_x, L0_attn_W, L0_attn_b, L0_ffn_w_fc, L0_ffn_w_proj):
    x = np.ascontiguousarray(np.asarray(L0_attn_x, dtype=np.float32))
    W = np.ascontiguousarray(np.asarray(L0_attn_W, dtype=np.float32))
    b = np.ascontiguousarray(np.asarray(L0_attn_b, dtype=np.float32))
    wfc = np.ascontiguousarray(np.asarray(L0_ffn_w_fc, dtype=np.float32))
    wpj = np.ascontiguousarray(np.asarray(L0_ffn_w_proj, dtype=np.float32))

    nc = build_program(FULL_CFG)
    ND = FULL_CFG["D"] // 128
    bT = np.ascontiguousarray(b.reshape(ND, 128).T)
    id32 = np.eye(128, dtype=np.float32)
    idbf = np.eye(128, dtype=ml_dtypes.bfloat16)
    in_maps = []
    for i in range(8):
        shard = np.ascontiguousarray(x[2 * i:2 * i + 2].reshape(4096, 2048))
        in_maps.append(dict(x=shard, W=W, bT=bT, wfc=wfc, wpj=wpj,
                            id32=id32, idbf=idbf))
    res = run_bass_kernel_spmd(nc, in_maps, core_ids=list(range(8)))
    outs = [np.asarray(r["out"]).reshape(2, 2048, 2048) for r in res.results]
    return np.concatenate(outs, axis=0)


# version 23
# speedup vs baseline: 1.2000x; 1.0227x over previous
import contextlib
import numpy as np
import ml_dtypes
import concourse.bass as bass
import concourse.mybir as mybir
from concourse.bass_utils import run_bass_kernel_spmd

F32 = mybir.dt.float32
BF16 = mybir.dt.bfloat16
AF = mybir.ActivationFunctionType
ALU = mybir.AluOpType
AX = mybir.AxisListType

FULL_CFG = dict(T=4096, C=2048, D=4096, H=5120, TB=2048, CH1=256, CH2=512)
EPS = 1e-5


def _w(eng, s, v):
    if v > 0:
        eng.wait_ge(s, v)


def rinc(instr, ring, i):
    instr.then_inc(ring[i % len(ring)], 16)


def rwait(eng, ring, i):
    if i >= 0:
        n = len(ring)
        eng.wait_ge(ring[i % n], 16 * (i // n + 1))


def rwait_all(eng, ring, n):
    r = len(ring)
    for s in range(r):
        c = (n - s + r - 1) // r
        if c > 0:
            eng.wait_ge(ring[s], 16 * c)


def _bn(vec, st6, dst_mv, src, width, self_sem, ctr):
    nb = max(1, width // 512)
    step = width // nb
    assert step * nb == width and step <= 512 and step % 2 == 0
    for j in range(nb):
        ins = vec.bn_stats(st6[:, 6 * j:6 * j + 6], src[:, j * step:(j + 1) * step])
    ins.then_inc(self_sem, 1)
    ctr += 1
    vec.wait_ge(self_sem, ctr)
    vec.bn_aggr(dst_mv, st6[:, 0:6 * nb]).then_inc(self_sem, 1)
    return ctr + 1


def build_program(cfg):
    T, C, D, H = cfg["T"], cfg["C"], cfg["D"], cfg["H"]
    TB, CH1, CH2 = cfg["TB"], cfg["CH1"], cfg["CH2"]

    NC = C // 128
    ND = D // 128
    NG = NC
    HALF = NG // 2
    NH = H // 128
    NHB = NH // 2
    NCH = T // CH1
    NQ = T // CH2
    TT1 = CH1 // 128
    TT2 = CH2 // 128
    RESET = TB // CH1
    GPC = 2 * HALF
    NT128 = T // 128

    nc = bass.Bass("TRN2", target_bir_lowering=False)

    x_d = nc.dram_tensor("x", [T, C], F32, kind="ExternalInput").ap()
    w_d = nc.dram_tensor("W", [D, C], F32, kind="ExternalInput").ap()
    bT_d = nc.dram_tensor("bT", [128, ND], F32, kind="ExternalInput").ap()
    wfc_d = nc.dram_tensor("wfc", [H, C], F32, kind="ExternalInput").ap()
    wpj_d = nc.dram_tensor("wpj", [C, H], F32, kind="ExternalInput").ap()
    id32_d = nc.dram_tensor("id32", [128, 128], F32, kind="ExternalInput").ap()
    idbf_d = nc.dram_tensor("idbf", [128, 128], BF16, kind="ExternalInput").ap()
    out_d = nc.dram_tensor("out", [T, C], F32, kind="ExternalOutput").ap()
    wbf_d = nc.dram_tensor("wbf_s", [D, C], BF16).ap()
    xn_d = nc.dram_tensor("xn_s", [T, C], BF16).ap()
    h_d = None
    x1_d = nc.dram_tensor("x1_s", [T, C], F32).ap()
    xn1_d = nc.dram_tensor("xn1_s", [T, C], BF16).ap()
    h_t = nc.dram_tensor("h_t", [T // CH1, 128, C // 128, CH1], F32).ap()
    sfcT_t = nc.dram_tensor("sfcT_t", [H // 256, 128, C // 128, 256], BF16).ap()
    spjT_t = nc.dram_tensor("spjT_t", [C // 128, 128, H // 128, 128], BF16).ap()

    x_ch1 = x_d.rearrange("(k tt p) c -> k p tt c", p=128, tt=TT1)
    xn_ch1 = xn_d.rearrange("(k tt p) c -> k p tt c", p=128, tt=TT1)
    x1_ch1 = x1_d.rearrange("(k tt p) c -> k p tt c", p=128, tt=TT1)
    x1_ch2 = x1_d.rearrange("(q tt p) c -> q p tt c", p=128, tt=TT2)
    xn1_ch2 = xn1_d.rearrange("(q tt p) c -> q p tt c", p=128, tt=TT2)
    out_rows = out_d.rearrange("(n p) c -> n p c", p=128)

    est = contextlib.ExitStack()
    sbuf = lambda n, s, d: est.enter_context(nc.sbuf_tensor(n, s, d))
    sem = lambda n: est.enter_context(nc.semaphore(n))

    with est:
        bT_sb = sbuf("bT_sb", [128, ND], F32)
        id32 = sbuf("id32_sb", [128, 128], F32)
        idbf = sbuf("idbf_sb", [128, 128], BF16)
        ones_row = sbuf("ones_row", [1, 128], F32)
        am_fc = sbuf("am_fc", [128, 1], F32)
        thr_fc = sbuf("thr_fc", [128, 1], F32)
        nthr_fc = sbuf("nthr_fc", [128, 1], F32)
        am_pj = sbuf("am_pj", [128, 1], F32)
        thr_pj = sbuf("thr_pj", [128, 1], F32)
        nthr_pj = sbuf("nthr_pj", [128, 1], F32)
        amsq_pj = sbuf("amsq_pj", [128, 1], F32)
        tot1 = sbuf("tot1", [1, 2], F32)
        tot2 = sbuf("tot2", [1, 2], F32)
        red1 = sbuf("red1", [128, 1], F32)
        carry = sbuf("carry", [128, NG], F32)

        ring = lambda n, r=2: [sem(n + str(i)) for i in range(r)]
        d_prep = sem("d_prep"); d_cast = sem("d_cast"); v_prep = sem("v_prep")
        d_wf = ring("d_wf"); v_abs = sem("v_abs"); fz = sem("fz")
        d_sf = ring("d_sf"); v_sg = sem("v_sg"); g_sg = sem("g_sg")
        p_tr2 = sem("p_tr2"); s_ev2 = sem("s_ev2"); d_sw = ring("d_sw")
        d_wt3 = sem("d_wt3"); d_wt5 = sem("d_wt5")
        d_x4 = ring("d_x4"); vs4 = sem("vs4"); s_sq4 = sem("s_sq4")
        v_lna = sem("v_lna"); v_lnb = sem("v_lnb"); s_nm4 = sem("s_nm4")
        d_xnw4 = ring("d_xnw4")
        d_xnT4 = ring("d_xnT4", 3); p_mm4 = sem("p_mm4"); s_ev4 = sem("s_ev4")
        v_sc4 = sem("v_sc4"); v_cr4 = sem("v_cr4"); d_hw4 = ring("d_hw4")
        d_xnT6 = ring("d_xnT6"); p_mm6 = sem("p_mm6"); s_ev6 = sem("s_ev6")
        v_sc6 = sem("v_sc6"); v_cr6 = sem("v_cr6"); d_hw6 = ring("d_hw6")
        d_hr7 = ring("d_hr7"); d_xr7 = ring("d_xr7"); p_tr7 = sem("p_tr7")
        vs7 = sem("vs7"); s_sq7 = sem("s_sq7"); v_nm7 = sem("v_nm7")
        g_xp7 = sem("g_xp7"); v_out7 = sem("v_out7"); d_x1w7 = ring("d_x1w7")
        d_x1a = ring("d_x1a"); d_x1b = ring("d_x1b")
        vs8 = sem("vs8"); s_sq8 = sem("s_sq8")
        v_st8 = sem("v_st8"); v_nm3 = sem("v_nm3"); s_nm8 = sem("s_nm8")
        d_xn1w = ring("d_xn1w"); d_xn1T = ring("d_xn1T")
        d_sfc = ring("d_sfc", 2); d_spr = ring("d_spr")
        p_fc = sem("p_fc"); s_rev = sem("s_rev"); s_sqq = sem("s_sqq")
        p_pj = sem("p_pj"); s_yev = sem("s_yev"); p_yt = sem("p_yt")
        s_sq8b = sem("s_sq8b"); v_fold8 = sem("v_fold8")
        g_xp8 = sem("g_xp8"); v_o8 = sem("v_o8"); d_out8 = ring("d_out8", 2)

        with nc.Block() as block:
            @block.sync
            def _(sync):
                sync.dma_start(out=bT_sb[:, :], in_=bT_d[:, :]).then_inc(d_prep, 16)
                sync.dma_start(out=id32[:, :], in_=id32_d[:, :]).then_inc(d_prep, 16)
                sync.dma_start(out=idbf[:, :], in_=idbf_d[:, :]).then_inc(d_prep, 16)

            @block.gpsimd
            def _(gpsimd):
                q4 = D // 4
                for i in range(4):
                    gpsimd.dma_start(out=wbf_d[i * q4:(i + 1) * q4, :],
                                     in_=w_d[i * q4:(i + 1) * q4, :]).then_inc(d_cast, 16)

            @block.vector
            def _(vector):
                vector.memset(ones_row[:, :], 1.0).then_inc(v_prep, 1)

            with contextlib.ExitStack() as s12:
                sb12 = lambda n, s, d: s12.enter_context(nc.sbuf_tensor(n, s, d))
                ps12 = lambda n, s, d: s12.enter_context(nc.psum_tensor(n, s, d))
                MW = max(C, H)
                wslab = sb12("wslab", [128, 2, MW], F32)
                acc_fc = sb12("acc_fc", [128, NH], F32)
                acc_pj = sb12("acc_pj", [128, NC], F32)
                nbs = sb12("nbs", [128, 2, MW], BF16)
                ssl = sb12("ssl", [128, 2, MW], BF16)
                stage_fc = sb12("stage_fc", [128, 2, NC, 256], BF16)
                stage_pj = sb12("stage_pj", [128, 2, NH, 128], BF16)
                ps_misc = ps12("ps_misc", [128, 512], F32)
                ps_bf = [ps12("ps_bf0", [128, 1024], BF16),
                         ps12("ps_bf1", [128, 1024], BF16)]

                @block.sync
                def _(sync):
                    for i in range(NH):
                        _w(sync, v_abs, i - 1)
                        rinc(sync.dma_start(out=wslab[:, i % 2, 0:C],
                                            in_=wfc_d[i * 128:(i + 1) * 128, :]), d_wf, i)
                    for j in range(NC):
                        _w(sync, v_abs, NH + j - 1)
                        rinc(sync.dma_start(out=wslab[:, j % 2, 0:H],
                                            in_=wpj_d[j * 128:(j + 1) * 128, :]), d_wf, NH + j)

                @block.vector
                def _(vector):
                    for i in range(NH):
                        rwait(vector, d_wf, i)
                        vector.tensor_reduce(out=acc_fc[:, i:i + 1], in_=wslab[:, i % 2, 0:C],
                                             axis=AX.X, op=ALU.add,
                                             apply_absolute_value=True).then_inc(v_abs, 1)
                    for j in range(NC):
                        rwait(vector, d_wf, NH + j)
                        vector.tensor_reduce(out=acc_pj[:, j:j + 1], in_=wslab[:, j % 2, 0:H],
                                             axis=AX.X, op=ALU.add,
                                             apply_absolute_value=True).then_inc(v_abs, 1)

                fzc = [0]

                def vst(vec, f):
                    _w(vec, fz, fzc[0])
                    f().then_inc(fz, 1)
                    fzc[0] += 1

                def tst(tensor, f):
                    _w(tensor, fz, fzc[0])
                    f().then_inc(fz, 1)
                    fzc[0] += 1

                @block.vector
                def _(vector):
                    vector.wait_ge(v_abs, NH + NC)
                    vst(vector, lambda: vector.tensor_reduce(
                        out=red1[:, 0:1], in_=acc_fc[:, 0:NH], axis=AX.X, op=ALU.add))

                @block.tensor
                def _(tensor):
                    tensor.wait_ge(d_prep, 48)
                    tensor.wait_ge(v_prep, 1)
                    tst(tensor, lambda: tensor.transpose(
                        ps_misc[0:1, 0:128], red1[:, 0:1], id32[:, :]))

                @block.vector
                def _(vector):
                    vst(vector, lambda: vector.tensor_reduce(
                        out=tot1[0:1, 0:1], in_=ps_misc[0:1, 0:128], axis=AX.X, op=ALU.add))
                    vst(vector, lambda: vector.tensor_scalar(
                        out=tot2[0:1, 0:1], in0=tot1[0:1, 0:1],
                        scalar1=1.0 / (H * C), scalar2=EPS, op0=ALU.mult, op1=ALU.max))

                @block.tensor
                def _(tensor):
                    tst(tensor, lambda: tensor.matmul(
                        ps_misc[0:128, 256:257], ones_row[0:1, 0:128],
                        tot2[0:1, 0:1], start=True, stop=True))

                @block.vector
                def _(vector):
                    vst(vector, lambda: vector.tensor_copy(
                        out=am_fc[:, 0:1], in_=ps_misc[0:128, 256:257]))
                    vst(vector, lambda: vector.tensor_scalar(
                        out=thr_fc[:, 0:1], in0=am_fc[:, 0:1], scalar1=0.7,
                        scalar2=None, op0=ALU.mult))
                    vst(vector, lambda: vector.tensor_scalar(
                        out=nthr_fc[:, 0:1], in0=thr_fc[:, 0:1], scalar1=-1.0,
                        scalar2=None, op0=ALU.mult))
                    vst(vector, lambda: vector.tensor_reduce(
                        out=red1[:, 0:1], in_=acc_pj[:, 0:NC], axis=AX.X, op=ALU.add))

                @block.tensor
                def _(tensor):
                    tst(tensor, lambda: tensor.transpose(
                        ps_misc[0:1, 0:128], red1[:, 0:1], id32[:, :]))

                @block.vector
                def _(vector):
                    vst(vector, lambda: vector.tensor_reduce(
                        out=tot1[0:1, 1:2], in_=ps_misc[0:1, 0:128], axis=AX.X, op=ALU.add))
                    vst(vector, lambda: vector.tensor_scalar(
                        out=tot2[0:1, 1:2], in0=tot1[0:1, 1:2],
                        scalar1=1.0 / (H * C), scalar2=EPS, op0=ALU.mult, op1=ALU.max))

                @block.tensor
                def _(tensor):
                    tst(tensor, lambda: tensor.matmul(
                        ps_misc[0:128, 256:257], ones_row[0:1, 0:128],
                        tot2[0:1, 1:2], start=True, stop=True))

                @block.vector
                def _(vector):
                    vst(vector, lambda: vector.tensor_copy(
                        out=am_pj[:, 0:1], in_=ps_misc[0:128, 256:257]))
                    vst(vector, lambda: vector.tensor_scalar(
                        out=thr_pj[:, 0:1], in0=am_pj[:, 0:1], scalar1=0.7,
                        scalar2=None, op0=ALU.mult))
                    vst(vector, lambda: vector.tensor_scalar(
                        out=nthr_pj[:, 0:1], in0=thr_pj[:, 0:1], scalar1=-1.0,
                        scalar2=None, op0=ALU.mult))
                    vst(vector, lambda: vector.tensor_scalar(
                        out=amsq_pj[:, 0:1], in0=am_pj[:, 0:1], scalar1=am_pj[:, 0:1],
                        scalar2=None, op0=ALU.mult))

                FZ_END = fzc[0]

                @block.sync
                def _(sync):
                    for i in range(NH):
                        if i < 2:
                            sync.wait_ge(v_abs, NH + NC)
                        else:
                            sync.wait_ge(v_sg, 2 * (i - 2) + 2)
                        rinc(sync.dma_start(out=wslab[:, i % 2, 0:C],
                                            in_=wfc_d[i * 128:(i + 1) * 128, :]), d_sf, i)
                    for j in range(NC):
                        if j < 2:
                            sync.wait_ge(v_sg, 2 * (NH - 2 + j) + 2)
                        else:
                            sync.wait_ge(g_sg, 2 * (j - 2) + 2)
                        rinc(sync.dma_start(out=wslab[:, j % 2, 0:H],
                                            in_=wpj_d[j * 128:(j + 1) * 128, :]), d_sf, NH + j)

                @block.vector
                def _(vector):
                    vector.wait_ge(fz, FZ_END)
                    for i in range(NH):
                        rwait(vector, d_sf, i)
                        _w(vector, p_tr2, NC * (i - 1))
                        vector.tensor_scalar(
                            out=nbs[:, i % 2, 0:C], in0=wslab[:, i % 2, 0:C],
                            scalar1=nthr_fc[:, 0:1], scalar2=-1.0,
                            op0=ALU.is_le, op1=ALU.mult).then_inc(v_sg, 1)
                        vector.wait_ge(v_sg, 2 * i + 1)
                        vector.scalar_tensor_tensor(
                            out=ssl[:, i % 2, 0:C], in0=wslab[:, i % 2, 0:C],
                            scalar=thr_fc[:, 0:1], in1=nbs[:, i % 2, 0:C],
                            op0=ALU.is_ge, op1=ALU.add).then_inc(v_sg, 1)
                    for j in range(NC):
                        rwait(vector, d_sf, NH + j)
                        if j < 2:
                            vector.wait_ge(p_tr2, NC * (NH - 1 + j))
                        else:
                            vector.wait_ge(p_tr2, NC * NH + NH * (j - 1))
                        vector.tensor_scalar(
                            out=nbs[:, j % 2, 0:H], in0=wslab[:, j % 2, 0:H],
                            scalar1=nthr_pj[:, 0:1], scalar2=-1.0,
                            op0=ALU.is_le, op1=ALU.mult).then_inc(g_sg, 1)
                        vector.wait_ge(g_sg, 2 * j + 1)
                        vector.scalar_tensor_tensor(
                            out=ssl[:, j % 2, 0:H], in0=wslab[:, j % 2, 0:H],
                            scalar=thr_pj[:, 0:1], in1=nbs[:, j % 2, 0:H],
                            op0=ALU.is_ge, op1=ALU.add).then_inc(g_sg, 1)


                @block.tensor
                def _(tensor):
                    tcnt = 0
                    for i in range(NH):
                        tensor.wait_ge(v_sg, 2 * i + 2)
                        for ct in range(NC):
                            _w(tensor, s_ev2, tcnt - 1)
                            tensor.transpose(ps_bf[tcnt % 2][0:128, 0:128],
                                             ssl[:, i % 2, ct * 128:(ct + 1) * 128],
                                             idbf[:, :]).then_inc(p_tr2, 1)
                            tcnt += 1
                    for j in range(NC):
                        tensor.wait_ge(g_sg, 2 * j + 2)
                        for ht in range(NH):
                            _w(tensor, s_ev2, tcnt - 1)
                            tensor.transpose(ps_bf[tcnt % 2][0:128, 0:128],
                                             ssl[:, j % 2, ht * 128:(ht + 1) * 128],
                                             idbf[:, :]).then_inc(p_tr2, 1)
                            tcnt += 1

                @block.scalar
                def _(scalar):
                    tcnt = 0
                    for i in range(NH):
                        hb, hoff = i // 2, (i % 2) * 128
                        if i % 2 == 0:
                            rwait(scalar, d_sw, hb - 2)
                        for ct in range(NC):
                            scalar.wait_ge(p_tr2, tcnt + 1)
                            scalar.copy(out=stage_fc[:, hb % 2, ct, hoff:hoff + 128],
                                        in_=ps_bf[tcnt % 2][0:128, 0:128]).then_inc(s_ev2, 1)
                            tcnt += 1
                    for j in range(NC):
                        rwait(scalar, d_sw, NHB + j - 2)
                        for ht in range(NH):
                            scalar.wait_ge(p_tr2, tcnt + 1)
                            scalar.copy(out=stage_pj[:, j % 2, ht, :],
                                        in_=ps_bf[tcnt % 2][0:128, 0:128]).then_inc(s_ev2, 1)
                            tcnt += 1

                @block.gpsimd
                def _(gpsimd):
                    for hb in range(NHB):
                        gpsimd.wait_ge(s_ev2, NC * (2 * hb + 2))
                        rinc(gpsimd.dma_start(out=sfcT_t[hb],
                                         in_=stage_fc[:, hb % 2, :, :]), d_sw, hb)
                    for j in range(NC):
                        gpsimd.wait_ge(s_ev2, NC * NH + NH * (j + 1))
                        rinc(gpsimd.dma_start(out=spjT_t[j],
                                         in_=stage_pj[:, j % 2, :, :]), d_sw, NHB + j)

            S2_DONE = 16 * (NH + NC)

            with contextlib.ExitStack() as sp1:
                sb1 = lambda n, s, d: sp1.enter_context(nc.sbuf_tensor(n, s, d))
                wt_sb = sb1("wt_sb", [128, NC, 2 * NG * 128], BF16)
                x_sb = sb1("x_sb", [128, 1, TT1, C], F32)
                xnb_sb = sb1("xnb_sb", [128, 1, TT1, C], BF16)
                xnT_sb = sb1("xnT_sb", [128, 3, NC, CH1], BF16)
                g_sb = sb1("g_sb", [128, 2, CH1], F32)
                v_sb = sb1("v_sb", [128, 2, CH1], F32)
                h_sb = sb1("h_sb", [128, 1, NG, CH1], F32)
                st6a = sb1("st6a", [128, TT1, 24], F32)
                mv1 = sb1("mv1", [128, TT1, 2], F32)
                veps1 = sb1("veps1", [128, TT1], F32)
                sd1 = sb1("sd1", [128, TT1], F32)
                rs1 = sb1("rs1", [128, TT1], F32)
                nmrs1 = sb1("nmrs1", [128, TT1], F32)
                ps_par = [sp1.enter_context(nc.psum_tensor(f"ps_par{i}", [128, 512], F32))
                          for i in range(4)]
                GP1 = 2 * NG

                @block.sync
                def _(sync):
                    sync.wait_ge(d_cast, 64)
                    rwait_all(sync, d_sw, NHB + NC)
                    for ct in range(NC):
                        sync.dma_start(out=wt_sb[:, ct, 0:NG * 128],
                                       in_=wbf_d[0:C, ct * 128:(ct + 1) * 128],
                                       transpose=True).then_inc(d_wt3, 16)
                        sync.dma_start(out=wt_sb[:, ct, NG * 128:2 * NG * 128],
                                       in_=wbf_d[C:D, ct * 128:(ct + 1) * 128],
                                       transpose=True).then_inc(d_wt3, 16)

                @block.sync
                def _(sync):
                    for k in range(NCH):
                        _w(sync, s_nm4, TT1 * k)
                        rinc(sync.dma_start(out=x_sb[:, 0, :, :],
                                       in_=x_ch1[k]), d_x4, k)
                        rwait(sync, d_xnw4, k)
                        _w(sync, p_mm4, GP1 * (k - 2))
                        for ct in range(NC):
                            sync.dma_start(
                                out=xnT_sb[:, k % 3, ct, :],
                                in_=xn_d[k * CH1:(k + 1) * CH1, ct * 128:(ct + 1) * 128],
                                transpose=True).then_inc(d_xnT4[k % 3], 16)

                @block.vector
                def _(vector):
                    vcnt = 0

                    def stats_seg(vector, k, vcnt):
                        rwait(vector, d_x4, k)
                        for tt in range(TT1):
                            vcnt = _bn(vector, st6a[:, tt, :], mv1[:, tt, :],
                                       x_sb[:, 0, tt, :], C, vs4, vcnt)
                        vector.wait_ge(vs4, vcnt)
                        _w(vector, s_sq4, k)
                        for tt in range(TT1):
                            ins = vector.tensor_scalar(
                                out=veps1[:, tt:tt + 1], in0=mv1[:, tt, 1:2],
                                scalar1=EPS, scalar2=None, op0=ALU.add)
                        ins.then_inc(v_lna, 1)
                        vector.wait_ge(s_sq4, k + 1)
                        _w(vector, s_nm4, TT1 * k)
                        vector.reciprocal(out=rs1[:, 0:TT1],
                                          in_=sd1[:, 0:TT1]).then_inc(vs4, 1)
                        vcnt += 1
                        vector.wait_ge(vs4, vcnt)
                        for tt in range(TT1):
                            ins = vector.scalar_tensor_tensor(
                                out=nmrs1[:, tt:tt + 1], in0=mv1[:, tt, 0:1],
                                scalar=-1.0, in1=rs1[:, tt:tt + 1],
                                op0=ALU.mult, op1=ALU.mult)
                        ins.then_inc(v_lnb, 1)
                        return vcnt

                    def scan_seg(vector, k):
                        for pair in range(NG):
                            gp = k * NG + pair
                            vector.wait_ge(s_ev4, k * GP1 + 2 * pair + 2)
                            rwait(vector, d_hw4, k - 1)
                            if k % RESET == 0:
                                init = 0.0
                            else:
                                vector.wait_ge(v_cr4, (k - 1) * NG + pair + 1)
                                init = carry[:, pair:pair + 1]
                            vector.tensor_tensor_scan(
                                out=h_sb[:, 0, pair, :],
                                data0=g_sb[:, gp % 2, :], data1=v_sb[:, gp % 2, :],
                                initial=init, op0=ALU.mult,
                                op1=ALU.add).then_inc(v_sc4, 1)
                            vector.wait_ge(v_sc4, gp + 1)
                            vector.tensor_copy(
                                out=carry[:, pair:pair + 1],
                                in_=h_sb[:, 0, pair, CH1 - 1:CH1]).then_inc(v_cr4, 1)

                    vcnt = stats_seg(vector, 0, vcnt)
                    for k in range(1, NCH):
                        vcnt = stats_seg(vector, k, vcnt)
                        scan_seg(vector, k - 1)
                    scan_seg(vector, NCH - 1)

                @block.scalar
                def _(scalar):
                    scalar.wait_ge(d_prep, 48)

                    def ln1_seg(scalar, k):
                        scalar.wait_ge(v_lna, k + 1)
                        scalar.sqrt(sd1[:, 0:TT1],
                                    veps1[:, 0:TT1]).then_inc(s_sq4, 1)
                        scalar.wait_ge(v_lnb, k + 1)
                        rwait(scalar, d_xnw4, k - 1)
                        for tt in range(TT1):
                            scalar.activation(
                                out=xnb_sb[:, 0, tt, :],
                                in_=x_sb[:, 0, tt, :], func=AF.Identity,
                                bias=nmrs1[:, tt:tt + 1],
                                scale=rs1[:, tt:tt + 1]).then_inc(s_nm4, 1)
                        scalar.wait_ge(s_nm4, TT1 * (k + 1))
                        rinc(scalar.dma_start(out=xn_ch1[k],
                                         in_=xnb_sb[:, 0, :, :]), d_xnw4, k)

                    def evac_seg(scalar, k):
                        for pair in range(NG):
                            for role in range(2):
                                gi = k * GP1 + 2 * pair + role
                                gp = k * NG + pair
                                scalar.wait_ge(p_mm4, gi + 1)
                                _w(scalar, v_sc4, gp - 1)
                                if role == 0:
                                    scalar.activation(
                                        out=g_sb[:, gp % 2, :],
                                        in_=ps_par[gi % 4][:, 0:CH1],
                                        func=AF.Sigmoid,
                                        bias=bT_sb[:, pair:pair + 1]).then_inc(s_ev4, 1)
                                else:
                                    scalar.activation(
                                        out=v_sb[:, gp % 2, :],
                                        in_=ps_par[gi % 4][:, 0:CH1],
                                        func=AF.Tanh,
                                        bias=bT_sb[:, NG + pair:NG + pair + 1]).then_inc(s_ev4, 1)

                    ln1_seg(scalar, 0)
                    for k in range(1, NCH):
                        ln1_seg(scalar, k)
                        evac_seg(scalar, k - 1)
                    evac_seg(scalar, NCH - 1)

                @block.tensor
                def _(tensor):
                    tensor.wait_ge(d_wt3, 32 * NC)
                    for k in range(NCH):
                        tensor.wait_ge(d_xnT4[k % 3], 16 * NC * (k // 3 + 1))
                        for pair in range(NG):
                            for role in range(2):
                                gi = k * GP1 + 2 * pair + role
                                _w(tensor, s_ev4, gi - 3)
                                col0 = (role * NG + pair) * 128
                                pp = ps_par[gi % 4]
                                for ct in range(NC):
                                    ins = tensor.matmul(
                                        pp[:, 0:CH1],
                                        wt_sb[:, ct, col0:col0 + 128],
                                        xnT_sb[:, k % 3, ct, :],
                                        start=(ct == 0), stop=(ct == NC - 1))
                                ins.then_inc(p_mm4, 1)

                @block.gpsimd
                def _(gpsimd):
                    for k in range(NCH):
                        gpsimd.wait_ge(v_sc4, NG * (k + 1))
                        rinc(gpsimd.dma_start(
                            out=h_t[k][:, 0:NG, :],
                            in_=h_sb[:, 0, :, :]), d_hw4, k)

            with contextlib.ExitStack() as sp7:
                sb7 = lambda n, s, d: sp7.enter_context(nc.sbuf_tensor(n, s, d))
                htin_sb = sb7("htin_sb", [128, 2, NG, CH1], F32)
                x2_sb = sb7("x2_sb", [128, 2, TT1, C], F32)
                xp_sb = sb7("xp_sb", [128, 2, C], F32)
                x1o_sb = sb7("x1o_sb", [128, 2, TT1, C], F32)
                st6b = sb7("st6b", [128, 24], F32)
                mv2 = sb7("mv2", [128, 2], F32)
                veps2 = sb7("veps2", [128, 1], F32)
                sd2 = sb7("sd2", [128, 1], F32)
                rs2 = sb7("rs2", [128, 1], F32)
                nmrs2 = sb7("nmrs2", [128, 1], F32)
                ps_hT = [sp7.enter_context(nc.psum_tensor(f"ps_hT{i}", [128, 2048], F32))
                         for i in range(2)]

                @block.sync
                def _(sync):
                    for k in range(NCH):
                        if k == 0:
                            rwait_all(sync, d_hw4, NCH)
                        _w(sync, p_tr7, NG * TT1 * (k - 1))
                        rinc(sync.dma_start(out=htin_sb[:, k % 2, :, :],
                                       in_=h_t[k]), d_hr7, k)

                @block.scalar
                def _(scalar):
                    for k in range(NCH):
                        if k == 0:
                            rwait_all(scalar, d_hw4, NCH)
                        _w(scalar, g_xp7, TT1 * (k - 1))
                        rinc(scalar.dma_start(out=x2_sb[:, k % 2, :, :],
                                         in_=x_ch1[k]), d_xr7, k)
                        for tt in range(TT1):
                            gt = k * TT1 + tt
                            scalar.wait_ge(vs7, 4 * gt + 3)
                            scalar.sqrt(sd2[:, 0:1], veps2[:, 0:1]).then_inc(s_sq7, 1)

                @block.tensor
                def _(tensor):
                    for k in range(NCH):
                        rwait(tensor, d_hr7, k)
                        for tt in range(TT1):
                            gt = k * TT1 + tt
                            _w(tensor, v_out7, gt - 1)
                            for ct in range(NG):
                                tensor.transpose(
                                    ps_hT[gt % 2][:, ct * 128:(ct + 1) * 128],
                                    htin_sb[:, k % 2, ct, tt * 128:(tt + 1) * 128],
                                    id32[:, :]).then_inc(p_tr7, 1)

                @block.vector
                def _(vector):
                    vcnt = 0
                    for k in range(NCH):
                        rwait(vector, d_x1w7, k - 2)
                        for tt in range(TT1):
                            gt = k * TT1 + tt
                            vector.wait_ge(p_tr7, NG * (gt + 1))
                            vcnt = _bn(vector, st6b[:, :], mv2[:, :],
                                       ps_hT[gt % 2][:, 0:C], C, vs7, vcnt)
                            vector.wait_ge(vs7, vcnt)
                            vector.tensor_scalar(
                                out=veps2[:, 0:1], in0=mv2[:, 1:2], scalar1=EPS,
                                scalar2=None, op0=ALU.add).then_inc(vs7, 1)
                            vcnt += 1
                            vector.wait_ge(s_sq7, gt + 1)
                            vector.reciprocal(out=rs2[:, 0:1],
                                              in_=sd2[:, 0:1]).then_inc(vs7, 1)
                            vcnt += 1
                            vector.wait_ge(vs7, vcnt)
                            vector.scalar_tensor_tensor(
                                out=nmrs2[:, 0:1], in0=mv2[:, 0:1], scalar=-1.0,
                                in1=rs2[:, 0:1], op0=ALU.mult,
                                op1=ALU.mult).then_inc(v_nm7, 1)
                            rwait(vector, d_xr7, k)
                            vector.wait_ge(v_nm7, gt + 1)
                            vector.tensor_scalar(
                                out=xp_sb[:, gt % 2, :], in0=x2_sb[:, k % 2, tt, :],
                                scalar1=nmrs2[:, 0:1], scalar2=None,
                                op0=ALU.add).then_inc(g_xp7, 1)
                            vector.wait_ge(g_xp7, gt + 1)
                            vector.scalar_tensor_tensor(
                                out=x1o_sb[:, k % 2, tt, :], in0=ps_hT[gt % 2][:, 0:C],
                                scalar=rs2[:, 0:1], in1=xp_sb[:, gt % 2, :],
                                op0=ALU.mult, op1=ALU.add).then_inc(v_out7, 1)

                @block.gpsimd
                def _(gpsimd):
                    for k in range(NCH):
                        gpsimd.wait_ge(v_out7, TT1 * (k + 1))
                        rinc(gpsimd.dma_start(out=x1_ch1[k],
                                         in_=x1o_sb[:, k % 2, :, :]), d_x1w7, k)

            with contextlib.ExitStack() as sp8:
                sb8 = lambda n, s, d: sp8.enter_context(nc.sbuf_tensor(n, s, d))
                x1a_sb = sb8("x1a_sb", [128, 2, C], F32)
                x1b_sb = sb8("x1b_sb", [128, 2, C], F32)
                xn1b_sb = sb8("xn1b_sb", [128, 2, C], BF16)
                xn1T_sb = sb8("xn1T_sb", [128, 2, NC, CH2], BF16)
                sfc_sb = sb8("sfc_sb", [128, 2, NC, 256], BF16)
                spj_sb = sb8("spj_sb", [128, 2, NH, 128], BF16)
                hfc_sb = sb8("hfc_sb", [128, NH, CH2], BF16)
                r_sb = sb8("r_sb", [128, 1, CH2], F32)
                y_sb = sb8("y_sb", [128, NC, CH2], BF16)
                outb_sb = sb8("outb_sb", [128, 1, C], F32)
                out2_sb = sb8("out2_sb", [128, 2, C], F32)
                st6c = sb8("st6c", [128, 24], F32)
                mv3 = sb8("mv3", [128, 2], F32)
                veps3 = sb8("veps3", [128, 2], F32)
                sd3 = sb8("sd3", [128, 2], F32)
                rs3 = sb8("rs3", [128, 2], F32)
                nmrs3 = sb8("nmrs3", [128, 2], F32)
                mv4 = sb8("mv4", [128, 2], F32)
                veps4 = sb8("veps4", [128, 1], F32)
                sd4 = sb8("sd4", [128, 1], F32)
                rstd4 = sb8("rstd4", [128, 1], F32)
                sceff = sb8("sceff", [128, 1], F32)
                bseff = sb8("bseff", [128, 1], F32)
                ps_fc = [sp8.enter_context(nc.psum_tensor(f"ps_fc{i}", [128, 512], F32))
                         for i in range(2)]
                ps_pj = [sp8.enter_context(nc.psum_tensor(f"ps_pj{i}", [128, 512], F32))
                         for i in range(2)]
                ps_yT = sp8.enter_context(nc.psum_tensor("ps_yT", [128, 2048], BF16))
                xn1_rows = xn1_d.rearrange("(n p) c -> n p c", p=128)
                x1_rows128 = x1_d.rearrange("(n p) c -> n p c", p=128)
                NGQ = NQ * TT2
                VS8_CHUNK = 3 * TT2 + 5 * TT2

                @block.gpsimd
                def _(gpsimd):
                    def x1b_seg(gpsimd, q):
                        for tt in range(TT2):
                            gq = q * TT2 + tt
                            _w(gpsimd, v_o8, gq - 1)
                            rinc(gpsimd.dma_start(out=x1b_sb[:, gq % 2, :],
                                           in_=x1_rows128[gq]), d_x1b, gq)

                    for q in range(NQ):
                        for tt in range(TT2):
                            gq = q * TT2 + tt
                            if gq == 0:
                                rwait_all(gpsimd, d_x1w7, NCH)
                            _w(gpsimd, s_nm8, gq - 1)
                            rinc(gpsimd.dma_start(out=x1a_sb[:, gq % 2, :],
                                           in_=x1_rows128[gq]), d_x1a, gq)
                        if q >= 1:
                            x1b_seg(gpsimd, q - 1)
                    x1b_seg(gpsimd, NQ - 1)

                @block.vector
                def _(vector):
                    vcnt = [0]

                    def ln3_seg(vector, q):
                        for tt in range(TT2):
                            gq = q * TT2 + tt
                            rwait(vector, d_x1a, gq)
                            _w(vector, s_sq8, gq - 1)
                            vcnt[0] = _bn(vector, st6c[:, :], mv3[:, :],
                                          x1a_sb[:, gq % 2, :], C, vs8, vcnt[0])
                            vector.wait_ge(vs8, vcnt[0])
                            vector.tensor_scalar(
                                out=veps3[:, gq % 2:gq % 2 + 1], in0=mv3[:, 1:2],
                                scalar1=EPS, scalar2=None,
                                op0=ALU.add).then_inc(v_st8, 1)
                            vector.wait_ge(s_sq8, gq + 1)
                            _w(vector, s_nm8, gq - 1)
                            vector.reciprocal(
                                out=rs3[:, gq % 2:gq % 2 + 1],
                                in_=sd3[:, gq % 2:gq % 2 + 1]).then_inc(vs8, 1)
                            vcnt[0] += 1
                            vector.wait_ge(vs8, vcnt[0])
                            vector.scalar_tensor_tensor(
                                out=nmrs3[:, gq % 2:gq % 2 + 1], in0=mv3[:, 0:1],
                                scalar=-1.0, in1=rs3[:, gq % 2:gq % 2 + 1],
                                op0=ALU.mult, op1=ALU.mult).then_inc(v_nm3, 1)

                    def ln4_seg(vector, q):
                        for tt in range(TT2):
                            gq = q * TT2 + tt
                            vector.wait_ge(p_yt, NC * (gq + 1))
                            vcnt[0] = _bn(vector, st6c[:, :], mv4[:, :],
                                          ps_yT[:, 0:C], C, vs8, vcnt[0])
                            vector.wait_ge(vs8, vcnt[0])
                            vector.tensor_scalar(
                                out=veps4[:, 0:1], in0=mv4[:, 1:2],
                                scalar1=amsq_pj[:, 0:1], scalar2=EPS,
                                op0=ALU.mult, op1=ALU.add).then_inc(vs8, 1)
                            vcnt[0] += 1
                            vector.wait_ge(s_sq8b, gq + 1)
                            vector.reciprocal(out=rstd4[:, 0:1],
                                              in_=sd4[:, 0:1]).then_inc(vs8, 1)
                            vcnt[0] += 1
                            vector.wait_ge(vs8, vcnt[0])
                            _w(vector, g_xp8, gq)
                            vector.tensor_scalar(
                                out=sceff[:, 0:1], in0=am_pj[:, 0:1],
                                scalar1=rstd4[:, 0:1], scalar2=None,
                                op0=ALU.mult).then_inc(vs8, 1)
                            vcnt[0] += 1
                            vector.wait_ge(vs8, vcnt[0])
                            vector.scalar_tensor_tensor(
                                out=bseff[:, 0:1], in0=mv4[:, 0:1], scalar=-1.0,
                                in1=sceff[:, 0:1], op0=ALU.mult,
                                op1=ALU.mult).then_inc(v_fold8, 1)
                            vector.wait_ge(g_xp8, gq + 1)
                            rwait(vector, d_x1b, gq)
                            rwait(vector, d_out8, gq - 2)
                            vector.scalar_tensor_tensor(
                                out=out2_sb[:, gq % 2, :], in0=outb_sb[:, 0, :],
                                scalar=1.0, in1=x1b_sb[:, gq % 2, :],
                                op0=ALU.mult, op1=ALU.add).then_inc(v_o8, 1)

                    ln3_seg(vector, 0)
                    for q in range(1, NQ):
                        ln3_seg(vector, q)
                        ln4_seg(vector, q - 1)
                    ln4_seg(vector, NQ - 1)

                @block.scalar
                def _(scalar):
                    LN3P = 3 * TT2
                    LN4P = 5 * TT2

                    def ln3s_seg(scalar, q):
                        for tt in range(TT2):
                            gq = q * TT2 + tt
                            scalar.wait_ge(v_st8, gq + 1)
                            scalar.sqrt(sd3[:, gq % 2:gq % 2 + 1],
                                        veps3[:, gq % 2:gq % 2 + 1]).then_inc(s_sq8, 1)
                            scalar.wait_ge(v_nm3, gq + 1)
                            rwait(scalar, d_xn1w, gq - 2)
                            scalar.activation(
                                out=xn1b_sb[:, gq % 2, :], in_=x1a_sb[:, gq % 2, :],
                                func=AF.Identity, bias=nmrs3[:, gq % 2:gq % 2 + 1],
                                scale=rs3[:, gq % 2:gq % 2 + 1]).then_inc(s_nm8, 1)
                            scalar.wait_ge(s_nm8, gq + 1)
                            rinc(scalar.dma_start(out=xn1_rows[gq],
                                             in_=xn1b_sb[:, gq % 2, :]), d_xn1w, gq)

                    def comp_seg(scalar, q):
                        for ht in range(NH):
                            gh = q * NH + ht
                            scalar.wait_ge(p_fc, gh + 1)
                            _w(scalar, s_sqq, gh)
                            scalar.activation(
                                out=r_sb[:, 0, :], in_=ps_fc[gh % 2][:, 0:CH2],
                                func=AF.Relu, scale=am_fc[:, 0:1]).then_inc(s_rev, 1)
                            scalar.wait_ge(s_rev, gh + 1)
                            scalar.activation(
                                out=hfc_sb[:, ht, :], in_=r_sb[:, 0, :],
                                func=AF.Square).then_inc(s_sqq, 1)
                        for ct in range(NC):
                            gc = q * NC + ct
                            scalar.wait_ge(p_pj, gc + 1)
                            if ct == 0:
                                _w(scalar, p_yt, NC * TT2 * q)
                            scalar.copy(out=y_sb[:, ct, :],
                                        in_=ps_pj[gc % 2][:, 0:CH2]).then_inc(s_yev, 1)
                        for tt in range(TT2):
                            gq = q * TT2 + tt
                            target = LN3P * min(q + 2, NQ) + LN4P * q + 5 * tt + 3
                            scalar.wait_ge(vs8, target)
                            scalar.sqrt(sd4[:, 0:1], veps4[:, 0:1]).then_inc(s_sq8b, 1)
                            scalar.wait_ge(v_fold8, gq + 1)
                            _w(scalar, v_o8, gq)
                            scalar.activation(
                                out=outb_sb[:, 0, :], in_=ps_yT[:, 0:C],
                                func=AF.Identity, bias=bseff[:, 0:1],
                                scale=sceff[:, 0:1]).then_inc(g_xp8, 1)

                    ln3s_seg(scalar, 0)
                    for q in range(1, NQ):
                        ln3s_seg(scalar, q)
                        comp_seg(scalar, q - 1)
                    comp_seg(scalar, NQ - 1)

                @block.sync
                def _(sync):
                    for q in range(NQ):
                        if q == 0:
                            rwait_all(sync, d_sw, NHB + NC)
                        rwait_all(sync, d_xn1w, TT2 * (q + 1))
                        _w(sync, p_fc, NH * (q - 1))
                        for ct in range(NC):
                            sync.dma_start(
                                out=xn1T_sb[:, q % 2, ct, :],
                                in_=xn1_d[q * CH2:(q + 1) * CH2, ct * 128:(ct + 1) * 128],
                                transpose=True).then_inc(d_xn1T[q % 2], 16)
                        if q >= 1:
                            for tt in range(TT2):
                                gq = (q - 1) * TT2 + tt
                                sync.wait_ge(v_o8, gq + 1)
                                rinc(sync.dma_start(out=out_rows[gq],
                                                 in_=out2_sb[:, gq % 2, :]), d_out8, gq)
                        for hb in range(NHB):
                            ghb = q * NHB + hb
                            _w(sync, p_fc, 2 * ghb - 2)
                            rinc(sync.dma_start(
                                out=sfc_sb[:, ghb % 2, :, :],
                                in_=sfcT_t[hb]), d_sfc, ghb)
                        for ct in range(NC):
                            gc = q * NC + ct
                            _w(sync, p_pj, gc - 1)
                            rinc(sync.dma_start(
                                out=spj_sb[:, gc % 2, :, :],
                                in_=spjT_t[ct]), d_spr, gc)
                    for tt in range(TT2):
                        gq = (NQ - 1) * TT2 + tt
                        sync.wait_ge(v_o8, gq + 1)
                        rinc(sync.dma_start(out=out_rows[gq],
                                         in_=out2_sb[:, gq % 2, :]), d_out8, gq)
                    rwait_all(sync, d_out8, NT128)

                @block.tensor
                def _(tensor):
                    for q in range(NQ):
                        tensor.wait_ge(d_xn1T[q % 2], 16 * NC * (q // 2 + 1))
                        for ht in range(NH):
                            gh = q * NH + ht
                            ghb = q * NHB + ht // 2
                            rwait(tensor, d_sfc, ghb)
                            _w(tensor, s_rev, gh - 1)
                            for ct in range(NC):
                                ins = tensor.matmul(
                                    ps_fc[gh % 2][:, 0:CH2],
                                    sfc_sb[:, ghb % 2, ct,
                                           (ht % 2) * 128:(ht % 2 + 1) * 128],
                                    xn1T_sb[:, q % 2, ct, :],
                                    start=(ct == 0), stop=(ct == NC - 1))
                            ins.then_inc(p_fc, 1)
                        tensor.wait_ge(s_sqq, NH * (q + 1))
                        for ct in range(NC):
                            gc = q * NC + ct
                            rwait(tensor, d_spr, gc)
                            _w(tensor, s_yev, gc - 1)
                            for ht in range(NH):
                                ins = tensor.matmul(
                                    ps_pj[gc % 2][:, 0:CH2],
                                    spj_sb[:, gc % 2, ht, :],
                                    hfc_sb[:, ht, :],
                                    start=(ht == 0), stop=(ht == NH - 1))
                            ins.then_inc(p_pj, 1)
                        for tt in range(TT2):
                            gq = q * TT2 + tt
                            _w(tensor, g_xp8, gq)
                            for ct in range(NC):
                                tensor.wait_ge(s_yev, q * NC + ct + 1)
                                tensor.transpose(
                                    ps_yT[:, ct * 128:(ct + 1) * 128],
                                    y_sb[:, ct, tt * 128:(tt + 1) * 128],
                                    idbf[:, :]).then_inc(p_yt, 1)

    return nc


def kernel(L0_attn_x, L0_attn_W, L0_attn_b, L0_ffn_w_fc, L0_ffn_w_proj):
    x = np.ascontiguousarray(np.asarray(L0_attn_x, dtype=np.float32))
    W = np.ascontiguousarray(np.asarray(L0_attn_W, dtype=np.float32))
    b = np.ascontiguousarray(np.asarray(L0_attn_b, dtype=np.float32))
    wfc = np.ascontiguousarray(np.asarray(L0_ffn_w_fc, dtype=np.float32))
    wpj = np.ascontiguousarray(np.asarray(L0_ffn_w_proj, dtype=np.float32))

    nc = build_program(FULL_CFG)
    ND = FULL_CFG["D"] // 128
    bT = np.ascontiguousarray(b.reshape(ND, 128).T)
    id32 = np.eye(128, dtype=np.float32)
    idbf = np.eye(128, dtype=ml_dtypes.bfloat16)
    in_maps = []
    for i in range(8):
        shard = np.ascontiguousarray(x[2 * i:2 * i + 2].reshape(4096, 2048))
        in_maps.append(dict(x=shard, W=W, bT=bT, wfc=wfc, wpj=wpj,
                            id32=id32, idbf=idbf))
    res = run_bass_kernel_spmd(nc, in_maps, core_ids=list(range(8)))
    outs = [np.asarray(r["out"]).reshape(2, 2048, 2048) for r in res.results]
    return np.concatenate(outs, axis=0)


# version 24
# speedup vs baseline: 1.2146x; 1.0122x over previous
import contextlib
import numpy as np
import ml_dtypes
import concourse.bass as bass
import concourse.mybir as mybir
from concourse.bass_utils import run_bass_kernel_spmd

F32 = mybir.dt.float32
BF16 = mybir.dt.bfloat16
AF = mybir.ActivationFunctionType
ALU = mybir.AluOpType
AX = mybir.AxisListType

FULL_CFG = dict(T=4096, C=2048, D=4096, H=5120, TB=2048, CH1=256, CH2=512)
EPS = 1e-5


def _w(eng, s, v):
    if v > 0:
        eng.wait_ge(s, v)


def rinc(instr, ring, i):
    instr.then_inc(ring[i % len(ring)], 16)


def rwait(eng, ring, i):
    if i >= 0:
        n = len(ring)
        eng.wait_ge(ring[i % n], 16 * (i // n + 1))


def rwait_all(eng, ring, n):
    r = len(ring)
    for s in range(r):
        c = (n - s + r - 1) // r
        if c > 0:
            eng.wait_ge(ring[s], 16 * c)


def _bn(vec, st6, dst_mv, src, width, self_sem, ctr):
    nb = max(1, width // 512)
    step = width // nb
    assert step * nb == width and step <= 512 and step % 2 == 0
    for j in range(nb):
        ins = vec.bn_stats(st6[:, 6 * j:6 * j + 6], src[:, j * step:(j + 1) * step])
    ins.then_inc(self_sem, 1)
    ctr += 1
    vec.wait_ge(self_sem, ctr)
    vec.bn_aggr(dst_mv, st6[:, 0:6 * nb]).then_inc(self_sem, 1)
    return ctr + 1


def build_program(cfg):
    T, C, D, H = cfg["T"], cfg["C"], cfg["D"], cfg["H"]
    TB, CH1, CH2 = cfg["TB"], cfg["CH1"], cfg["CH2"]

    NC = C // 128
    ND = D // 128
    NG = NC
    HALF = NG // 2
    NH = H // 128
    NHB = NH // 2
    NCH = T // CH1
    NQ = T // CH2
    TT1 = CH1 // 128
    TT2 = CH2 // 128
    RESET = TB // CH1
    GPC = 2 * HALF
    NT128 = T // 128

    nc = bass.Bass("TRN2", target_bir_lowering=False)

    x_d = nc.dram_tensor("x", [T, C], F32, kind="ExternalInput").ap()
    w_d = nc.dram_tensor("W", [D, C], F32, kind="ExternalInput").ap()
    bT_d = nc.dram_tensor("bT", [128, ND], F32, kind="ExternalInput").ap()
    wfc_d = nc.dram_tensor("wfc", [H, C], F32, kind="ExternalInput").ap()
    wpj_d = nc.dram_tensor("wpj", [C, H], F32, kind="ExternalInput").ap()
    id32_d = nc.dram_tensor("id32", [128, 128], F32, kind="ExternalInput").ap()
    idbf_d = nc.dram_tensor("idbf", [128, 128], BF16, kind="ExternalInput").ap()
    out_d = nc.dram_tensor("out", [T, C], F32, kind="ExternalOutput").ap()
    wbf_d = nc.dram_tensor("wbf_s", [D, C], BF16).ap()
    xn_d = nc.dram_tensor("xn_s", [T, C], BF16).ap()
    h_d = None
    x1_d = nc.dram_tensor("x1_s", [T, C], F32).ap()
    xn1_d = nc.dram_tensor("xn1_s", [T, C], BF16).ap()
    h_t = nc.dram_tensor("h_t", [T // CH1, 128, C // 128, CH1], F32).ap()
    sfcT_t = nc.dram_tensor("sfcT_t", [H // 256, 128, C // 128, 256], BF16).ap()
    spjT_t = nc.dram_tensor("spjT_t", [C // 128, 128, H // 128, 128], BF16).ap()

    x_ch1 = x_d.rearrange("(k tt p) c -> k p tt c", p=128, tt=TT1)
    xn_ch1 = xn_d.rearrange("(k tt p) c -> k p tt c", p=128, tt=TT1)
    x1_ch1 = x1_d.rearrange("(k tt p) c -> k p tt c", p=128, tt=TT1)
    x1_ch2 = x1_d.rearrange("(q tt p) c -> q p tt c", p=128, tt=TT2)
    xn1_ch2 = xn1_d.rearrange("(q tt p) c -> q p tt c", p=128, tt=TT2)
    out_rows = out_d.rearrange("(n p) c -> n p c", p=128)

    est = contextlib.ExitStack()
    sbuf = lambda n, s, d: est.enter_context(nc.sbuf_tensor(n, s, d))
    sem = lambda n: est.enter_context(nc.semaphore(n))

    with est:
        bT_sb = sbuf("bT_sb", [128, ND], F32)
        id32 = sbuf("id32_sb", [128, 128], F32)
        idbf = sbuf("idbf_sb", [128, 128], BF16)
        ones_row = sbuf("ones_row", [1, 128], F32)
        am_fc = sbuf("am_fc", [128, 1], F32)
        thr_fc = sbuf("thr_fc", [128, 1], F32)
        nthr_fc = sbuf("nthr_fc", [128, 1], F32)
        am_pj = sbuf("am_pj", [128, 1], F32)
        thr_pj = sbuf("thr_pj", [128, 1], F32)
        nthr_pj = sbuf("nthr_pj", [128, 1], F32)
        amsq_pj = sbuf("amsq_pj", [128, 1], F32)
        tot1 = sbuf("tot1", [1, 2], F32)
        tot2 = sbuf("tot2", [1, 2], F32)
        red1 = sbuf("red1", [128, 1], F32)
        carry = sbuf("carry", [128, NG], F32)

        ring = lambda n, r=2: [sem(n + str(i)) for i in range(r)]
        d_prep = sem("d_prep"); d_cast = sem("d_cast"); v_prep = sem("v_prep")
        d_wf = ring("d_wf"); v_abs = sem("v_abs"); fz = sem("fz")
        d_sf = ring("d_sf"); v_sg = sem("v_sg"); g_sg = sem("g_sg")
        p_tr2 = sem("p_tr2"); s_ev2 = sem("s_ev2"); d_sw = ring("d_sw")
        d_wt3 = sem("d_wt3"); d_wt5 = sem("d_wt5")
        d_x4 = ring("d_x4"); vs4 = sem("vs4"); s_sq4 = sem("s_sq4")
        v_lna = sem("v_lna"); v_lnb = sem("v_lnb"); s_nm4 = sem("s_nm4")
        d_xnw4 = ring("d_xnw4")
        d_xnT4 = ring("d_xnT4", 3); p_mm4 = sem("p_mm4"); s_ev4 = sem("s_ev4")
        v_sc4 = sem("v_sc4"); v_cr4 = sem("v_cr4"); d_hw4 = ring("d_hw4")
        d_xnT6 = ring("d_xnT6"); p_mm6 = sem("p_mm6"); s_ev6 = sem("s_ev6")
        v_sc6 = sem("v_sc6"); v_cr6 = sem("v_cr6"); d_hw6 = ring("d_hw6")
        d_hr7 = ring("d_hr7"); d_xr7 = ring("d_xr7"); p_tr7 = sem("p_tr7")
        vs7 = sem("vs7"); s_sq7 = sem("s_sq7"); v_nm7 = sem("v_nm7")
        g_xp7 = sem("g_xp7"); v_out7 = sem("v_out7"); d_x1w7 = ring("d_x1w7")
        d_x1a = ring("d_x1a"); d_x1b = ring("d_x1b")
        vs8 = sem("vs8"); s_sq8 = sem("s_sq8")
        v_st8 = sem("v_st8"); v_nm3 = sem("v_nm3"); s_nm8 = sem("s_nm8")
        d_xn1w = ring("d_xn1w"); d_xn1T = ring("d_xn1T")
        d_sfc = ring("d_sfc", 2); d_spr = ring("d_spr")
        p_fc = sem("p_fc"); s_rev = sem("s_rev"); s_sqq = sem("s_sqq")
        p_pj = sem("p_pj"); s_yev = sem("s_yev"); p_yt = sem("p_yt")
        s_sq8b = sem("s_sq8b"); v_fold8 = sem("v_fold8")
        g_xp8 = sem("g_xp8"); v_o8 = sem("v_o8"); d_out8 = ring("d_out8", 2)

        with nc.Block() as block:
            @block.sync
            def _(sync):
                sync.dma_start(out=bT_sb[:, :], in_=bT_d[:, :]).then_inc(d_prep, 16)
                sync.dma_start(out=id32[:, :], in_=id32_d[:, :]).then_inc(d_prep, 16)
                sync.dma_start(out=idbf[:, :], in_=idbf_d[:, :]).then_inc(d_prep, 16)

            @block.gpsimd
            def _(gpsimd):
                q4 = D // 4
                for i in range(4):
                    gpsimd.dma_start(out=wbf_d[i * q4:(i + 1) * q4, :],
                                     in_=w_d[i * q4:(i + 1) * q4, :]).then_inc(d_cast, 16)

            @block.vector
            def _(vector):
                vector.memset(ones_row[:, :], 1.0).then_inc(v_prep, 1)

            with contextlib.ExitStack() as s12:
                sb12 = lambda n, s, d: s12.enter_context(nc.sbuf_tensor(n, s, d))
                ps12 = lambda n, s, d: s12.enter_context(nc.psum_tensor(n, s, d))
                MW = max(C, H)
                wslab = sb12("wslab", [128, 2, MW], F32)
                acc_fc = sb12("acc_fc", [128, NH], F32)
                acc_pj = sb12("acc_pj", [128, NC], F32)
                nbs = sb12("nbs", [128, 2, MW], BF16)
                ssl = sb12("ssl", [128, 2, MW], BF16)
                stage_fc = sb12("stage_fc", [128, 2, NC, 256], BF16)
                stage_pj = sb12("stage_pj", [128, 2, NH, 128], BF16)
                ps_misc = ps12("ps_misc", [128, 512], F32)
                ps_bf = [ps12("ps_bf0", [128, 1024], BF16),
                         ps12("ps_bf1", [128, 1024], BF16)]

                @block.sync
                def _(sync):
                    for i in range(NH):
                        _w(sync, v_abs, i - 1)
                        rinc(sync.dma_start(out=wslab[:, i % 2, 0:C],
                                            in_=wfc_d[i * 128:(i + 1) * 128, :]), d_wf, i)
                    for j in range(NC):
                        _w(sync, v_abs, NH + j - 1)
                        rinc(sync.dma_start(out=wslab[:, j % 2, 0:H],
                                            in_=wpj_d[j * 128:(j + 1) * 128, :]), d_wf, NH + j)

                @block.vector
                def _(vector):
                    for i in range(NH):
                        rwait(vector, d_wf, i)
                        vector.tensor_reduce(out=acc_fc[:, i:i + 1], in_=wslab[:, i % 2, 0:C],
                                             axis=AX.X, op=ALU.add,
                                             apply_absolute_value=True).then_inc(v_abs, 1)
                    for j in range(NC):
                        rwait(vector, d_wf, NH + j)
                        vector.tensor_reduce(out=acc_pj[:, j:j + 1], in_=wslab[:, j % 2, 0:H],
                                             axis=AX.X, op=ALU.add,
                                             apply_absolute_value=True).then_inc(v_abs, 1)

                fzc = [0]

                def vst(vec, f):
                    _w(vec, fz, fzc[0])
                    f().then_inc(fz, 1)
                    fzc[0] += 1

                def tst(tensor, f):
                    _w(tensor, fz, fzc[0])
                    f().then_inc(fz, 1)
                    fzc[0] += 1

                @block.vector
                def _(vector):
                    vector.wait_ge(v_abs, NH + NC)
                    vst(vector, lambda: vector.tensor_reduce(
                        out=red1[:, 0:1], in_=acc_fc[:, 0:NH], axis=AX.X, op=ALU.add))

                @block.tensor
                def _(tensor):
                    tensor.wait_ge(d_prep, 48)
                    tensor.wait_ge(v_prep, 1)
                    tst(tensor, lambda: tensor.transpose(
                        ps_misc[0:1, 0:128], red1[:, 0:1], id32[:, :]))

                @block.vector
                def _(vector):
                    vst(vector, lambda: vector.tensor_reduce(
                        out=tot1[0:1, 0:1], in_=ps_misc[0:1, 0:128], axis=AX.X, op=ALU.add))
                    vst(vector, lambda: vector.tensor_scalar(
                        out=tot2[0:1, 0:1], in0=tot1[0:1, 0:1],
                        scalar1=1.0 / (H * C), scalar2=EPS, op0=ALU.mult, op1=ALU.max))

                @block.tensor
                def _(tensor):
                    tst(tensor, lambda: tensor.matmul(
                        ps_misc[0:128, 256:257], ones_row[0:1, 0:128],
                        tot2[0:1, 0:1], start=True, stop=True))

                @block.vector
                def _(vector):
                    vst(vector, lambda: vector.tensor_copy(
                        out=am_fc[:, 0:1], in_=ps_misc[0:128, 256:257]))
                    vst(vector, lambda: vector.tensor_scalar(
                        out=thr_fc[:, 0:1], in0=am_fc[:, 0:1], scalar1=0.7,
                        scalar2=None, op0=ALU.mult))
                    vst(vector, lambda: vector.tensor_scalar(
                        out=nthr_fc[:, 0:1], in0=thr_fc[:, 0:1], scalar1=-1.0,
                        scalar2=None, op0=ALU.mult))
                    vst(vector, lambda: vector.tensor_reduce(
                        out=red1[:, 0:1], in_=acc_pj[:, 0:NC], axis=AX.X, op=ALU.add))

                @block.tensor
                def _(tensor):
                    tst(tensor, lambda: tensor.transpose(
                        ps_misc[0:1, 0:128], red1[:, 0:1], id32[:, :]))

                @block.vector
                def _(vector):
                    vst(vector, lambda: vector.tensor_reduce(
                        out=tot1[0:1, 1:2], in_=ps_misc[0:1, 0:128], axis=AX.X, op=ALU.add))
                    vst(vector, lambda: vector.tensor_scalar(
                        out=tot2[0:1, 1:2], in0=tot1[0:1, 1:2],
                        scalar1=1.0 / (H * C), scalar2=EPS, op0=ALU.mult, op1=ALU.max))

                @block.tensor
                def _(tensor):
                    tst(tensor, lambda: tensor.matmul(
                        ps_misc[0:128, 256:257], ones_row[0:1, 0:128],
                        tot2[0:1, 1:2], start=True, stop=True))

                @block.vector
                def _(vector):
                    vst(vector, lambda: vector.tensor_copy(
                        out=am_pj[:, 0:1], in_=ps_misc[0:128, 256:257]))
                    vst(vector, lambda: vector.tensor_scalar(
                        out=thr_pj[:, 0:1], in0=am_pj[:, 0:1], scalar1=0.7,
                        scalar2=None, op0=ALU.mult))
                    vst(vector, lambda: vector.tensor_scalar(
                        out=nthr_pj[:, 0:1], in0=thr_pj[:, 0:1], scalar1=-1.0,
                        scalar2=None, op0=ALU.mult))
                    vst(vector, lambda: vector.tensor_scalar(
                        out=amsq_pj[:, 0:1], in0=am_pj[:, 0:1], scalar1=am_pj[:, 0:1],
                        scalar2=None, op0=ALU.mult))

                FZ_END = fzc[0]

                @block.sync
                def _(sync):
                    for i in range(NH):
                        if i < 2:
                            sync.wait_ge(v_abs, NH + NC)
                        else:
                            sync.wait_ge(v_sg, 2 * (i - 2) + 2)
                        rinc(sync.dma_start(out=wslab[:, i % 2, 0:C],
                                            in_=wfc_d[i * 128:(i + 1) * 128, :]), d_sf, i)
                    for j in range(NC):
                        if j < 2:
                            sync.wait_ge(v_sg, 2 * (NH - 2 + j) + 2)
                        else:
                            sync.wait_ge(g_sg, 2 * (j - 2) + 2)
                        rinc(sync.dma_start(out=wslab[:, j % 2, 0:H],
                                            in_=wpj_d[j * 128:(j + 1) * 128, :]), d_sf, NH + j)

                @block.vector
                def _(vector):
                    vector.wait_ge(fz, FZ_END)
                    for i in range(NH):
                        rwait(vector, d_sf, i)
                        _w(vector, p_tr2, NC * (i - 1))
                        vector.tensor_scalar(
                            out=nbs[:, i % 2, 0:C], in0=wslab[:, i % 2, 0:C],
                            scalar1=nthr_fc[:, 0:1], scalar2=-1.0,
                            op0=ALU.is_le, op1=ALU.mult).then_inc(v_sg, 1)
                        vector.wait_ge(v_sg, 2 * i + 1)
                        vector.scalar_tensor_tensor(
                            out=ssl[:, i % 2, 0:C], in0=wslab[:, i % 2, 0:C],
                            scalar=thr_fc[:, 0:1], in1=nbs[:, i % 2, 0:C],
                            op0=ALU.is_ge, op1=ALU.add).then_inc(v_sg, 1)
                    for j in range(NC):
                        rwait(vector, d_sf, NH + j)
                        if j < 2:
                            vector.wait_ge(p_tr2, NC * (NH - 1 + j))
                        else:
                            vector.wait_ge(p_tr2, NC * NH + NH * (j - 1))
                        vector.tensor_scalar(
                            out=nbs[:, j % 2, 0:H], in0=wslab[:, j % 2, 0:H],
                            scalar1=nthr_pj[:, 0:1], scalar2=-1.0,
                            op0=ALU.is_le, op1=ALU.mult).then_inc(g_sg, 1)
                        vector.wait_ge(g_sg, 2 * j + 1)
                        vector.scalar_tensor_tensor(
                            out=ssl[:, j % 2, 0:H], in0=wslab[:, j % 2, 0:H],
                            scalar=thr_pj[:, 0:1], in1=nbs[:, j % 2, 0:H],
                            op0=ALU.is_ge, op1=ALU.add).then_inc(g_sg, 1)


                @block.tensor
                def _(tensor):
                    tcnt = 0
                    for i in range(NH):
                        tensor.wait_ge(v_sg, 2 * i + 2)
                        for ct in range(NC):
                            _w(tensor, s_ev2, tcnt - 1)
                            tensor.transpose(ps_bf[tcnt % 2][0:128, 0:128],
                                             ssl[:, i % 2, ct * 128:(ct + 1) * 128],
                                             idbf[:, :]).then_inc(p_tr2, 1)
                            tcnt += 1
                    for j in range(NC):
                        tensor.wait_ge(g_sg, 2 * j + 2)
                        for ht in range(NH):
                            _w(tensor, s_ev2, tcnt - 1)
                            tensor.transpose(ps_bf[tcnt % 2][0:128, 0:128],
                                             ssl[:, j % 2, ht * 128:(ht + 1) * 128],
                                             idbf[:, :]).then_inc(p_tr2, 1)
                            tcnt += 1

                @block.scalar
                def _(scalar):
                    tcnt = 0
                    for i in range(NH):
                        hb, hoff = i // 2, (i % 2) * 128
                        if i % 2 == 0:
                            rwait(scalar, d_sw, hb - 2)
                        for ct in range(NC):
                            scalar.wait_ge(p_tr2, tcnt + 1)
                            scalar.copy(out=stage_fc[:, hb % 2, ct, hoff:hoff + 128],
                                        in_=ps_bf[tcnt % 2][0:128, 0:128]).then_inc(s_ev2, 1)
                            tcnt += 1
                    for j in range(NC):
                        rwait(scalar, d_sw, NHB + j - 2)
                        for ht in range(NH):
                            scalar.wait_ge(p_tr2, tcnt + 1)
                            scalar.copy(out=stage_pj[:, j % 2, ht, :],
                                        in_=ps_bf[tcnt % 2][0:128, 0:128]).then_inc(s_ev2, 1)
                            tcnt += 1

                @block.gpsimd
                def _(gpsimd):
                    for hb in range(NHB):
                        gpsimd.wait_ge(s_ev2, NC * (2 * hb + 2))
                        rinc(gpsimd.dma_start(out=sfcT_t[hb],
                                         in_=stage_fc[:, hb % 2, :, :]), d_sw, hb)
                    for j in range(NC):
                        gpsimd.wait_ge(s_ev2, NC * NH + NH * (j + 1))
                        rinc(gpsimd.dma_start(out=spjT_t[j],
                                         in_=stage_pj[:, j % 2, :, :]), d_sw, NHB + j)

            S2_DONE = 16 * (NH + NC)

            with contextlib.ExitStack() as sp1:
                sb1 = lambda n, s, d: sp1.enter_context(nc.sbuf_tensor(n, s, d))
                wt_sb = sb1("wt_sb", [128, NC, 2 * NG * 128], BF16)
                x_sb = sb1("x_sb", [128, 1, TT1, C], F32)
                xnb_sb = sb1("xnb_sb", [128, 1, TT1, C], BF16)
                xnT_sb = sb1("xnT_sb", [128, 3, NC, CH1], BF16)
                g_sb = sb1("g_sb", [128, 2, CH1], F32)
                v_sb = sb1("v_sb", [128, 2, CH1], F32)
                h_sb = sb1("h_sb", [128, 1, NG, CH1], F32)
                st6a = sb1("st6a", [128, TT1, 24], F32)
                mv1 = sb1("mv1", [128, TT1, 2], F32)
                veps1 = sb1("veps1", [128, TT1], F32)
                sd1 = sb1("sd1", [128, TT1], F32)
                rs1 = sb1("rs1", [128, TT1], F32)
                nmrs1 = sb1("nmrs1", [128, TT1], F32)
                ps_par = [sp1.enter_context(nc.psum_tensor(f"ps_par{i}", [128, 512], F32))
                          for i in range(4)]
                GP1 = 2 * NG

                @block.sync
                def _(sync):
                    sync.wait_ge(d_cast, 64)
                    rwait_all(sync, d_sw, NHB + NC)
                    for ct in range(NC):
                        sync.dma_start(out=wt_sb[:, ct, 0:NG * 128],
                                       in_=wbf_d[0:C, ct * 128:(ct + 1) * 128],
                                       transpose=True).then_inc(d_wt3, 16)
                        sync.dma_start(out=wt_sb[:, ct, NG * 128:2 * NG * 128],
                                       in_=wbf_d[C:D, ct * 128:(ct + 1) * 128],
                                       transpose=True).then_inc(d_wt3, 16)

                @block.sync
                def _(sync):
                    for k in range(NCH):
                        _w(sync, s_nm4, TT1 * k)
                        rinc(sync.dma_start(out=x_sb[:, 0, :, :],
                                       in_=x_ch1[k]), d_x4, k)
                        rwait(sync, d_xnw4, k)
                        _w(sync, p_mm4, GP1 * (k - 2))
                        for ct in range(NC):
                            sync.dma_start(
                                out=xnT_sb[:, k % 3, ct, :],
                                in_=xn_d[k * CH1:(k + 1) * CH1, ct * 128:(ct + 1) * 128],
                                transpose=True).then_inc(d_xnT4[k % 3], 16)

                @block.vector
                def _(vector):
                    vcnt = 0

                    def stats_seg(vector, k, vcnt):
                        rwait(vector, d_x4, k)
                        for tt in range(TT1):
                            vcnt = _bn(vector, st6a[:, tt, :], mv1[:, tt, :],
                                       x_sb[:, 0, tt, :], C, vs4, vcnt)
                        vector.wait_ge(vs4, vcnt)
                        _w(vector, s_sq4, k)
                        for tt in range(TT1):
                            ins = vector.tensor_scalar(
                                out=veps1[:, tt:tt + 1], in0=mv1[:, tt, 1:2],
                                scalar1=EPS, scalar2=None, op0=ALU.add)
                        ins.then_inc(v_lna, 1)
                        vector.wait_ge(s_sq4, k + 1)
                        _w(vector, s_nm4, TT1 * k)
                        vector.reciprocal(out=rs1[:, 0:TT1],
                                          in_=sd1[:, 0:TT1]).then_inc(vs4, 1)
                        vcnt += 1
                        vector.wait_ge(vs4, vcnt)
                        for tt in range(TT1):
                            ins = vector.scalar_tensor_tensor(
                                out=nmrs1[:, tt:tt + 1], in0=mv1[:, tt, 0:1],
                                scalar=-1.0, in1=rs1[:, tt:tt + 1],
                                op0=ALU.mult, op1=ALU.mult)
                        ins.then_inc(v_lnb, 1)
                        return vcnt

                    def scan_seg(vector, k):
                        for pair in range(NG):
                            gp = k * NG + pair
                            vector.wait_ge(s_ev4, k * GP1 + 2 * pair + 2)
                            rwait(vector, d_hw4, k - 1)
                            if k % RESET == 0:
                                init = 0.0
                            else:
                                vector.wait_ge(v_cr4, (k - 1) * NG + pair + 1)
                                init = carry[:, pair:pair + 1]
                            vector.tensor_tensor_scan(
                                out=h_sb[:, 0, pair, :],
                                data0=g_sb[:, gp % 2, :], data1=v_sb[:, gp % 2, :],
                                initial=init, op0=ALU.mult,
                                op1=ALU.add).then_inc(v_sc4, 1)
                            vector.wait_ge(v_sc4, gp + 1)
                            vector.tensor_copy(
                                out=carry[:, pair:pair + 1],
                                in_=h_sb[:, 0, pair, CH1 - 1:CH1]).then_inc(v_cr4, 1)

                    vcnt = stats_seg(vector, 0, vcnt)
                    for k in range(1, NCH):
                        vcnt = stats_seg(vector, k, vcnt)
                        scan_seg(vector, k - 1)
                    scan_seg(vector, NCH - 1)

                @block.scalar
                def _(scalar):
                    scalar.wait_ge(d_prep, 48)

                    def ln1_seg(scalar, k):
                        scalar.wait_ge(v_lna, k + 1)
                        scalar.sqrt(sd1[:, 0:TT1],
                                    veps1[:, 0:TT1]).then_inc(s_sq4, 1)
                        scalar.wait_ge(v_lnb, k + 1)
                        rwait(scalar, d_xnw4, k - 1)
                        for tt in range(TT1):
                            scalar.activation(
                                out=xnb_sb[:, 0, tt, :],
                                in_=x_sb[:, 0, tt, :], func=AF.Identity,
                                bias=nmrs1[:, tt:tt + 1],
                                scale=rs1[:, tt:tt + 1]).then_inc(s_nm4, 1)
                        scalar.wait_ge(s_nm4, TT1 * (k + 1))
                        rinc(scalar.dma_start(out=xn_ch1[k],
                                         in_=xnb_sb[:, 0, :, :]), d_xnw4, k)

                    def evac_seg(scalar, k):
                        for pair in range(NG):
                            for role in range(2):
                                gi = k * GP1 + 2 * pair + role
                                gp = k * NG + pair
                                scalar.wait_ge(p_mm4, gi + 1)
                                _w(scalar, v_sc4, gp - 1)
                                if role == 0:
                                    scalar.activation(
                                        out=g_sb[:, gp % 2, :],
                                        in_=ps_par[gi % 4][:, 0:CH1],
                                        func=AF.Sigmoid,
                                        bias=bT_sb[:, pair:pair + 1]).then_inc(s_ev4, 1)
                                else:
                                    scalar.activation(
                                        out=v_sb[:, gp % 2, :],
                                        in_=ps_par[gi % 4][:, 0:CH1],
                                        func=AF.Tanh,
                                        bias=bT_sb[:, NG + pair:NG + pair + 1]).then_inc(s_ev4, 1)

                    ln1_seg(scalar, 0)
                    for k in range(1, NCH):
                        ln1_seg(scalar, k)
                        evac_seg(scalar, k - 1)
                    evac_seg(scalar, NCH - 1)

                @block.tensor
                def _(tensor):
                    tensor.wait_ge(d_wt3, 32 * NC)
                    for k in range(NCH):
                        tensor.wait_ge(d_xnT4[k % 3], 16 * NC * (k // 3 + 1))
                        for pair in range(NG):
                            for role in range(2):
                                gi = k * GP1 + 2 * pair + role
                                _w(tensor, s_ev4, gi - 3)
                                col0 = (role * NG + pair) * 128
                                pp = ps_par[gi % 4]
                                for ct in range(NC):
                                    ins = tensor.matmul(
                                        pp[:, 0:CH1],
                                        wt_sb[:, ct, col0:col0 + 128],
                                        xnT_sb[:, k % 3, ct, :],
                                        start=(ct == 0), stop=(ct == NC - 1))
                                ins.then_inc(p_mm4, 1)

                @block.gpsimd
                def _(gpsimd):
                    for k in range(NCH):
                        gpsimd.wait_ge(v_sc4, NG * (k + 1))
                        rinc(gpsimd.dma_start(
                            out=h_t[k][:, 0:NG, :],
                            in_=h_sb[:, 0, :, :]), d_hw4, k)

            with contextlib.ExitStack() as sp7:
                sb7 = lambda n, s, d: sp7.enter_context(nc.sbuf_tensor(n, s, d))
                htin_sb = sb7("htin_sb", [128, 2, NG, CH1], F32)
                x2_sb = sb7("x2_sb", [128, 2, TT1, C], F32)
                xp_sb = sb7("xp_sb", [128, 2, C], F32)
                x1o_sb = sb7("x1o_sb", [128, 2, TT1, C], F32)
                st6b = sb7("st6b", [128, 24], F32)
                mv2 = sb7("mv2", [128, 2], F32)
                veps2 = sb7("veps2", [128, 1], F32)
                sd2 = sb7("sd2", [128, 1], F32)
                rs2 = sb7("rs2", [128, 1], F32)
                nmrs2 = sb7("nmrs2", [128, 1], F32)
                ps_hT = [sp7.enter_context(nc.psum_tensor(f"ps_hT{i}", [128, 2048], F32))
                         for i in range(2)]

                @block.sync
                def _(sync):
                    for k in range(NCH):
                        if k == 0:
                            rwait_all(sync, d_hw4, NCH)
                        _w(sync, p_tr7, NG * TT1 * (k - 1))
                        rinc(sync.dma_start(out=htin_sb[:, k % 2, :, :],
                                       in_=h_t[k]), d_hr7, k)

                @block.scalar
                def _(scalar):
                    for k in range(NCH):
                        if k == 0:
                            rwait_all(scalar, d_hw4, NCH)
                        _w(scalar, g_xp7, TT1 * (k - 1))
                        rinc(scalar.dma_start(out=x2_sb[:, k % 2, :, :],
                                         in_=x_ch1[k]), d_xr7, k)
                        for tt in range(TT1):
                            gt = k * TT1 + tt
                            scalar.wait_ge(vs7, 4 * gt + 3)
                            scalar.sqrt(sd2[:, 0:1], veps2[:, 0:1]).then_inc(s_sq7, 1)

                @block.tensor
                def _(tensor):
                    for k in range(NCH):
                        rwait(tensor, d_hr7, k)
                        for tt in range(TT1):
                            gt = k * TT1 + tt
                            _w(tensor, v_out7, gt - 1)
                            for ct in range(NG):
                                tensor.transpose(
                                    ps_hT[gt % 2][:, ct * 128:(ct + 1) * 128],
                                    htin_sb[:, k % 2, ct, tt * 128:(tt + 1) * 128],
                                    id32[:, :]).then_inc(p_tr7, 1)

                @block.vector
                def _(vector):
                    vcnt = 0
                    for k in range(NCH):
                        rwait(vector, d_x1w7, k - 2)
                        for tt in range(TT1):
                            gt = k * TT1 + tt
                            vector.wait_ge(p_tr7, NG * (gt + 1))
                            vcnt = _bn(vector, st6b[:, :], mv2[:, :],
                                       ps_hT[gt % 2][:, 0:C], C, vs7, vcnt)
                            vector.wait_ge(vs7, vcnt)
                            vector.tensor_scalar(
                                out=veps2[:, 0:1], in0=mv2[:, 1:2], scalar1=EPS,
                                scalar2=None, op0=ALU.add).then_inc(vs7, 1)
                            vcnt += 1
                            vector.wait_ge(s_sq7, gt + 1)
                            vector.reciprocal(out=rs2[:, 0:1],
                                              in_=sd2[:, 0:1]).then_inc(vs7, 1)
                            vcnt += 1
                            vector.wait_ge(vs7, vcnt)
                            vector.scalar_tensor_tensor(
                                out=nmrs2[:, 0:1], in0=mv2[:, 0:1], scalar=-1.0,
                                in1=rs2[:, 0:1], op0=ALU.mult,
                                op1=ALU.mult).then_inc(v_nm7, 1)
                            rwait(vector, d_xr7, k)
                            vector.wait_ge(v_nm7, gt + 1)
                            vector.tensor_scalar(
                                out=xp_sb[:, gt % 2, :], in0=x2_sb[:, k % 2, tt, :],
                                scalar1=nmrs2[:, 0:1], scalar2=None,
                                op0=ALU.add).then_inc(g_xp7, 1)
                            vector.wait_ge(g_xp7, gt + 1)
                            vector.scalar_tensor_tensor(
                                out=x1o_sb[:, k % 2, tt, :], in0=ps_hT[gt % 2][:, 0:C],
                                scalar=rs2[:, 0:1], in1=xp_sb[:, gt % 2, :],
                                op0=ALU.mult, op1=ALU.add).then_inc(v_out7, 1)

                @block.gpsimd
                def _(gpsimd):
                    for k in range(NCH):
                        gpsimd.wait_ge(v_out7, TT1 * (k + 1))
                        rinc(gpsimd.dma_start(out=x1_ch1[k],
                                         in_=x1o_sb[:, k % 2, :, :]), d_x1w7, k)

            with contextlib.ExitStack() as sp8:
                sb8 = lambda n, s, d: sp8.enter_context(nc.sbuf_tensor(n, s, d))
                x1a_sb = sb8("x1a_sb", [128, 2, C], F32)
                x1b_sb = sb8("x1b_sb", [128, 2, C], F32)
                xn1b_sb = sb8("xn1b_sb", [128, 2, C], BF16)
                xn1T_sb = sb8("xn1T_sb", [128, 2, NC, CH2], BF16)
                sfc_sb = sb8("sfc_sb", [128, 2, NC, 256], BF16)
                spj_sb = sb8("spj_sb", [128, 2, NH, 128], BF16)
                hfc_sb = sb8("hfc_sb", [128, NH, CH2], BF16)
                r_sb = sb8("r_sb", [128, 1, CH2], F32)
                y_sb = sb8("y_sb", [128, NC, CH2], BF16)
                outb_sb = sb8("outb_sb", [128, 1, C], F32)
                out2_sb = sb8("out2_sb", [128, 2, C], F32)
                st6c = sb8("st6c", [128, 24], F32)
                mv3 = sb8("mv3", [128, 2], F32)
                veps3 = sb8("veps3", [128, 2], F32)
                sd3 = sb8("sd3", [128, 2], F32)
                rs3 = sb8("rs3", [128, 2], F32)
                nmrs3 = sb8("nmrs3", [128, 2], F32)
                mv4 = sb8("mv4", [128, 2], F32)
                veps4 = sb8("veps4", [128, 1], F32)
                sd4 = sb8("sd4", [128, 1], F32)
                rstd4 = sb8("rstd4", [128, 1], F32)
                sceff = sb8("sceff", [128, 1], F32)
                bseff = sb8("bseff", [128, 1], F32)
                ps_fc = [sp8.enter_context(nc.psum_tensor(f"ps_fc{i}", [128, 512], F32))
                         for i in range(2)]
                ps_pj = [sp8.enter_context(nc.psum_tensor(f"ps_pj{i}", [128, 512], F32))
                         for i in range(2)]
                ps_yT = sp8.enter_context(nc.psum_tensor("ps_yT", [128, 2048], BF16))
                xn1_rows = xn1_d.rearrange("(n p) c -> n p c", p=128)
                x1_rows128 = x1_d.rearrange("(n p) c -> n p c", p=128)
                NGQ = NQ * TT2
                VS8_CHUNK = 3 * TT2 + 5 * TT2

                @block.gpsimd
                def _(gpsimd):
                    def x1b_seg(gpsimd, q):
                        for tt in range(TT2):
                            gq = q * TT2 + tt
                            _w(gpsimd, v_o8, gq - 1)
                            rinc(gpsimd.dma_start(out=x1b_sb[:, gq % 2, :],
                                           in_=x1_rows128[gq]), d_x1b, gq)

                    for q in range(NQ):
                        for tt in range(TT2):
                            gq = q * TT2 + tt
                            if gq == 0:
                                rwait_all(gpsimd, d_x1w7, NCH)
                            _w(gpsimd, s_nm8, gq - 1)
                            rinc(gpsimd.dma_start(out=x1a_sb[:, gq % 2, :],
                                           in_=x1_rows128[gq]), d_x1a, gq)
                        if q >= 1:
                            x1b_seg(gpsimd, q - 1)
                    x1b_seg(gpsimd, NQ - 1)

                @block.vector
                def _(vector):
                    vcnt = [0]

                    def ln3_seg(vector, q):
                        for tt in range(TT2):
                            gq = q * TT2 + tt
                            rwait(vector, d_x1a, gq)
                            _w(vector, s_sq8, gq - 1)
                            vcnt[0] = _bn(vector, st6c[:, :], mv3[:, :],
                                          x1a_sb[:, gq % 2, :], C, vs8, vcnt[0])
                            vector.wait_ge(vs8, vcnt[0])
                            vector.tensor_scalar(
                                out=veps3[:, gq % 2:gq % 2 + 1], in0=mv3[:, 1:2],
                                scalar1=EPS, scalar2=None,
                                op0=ALU.add).then_inc(v_st8, 1)
                            vector.wait_ge(s_sq8, gq + 1)
                            _w(vector, s_nm8, gq - 1)
                            vector.reciprocal(
                                out=rs3[:, gq % 2:gq % 2 + 1],
                                in_=sd3[:, gq % 2:gq % 2 + 1]).then_inc(vs8, 1)
                            vcnt[0] += 1
                            vector.wait_ge(vs8, vcnt[0])
                            vector.scalar_tensor_tensor(
                                out=nmrs3[:, gq % 2:gq % 2 + 1], in0=mv3[:, 0:1],
                                scalar=-1.0, in1=rs3[:, gq % 2:gq % 2 + 1],
                                op0=ALU.mult, op1=ALU.mult).then_inc(v_nm3, 1)

                    def ln4_seg(vector, q):
                        for tt in range(TT2):
                            gq = q * TT2 + tt
                            vector.wait_ge(p_yt, NC * (gq + 1))
                            vcnt[0] = _bn(vector, st6c[:, :], mv4[:, :],
                                          ps_yT[:, 0:C], C, vs8, vcnt[0])
                            vector.wait_ge(vs8, vcnt[0])
                            vector.tensor_scalar(
                                out=veps4[:, 0:1], in0=mv4[:, 1:2],
                                scalar1=amsq_pj[:, 0:1], scalar2=EPS,
                                op0=ALU.mult, op1=ALU.add).then_inc(vs8, 1)
                            vcnt[0] += 1
                            vector.wait_ge(s_sq8b, gq + 1)
                            vector.reciprocal(out=rstd4[:, 0:1],
                                              in_=sd4[:, 0:1]).then_inc(vs8, 1)
                            vcnt[0] += 1
                            vector.wait_ge(vs8, vcnt[0])
                            _w(vector, g_xp8, gq)
                            vector.tensor_scalar(
                                out=sceff[:, 0:1], in0=am_pj[:, 0:1],
                                scalar1=rstd4[:, 0:1], scalar2=None,
                                op0=ALU.mult).then_inc(vs8, 1)
                            vcnt[0] += 1
                            vector.wait_ge(vs8, vcnt[0])
                            vector.scalar_tensor_tensor(
                                out=bseff[:, 0:1], in0=mv4[:, 0:1], scalar=-1.0,
                                in1=sceff[:, 0:1], op0=ALU.mult,
                                op1=ALU.mult).then_inc(v_fold8, 1)
                            vector.wait_ge(g_xp8, gq + 1)
                            rwait(vector, d_x1b, gq)
                            rwait(vector, d_out8, gq - 2)
                            vector.scalar_tensor_tensor(
                                out=out2_sb[:, gq % 2, :], in0=outb_sb[:, 0, :],
                                scalar=1.0, in1=x1b_sb[:, gq % 2, :],
                                op0=ALU.mult, op1=ALU.add).then_inc(v_o8, 1)

                    ln3_seg(vector, 0)
                    for q in range(1, NQ):
                        ln3_seg(vector, q)
                        ln4_seg(vector, q - 1)
                    ln4_seg(vector, NQ - 1)

                @block.scalar
                def _(scalar):
                    LN3P = 3 * TT2
                    LN4P = 5 * TT2

                    def ln3s_seg(scalar, q):
                        for tt in range(TT2):
                            gq = q * TT2 + tt
                            scalar.wait_ge(v_st8, gq + 1)
                            scalar.sqrt(sd3[:, gq % 2:gq % 2 + 1],
                                        veps3[:, gq % 2:gq % 2 + 1]).then_inc(s_sq8, 1)
                            scalar.wait_ge(v_nm3, gq + 1)
                            rwait(scalar, d_xn1w, gq - 2)
                            scalar.activation(
                                out=xn1b_sb[:, gq % 2, :], in_=x1a_sb[:, gq % 2, :],
                                func=AF.Identity, bias=nmrs3[:, gq % 2:gq % 2 + 1],
                                scale=rs3[:, gq % 2:gq % 2 + 1]).then_inc(s_nm8, 1)
                            scalar.wait_ge(s_nm8, gq + 1)
                            rinc(scalar.dma_start(out=xn1_rows[gq],
                                             in_=xn1b_sb[:, gq % 2, :]), d_xn1w, gq)

                    def comp_seg(scalar, q):
                        for ht in range(NH):
                            gh = q * NH + ht
                            scalar.wait_ge(p_fc, gh + 1)
                            _w(scalar, s_sqq, gh)
                            scalar.activation(
                                out=r_sb[:, 0, :], in_=ps_fc[gh % 2][:, 0:CH2],
                                func=AF.Relu, scale=am_fc[:, 0:1]).then_inc(s_rev, 1)
                            scalar.wait_ge(s_rev, gh + 1)
                            scalar.activation(
                                out=hfc_sb[:, ht, :], in_=r_sb[:, 0, :],
                                func=AF.Square).then_inc(s_sqq, 1)
                        for ct in range(NC):
                            gc = q * NC + ct
                            scalar.wait_ge(p_pj, gc + 1)
                            if ct == 0:
                                _w(scalar, p_yt, NC * TT2 * q)
                            scalar.copy(out=y_sb[:, ct, :],
                                        in_=ps_pj[gc % 2][:, 0:CH2]).then_inc(s_yev, 1)
                        for tt in range(TT2):
                            gq = q * TT2 + tt
                            target = LN3P * min(q + 2, NQ) + LN4P * q + 5 * tt + 3
                            scalar.wait_ge(vs8, target)
                            scalar.sqrt(sd4[:, 0:1], veps4[:, 0:1]).then_inc(s_sq8b, 1)
                            scalar.wait_ge(v_fold8, gq + 1)
                            _w(scalar, v_o8, gq)
                            scalar.activation(
                                out=outb_sb[:, 0, :], in_=ps_yT[:, 0:C],
                                func=AF.Identity, bias=bseff[:, 0:1],
                                scale=sceff[:, 0:1]).then_inc(g_xp8, 1)

                    ln3s_seg(scalar, 0)
                    for q in range(1, NQ):
                        ln3s_seg(scalar, q)
                        comp_seg(scalar, q - 1)
                    comp_seg(scalar, NQ - 1)

                @block.sync
                def _(sync):
                    for q in range(NQ):
                        if q == 0:
                            rwait_all(sync, d_sw, NHB + NC)
                        rwait_all(sync, d_xn1w, TT2 * (q + 1))
                        _w(sync, p_fc, NH * (q - 1))
                        for ct in range(NC):
                            sync.dma_start(
                                out=xn1T_sb[:, q % 2, ct, :],
                                in_=xn1_d[q * CH2:(q + 1) * CH2, ct * 128:(ct + 1) * 128],
                                transpose=True).then_inc(d_xn1T[q % 2], 16)
                        def sfc_load(hb):
                            ghb = q * NHB + hb
                            _w(sync, p_fc, 2 * ghb - 2)
                            rinc(sync.dma_start(
                                out=sfc_sb[:, ghb % 2, :, :],
                                in_=sfcT_t[hb]), d_sfc, ghb)
                        for hb in range(2):
                            sfc_load(hb)
                        if q >= 1:
                            for tt in range(TT2):
                                gq = (q - 1) * TT2 + tt
                                sync.wait_ge(v_o8, gq + 1)
                                rinc(sync.dma_start(out=out_rows[gq],
                                                 in_=out2_sb[:, gq % 2, :]), d_out8, gq)
                        for hb in range(2, NHB):
                            sfc_load(hb)
                        for ct in range(NC):
                            gc = q * NC + ct
                            _w(sync, p_pj, gc - 1)
                            rinc(sync.dma_start(
                                out=spj_sb[:, gc % 2, :, :],
                                in_=spjT_t[ct]), d_spr, gc)
                    for tt in range(TT2):
                        gq = (NQ - 1) * TT2 + tt
                        sync.wait_ge(v_o8, gq + 1)
                        rinc(sync.dma_start(out=out_rows[gq],
                                         in_=out2_sb[:, gq % 2, :]), d_out8, gq)
                    rwait_all(sync, d_out8, NT128)

                @block.tensor
                def _(tensor):
                    for q in range(NQ):
                        tensor.wait_ge(d_xn1T[q % 2], 16 * NC * (q // 2 + 1))
                        for ht in range(NH):
                            gh = q * NH + ht
                            ghb = q * NHB + ht // 2
                            rwait(tensor, d_sfc, ghb)
                            _w(tensor, s_rev, gh - 1)
                            for ct in range(NC):
                                ins = tensor.matmul(
                                    ps_fc[gh % 2][:, 0:CH2],
                                    sfc_sb[:, ghb % 2, ct,
                                           (ht % 2) * 128:(ht % 2 + 1) * 128],
                                    xn1T_sb[:, q % 2, ct, :],
                                    start=(ct == 0), stop=(ct == NC - 1))
                            ins.then_inc(p_fc, 1)
                        tensor.wait_ge(s_sqq, NH * (q + 1))
                        for ct in range(NC):
                            gc = q * NC + ct
                            rwait(tensor, d_spr, gc)
                            _w(tensor, s_yev, gc - 1)
                            for ht in range(NH):
                                ins = tensor.matmul(
                                    ps_pj[gc % 2][:, 0:CH2],
                                    spj_sb[:, gc % 2, ht, :],
                                    hfc_sb[:, ht, :],
                                    start=(ht == 0), stop=(ht == NH - 1))
                            ins.then_inc(p_pj, 1)
                        for tt in range(TT2):
                            gq = q * TT2 + tt
                            _w(tensor, g_xp8, gq)
                            for ct in range(NC):
                                tensor.wait_ge(s_yev, q * NC + ct + 1)
                                tensor.transpose(
                                    ps_yT[:, ct * 128:(ct + 1) * 128],
                                    y_sb[:, ct, tt * 128:(tt + 1) * 128],
                                    idbf[:, :]).then_inc(p_yt, 1)

    return nc


def kernel(L0_attn_x, L0_attn_W, L0_attn_b, L0_ffn_w_fc, L0_ffn_w_proj):
    x = np.ascontiguousarray(np.asarray(L0_attn_x, dtype=np.float32))
    W = np.ascontiguousarray(np.asarray(L0_attn_W, dtype=np.float32))
    b = np.ascontiguousarray(np.asarray(L0_attn_b, dtype=np.float32))
    wfc = np.ascontiguousarray(np.asarray(L0_ffn_w_fc, dtype=np.float32))
    wpj = np.ascontiguousarray(np.asarray(L0_ffn_w_proj, dtype=np.float32))

    nc = build_program(FULL_CFG)
    ND = FULL_CFG["D"] // 128
    bT = np.ascontiguousarray(b.reshape(ND, 128).T)
    id32 = np.eye(128, dtype=np.float32)
    idbf = np.eye(128, dtype=ml_dtypes.bfloat16)
    in_maps = []
    for i in range(8):
        shard = np.ascontiguousarray(x[2 * i:2 * i + 2].reshape(4096, 2048))
        in_maps.append(dict(x=shard, W=W, bT=bT, wfc=wfc, wpj=wpj,
                            id32=id32, idbf=idbf))
    res = run_bass_kernel_spmd(nc, in_maps, core_ids=list(range(8)))
    outs = [np.asarray(r["out"]).reshape(2, 2048, 2048) for r in res.results]
    return np.concatenate(outs, axis=0)


# version 25 (final)
# speedup vs baseline: 1.2170x; 1.0019x over previous
import contextlib
import numpy as np
import ml_dtypes
import concourse.bass as bass
import concourse.mybir as mybir
from concourse.bass_utils import run_bass_kernel_spmd

F32 = mybir.dt.float32
BF16 = mybir.dt.bfloat16
AF = mybir.ActivationFunctionType
ALU = mybir.AluOpType
AX = mybir.AxisListType

FULL_CFG = dict(T=4096, C=2048, D=4096, H=5120, TB=2048, CH1=256, CH2=512)
EPS = 1e-5


def _w(eng, s, v):
    if v > 0:
        eng.wait_ge(s, v)


def rinc(instr, ring, i):
    instr.then_inc(ring[i % len(ring)], 16)


def rwait(eng, ring, i):
    if i >= 0:
        n = len(ring)
        eng.wait_ge(ring[i % n], 16 * (i // n + 1))


def rwait_all(eng, ring, n):
    r = len(ring)
    for s in range(r):
        c = (n - s + r - 1) // r
        if c > 0:
            eng.wait_ge(ring[s], 16 * c)


def _bn(vec, st6, dst_mv, src, width, self_sem, ctr):
    nb = max(1, width // 512)
    step = width // nb
    assert step * nb == width and step <= 512 and step % 2 == 0
    for j in range(nb):
        ins = vec.bn_stats(st6[:, 6 * j:6 * j + 6], src[:, j * step:(j + 1) * step])
    ins.then_inc(self_sem, 1)
    ctr += 1
    vec.wait_ge(self_sem, ctr)
    vec.bn_aggr(dst_mv, st6[:, 0:6 * nb]).then_inc(self_sem, 1)
    return ctr + 1


def build_program(cfg):
    T, C, D, H = cfg["T"], cfg["C"], cfg["D"], cfg["H"]
    TB, CH1, CH2 = cfg["TB"], cfg["CH1"], cfg["CH2"]

    NC = C // 128
    ND = D // 128
    NG = NC
    HALF = NG // 2
    NH = H // 128
    NHB = NH // 2
    NCH = T // CH1
    NQ = T // CH2
    TT1 = CH1 // 128
    TT2 = CH2 // 128
    RESET = TB // CH1
    GPC = 2 * HALF
    NT128 = T // 128

    nc = bass.Bass("TRN2", target_bir_lowering=False)

    x_d = nc.dram_tensor("x", [T, C], F32, kind="ExternalInput").ap()
    w_d = nc.dram_tensor("W", [D, C], F32, kind="ExternalInput").ap()
    bT_d = nc.dram_tensor("bT", [128, ND], F32, kind="ExternalInput").ap()
    wfc_d = nc.dram_tensor("wfc", [H, C], F32, kind="ExternalInput").ap()
    wpj_d = nc.dram_tensor("wpj", [C, H], F32, kind="ExternalInput").ap()
    id32_d = nc.dram_tensor("id32", [128, 128], F32, kind="ExternalInput").ap()
    idbf_d = nc.dram_tensor("idbf", [128, 128], BF16, kind="ExternalInput").ap()
    out_d = nc.dram_tensor("out", [T, C], F32, kind="ExternalOutput").ap()
    wbf_d = nc.dram_tensor("wbf_s", [D, C], BF16).ap()
    xn_d = nc.dram_tensor("xn_s", [T, C], BF16).ap()
    h_d = None
    x1_d = nc.dram_tensor("x1_s", [T, C], F32).ap()
    xn1_d = nc.dram_tensor("xn1_s", [T, C], BF16).ap()
    h_t = nc.dram_tensor("h_t", [T // CH1, 128, C // 128, CH1], F32).ap()
    sfcT_t = nc.dram_tensor("sfcT_t", [H // 256, 128, C // 128, 256], BF16).ap()
    spjT_t = nc.dram_tensor("spjT_t", [C // 128, 128, H // 128, 128], BF16).ap()

    x_ch1 = x_d.rearrange("(k tt p) c -> k p tt c", p=128, tt=TT1)
    xn_ch1 = xn_d.rearrange("(k tt p) c -> k p tt c", p=128, tt=TT1)
    x1_ch1 = x1_d.rearrange("(k tt p) c -> k p tt c", p=128, tt=TT1)
    x1_ch2 = x1_d.rearrange("(q tt p) c -> q p tt c", p=128, tt=TT2)
    xn1_ch2 = xn1_d.rearrange("(q tt p) c -> q p tt c", p=128, tt=TT2)
    out_rows = out_d.rearrange("(n p) c -> n p c", p=128)

    est = contextlib.ExitStack()
    sbuf = lambda n, s, d: est.enter_context(nc.sbuf_tensor(n, s, d))
    sem = lambda n: est.enter_context(nc.semaphore(n))

    with est:
        bT_sb = sbuf("bT_sb", [128, ND], F32)
        id32 = sbuf("id32_sb", [128, 128], F32)
        idbf = sbuf("idbf_sb", [128, 128], BF16)
        ones_row = sbuf("ones_row", [1, 128], F32)
        am_fc = sbuf("am_fc", [128, 1], F32)
        thr_fc = sbuf("thr_fc", [128, 1], F32)
        nthr_fc = sbuf("nthr_fc", [128, 1], F32)
        am_pj = sbuf("am_pj", [128, 1], F32)
        thr_pj = sbuf("thr_pj", [128, 1], F32)
        nthr_pj = sbuf("nthr_pj", [128, 1], F32)
        amsq_pj = sbuf("amsq_pj", [128, 1], F32)
        tot1 = sbuf("tot1", [1, 2], F32)
        tot2 = sbuf("tot2", [1, 2], F32)
        red1 = sbuf("red1", [128, 1], F32)
        carry = sbuf("carry", [128, NG], F32)

        ring = lambda n, r=2: [sem(n + str(i)) for i in range(r)]
        d_prep = sem("d_prep"); d_cast = sem("d_cast"); v_prep = sem("v_prep")
        d_wf = ring("d_wf"); v_abs = sem("v_abs"); fz = sem("fz")
        d_sf = ring("d_sf"); v_sg = sem("v_sg"); g_sg = sem("g_sg")
        p_tr2 = sem("p_tr2"); s_ev2 = sem("s_ev2"); d_sw = ring("d_sw")
        d_wt3 = sem("d_wt3"); d_wt5 = sem("d_wt5")
        d_x4 = ring("d_x4"); vs4 = sem("vs4"); s_sq4 = sem("s_sq4")
        v_lna = sem("v_lna"); v_lnb = sem("v_lnb"); s_nm4 = sem("s_nm4")
        d_xnw4 = ring("d_xnw4")
        d_xnT4 = ring("d_xnT4", 3); p_mm4 = sem("p_mm4"); s_ev4 = sem("s_ev4")
        v_sc4 = sem("v_sc4"); v_cr4 = sem("v_cr4"); d_hw4 = ring("d_hw4")
        d_xnT6 = ring("d_xnT6"); p_mm6 = sem("p_mm6"); s_ev6 = sem("s_ev6")
        v_sc6 = sem("v_sc6"); v_cr6 = sem("v_cr6"); d_hw6 = ring("d_hw6")
        d_hr7 = ring("d_hr7"); d_xr7 = ring("d_xr7"); p_tr7 = sem("p_tr7")
        vs7 = sem("vs7"); s_sq7 = sem("s_sq7"); v_nm7 = sem("v_nm7")
        g_xp7 = sem("g_xp7"); v_out7 = sem("v_out7"); d_x1w7 = ring("d_x1w7")
        d_x1a = ring("d_x1a"); d_x1b = ring("d_x1b")
        vs8 = sem("vs8"); s_sq8 = sem("s_sq8")
        v_st8 = sem("v_st8"); v_nm3 = sem("v_nm3"); s_nm8 = sem("s_nm8")
        d_xn1w = ring("d_xn1w"); d_xn1T = ring("d_xn1T")
        d_sfc = ring("d_sfc", 2); d_spr = ring("d_spr")
        p_fc = sem("p_fc"); s_rev = sem("s_rev"); s_sqq = sem("s_sqq")
        p_pj = sem("p_pj"); s_yev = sem("s_yev"); p_yt = sem("p_yt")
        s_sq8b = sem("s_sq8b"); v_fold8 = sem("v_fold8")
        g_xp8 = sem("g_xp8"); v_o8 = sem("v_o8"); d_out8 = ring("d_out8", 2)

        with nc.Block() as block:
            @block.sync
            def _(sync):
                sync.dma_start(out=bT_sb[:, :], in_=bT_d[:, :]).then_inc(d_prep, 16)
                sync.dma_start(out=id32[:, :], in_=id32_d[:, :]).then_inc(d_prep, 16)
                sync.dma_start(out=idbf[:, :], in_=idbf_d[:, :]).then_inc(d_prep, 16)

            @block.gpsimd
            def _(gpsimd):
                q4 = D // 4
                for i in range(4):
                    gpsimd.dma_start(out=wbf_d[i * q4:(i + 1) * q4, :],
                                     in_=w_d[i * q4:(i + 1) * q4, :]).then_inc(d_cast, 16)

            @block.vector
            def _(vector):
                vector.memset(ones_row[:, :], 1.0).then_inc(v_prep, 1)

            with contextlib.ExitStack() as s12:
                sb12 = lambda n, s, d: s12.enter_context(nc.sbuf_tensor(n, s, d))
                ps12 = lambda n, s, d: s12.enter_context(nc.psum_tensor(n, s, d))
                MW = max(C, H)
                wslab = sb12("wslab", [128, 2, MW], F32)
                acc_fc = sb12("acc_fc", [128, NH], F32)
                acc_pj = sb12("acc_pj", [128, NC], F32)
                nbs = sb12("nbs", [128, 2, MW], BF16)
                ssl = sb12("ssl", [128, 2, MW], BF16)
                stage_fc = sb12("stage_fc", [128, 2, NC, 256], BF16)
                stage_pj = sb12("stage_pj", [128, 2, NH, 128], BF16)
                ps_misc = ps12("ps_misc", [128, 512], F32)
                ps_bf = [ps12("ps_bf0", [128, 1024], BF16),
                         ps12("ps_bf1", [128, 1024], BF16)]

                @block.sync
                def _(sync):
                    for i in range(NH):
                        _w(sync, v_abs, i - 1)
                        rinc(sync.dma_start(out=wslab[:, i % 2, 0:C],
                                            in_=wfc_d[i * 128:(i + 1) * 128, :]), d_wf, i)
                    for j in range(NC):
                        _w(sync, v_abs, NH + j - 1)
                        rinc(sync.dma_start(out=wslab[:, j % 2, 0:H],
                                            in_=wpj_d[j * 128:(j + 1) * 128, :]), d_wf, NH + j)

                @block.vector
                def _(vector):
                    for i in range(NH):
                        rwait(vector, d_wf, i)
                        vector.tensor_reduce(out=acc_fc[:, i:i + 1], in_=wslab[:, i % 2, 0:C],
                                             axis=AX.X, op=ALU.add,
                                             apply_absolute_value=True).then_inc(v_abs, 1)
                    for j in range(NC):
                        rwait(vector, d_wf, NH + j)
                        vector.tensor_reduce(out=acc_pj[:, j:j + 1], in_=wslab[:, j % 2, 0:H],
                                             axis=AX.X, op=ALU.add,
                                             apply_absolute_value=True).then_inc(v_abs, 1)

                fzc = [0]

                def vst(vec, f):
                    _w(vec, fz, fzc[0])
                    f().then_inc(fz, 1)
                    fzc[0] += 1

                def tst(tensor, f):
                    _w(tensor, fz, fzc[0])
                    f().then_inc(fz, 1)
                    fzc[0] += 1

                @block.vector
                def _(vector):
                    vector.wait_ge(v_abs, NH + NC)
                    vst(vector, lambda: vector.tensor_reduce(
                        out=red1[:, 0:1], in_=acc_fc[:, 0:NH], axis=AX.X, op=ALU.add))

                @block.tensor
                def _(tensor):
                    tensor.wait_ge(d_prep, 48)
                    tensor.wait_ge(v_prep, 1)
                    tst(tensor, lambda: tensor.transpose(
                        ps_misc[0:1, 0:128], red1[:, 0:1], id32[:, :]))

                @block.vector
                def _(vector):
                    vst(vector, lambda: vector.tensor_reduce(
                        out=tot1[0:1, 0:1], in_=ps_misc[0:1, 0:128], axis=AX.X, op=ALU.add))
                    vst(vector, lambda: vector.tensor_scalar(
                        out=tot2[0:1, 0:1], in0=tot1[0:1, 0:1],
                        scalar1=1.0 / (H * C), scalar2=EPS, op0=ALU.mult, op1=ALU.max))

                @block.tensor
                def _(tensor):
                    tst(tensor, lambda: tensor.matmul(
                        ps_misc[0:128, 256:257], ones_row[0:1, 0:128],
                        tot2[0:1, 0:1], start=True, stop=True))

                @block.vector
                def _(vector):
                    vst(vector, lambda: vector.tensor_copy(
                        out=am_fc[:, 0:1], in_=ps_misc[0:128, 256:257]))
                    vst(vector, lambda: vector.tensor_scalar(
                        out=thr_fc[:, 0:1], in0=am_fc[:, 0:1], scalar1=0.7,
                        scalar2=None, op0=ALU.mult))
                    vst(vector, lambda: vector.tensor_scalar(
                        out=nthr_fc[:, 0:1], in0=thr_fc[:, 0:1], scalar1=-1.0,
                        scalar2=None, op0=ALU.mult))
                    vst(vector, lambda: vector.tensor_reduce(
                        out=red1[:, 0:1], in_=acc_pj[:, 0:NC], axis=AX.X, op=ALU.add))

                @block.tensor
                def _(tensor):
                    tst(tensor, lambda: tensor.transpose(
                        ps_misc[0:1, 0:128], red1[:, 0:1], id32[:, :]))

                @block.vector
                def _(vector):
                    vst(vector, lambda: vector.tensor_reduce(
                        out=tot1[0:1, 1:2], in_=ps_misc[0:1, 0:128], axis=AX.X, op=ALU.add))
                    vst(vector, lambda: vector.tensor_scalar(
                        out=tot2[0:1, 1:2], in0=tot1[0:1, 1:2],
                        scalar1=1.0 / (H * C), scalar2=EPS, op0=ALU.mult, op1=ALU.max))

                @block.tensor
                def _(tensor):
                    tst(tensor, lambda: tensor.matmul(
                        ps_misc[0:128, 256:257], ones_row[0:1, 0:128],
                        tot2[0:1, 1:2], start=True, stop=True))

                @block.vector
                def _(vector):
                    vst(vector, lambda: vector.tensor_copy(
                        out=am_pj[:, 0:1], in_=ps_misc[0:128, 256:257]))
                    vst(vector, lambda: vector.tensor_scalar(
                        out=thr_pj[:, 0:1], in0=am_pj[:, 0:1], scalar1=0.7,
                        scalar2=None, op0=ALU.mult))
                    vst(vector, lambda: vector.tensor_scalar(
                        out=nthr_pj[:, 0:1], in0=thr_pj[:, 0:1], scalar1=-1.0,
                        scalar2=None, op0=ALU.mult))
                    vst(vector, lambda: vector.tensor_scalar(
                        out=amsq_pj[:, 0:1], in0=am_pj[:, 0:1], scalar1=am_pj[:, 0:1],
                        scalar2=None, op0=ALU.mult))

                FZ_END = fzc[0]

                @block.sync
                def _(sync):
                    for i in range(NH):
                        if i < 2:
                            sync.wait_ge(v_abs, NH + NC)
                        else:
                            sync.wait_ge(v_sg, 2 * (i - 2) + 2)
                        rinc(sync.dma_start(out=wslab[:, i % 2, 0:C],
                                            in_=wfc_d[i * 128:(i + 1) * 128, :]), d_sf, i)
                    for j in range(NC):
                        if j < 2:
                            sync.wait_ge(v_sg, 2 * (NH - 2 + j) + 2)
                        else:
                            sync.wait_ge(g_sg, 2 * (j - 2) + 2)
                        rinc(sync.dma_start(out=wslab[:, j % 2, 0:H],
                                            in_=wpj_d[j * 128:(j + 1) * 128, :]), d_sf, NH + j)

                @block.vector
                def _(vector):
                    vector.wait_ge(fz, FZ_END)
                    for i in range(NH):
                        rwait(vector, d_sf, i)
                        _w(vector, p_tr2, NC * (i - 1))
                        vector.tensor_scalar(
                            out=nbs[:, i % 2, 0:C], in0=wslab[:, i % 2, 0:C],
                            scalar1=nthr_fc[:, 0:1], scalar2=-1.0,
                            op0=ALU.is_le, op1=ALU.mult).then_inc(v_sg, 1)
                        vector.wait_ge(v_sg, 2 * i + 1)
                        vector.scalar_tensor_tensor(
                            out=ssl[:, i % 2, 0:C], in0=wslab[:, i % 2, 0:C],
                            scalar=thr_fc[:, 0:1], in1=nbs[:, i % 2, 0:C],
                            op0=ALU.is_ge, op1=ALU.add).then_inc(v_sg, 1)
                    for j in range(NC):
                        rwait(vector, d_sf, NH + j)
                        if j < 2:
                            vector.wait_ge(p_tr2, NC * (NH - 1 + j))
                        else:
                            vector.wait_ge(p_tr2, NC * NH + NH * (j - 1))
                        vector.tensor_scalar(
                            out=nbs[:, j % 2, 0:H], in0=wslab[:, j % 2, 0:H],
                            scalar1=nthr_pj[:, 0:1], scalar2=-1.0,
                            op0=ALU.is_le, op1=ALU.mult).then_inc(g_sg, 1)
                        vector.wait_ge(g_sg, 2 * j + 1)
                        vector.scalar_tensor_tensor(
                            out=ssl[:, j % 2, 0:H], in0=wslab[:, j % 2, 0:H],
                            scalar=thr_pj[:, 0:1], in1=nbs[:, j % 2, 0:H],
                            op0=ALU.is_ge, op1=ALU.add).then_inc(g_sg, 1)


                @block.tensor
                def _(tensor):
                    tcnt = 0
                    for i in range(NH):
                        tensor.wait_ge(v_sg, 2 * i + 2)
                        for ct in range(NC):
                            _w(tensor, s_ev2, tcnt - 1)
                            tensor.transpose(ps_bf[tcnt % 2][0:128, 0:128],
                                             ssl[:, i % 2, ct * 128:(ct + 1) * 128],
                                             idbf[:, :]).then_inc(p_tr2, 1)
                            tcnt += 1
                    for j in range(NC):
                        tensor.wait_ge(g_sg, 2 * j + 2)
                        for ht in range(NH):
                            _w(tensor, s_ev2, tcnt - 1)
                            tensor.transpose(ps_bf[tcnt % 2][0:128, 0:128],
                                             ssl[:, j % 2, ht * 128:(ht + 1) * 128],
                                             idbf[:, :]).then_inc(p_tr2, 1)
                            tcnt += 1

                @block.scalar
                def _(scalar):
                    tcnt = 0
                    for i in range(NH):
                        hb, hoff = i // 2, (i % 2) * 128
                        if i % 2 == 0:
                            rwait(scalar, d_sw, hb - 2)
                        for ct in range(NC):
                            scalar.wait_ge(p_tr2, tcnt + 1)
                            scalar.copy(out=stage_fc[:, hb % 2, ct, hoff:hoff + 128],
                                        in_=ps_bf[tcnt % 2][0:128, 0:128]).then_inc(s_ev2, 1)
                            tcnt += 1
                    for j in range(NC):
                        rwait(scalar, d_sw, NHB + j - 2)
                        for ht in range(NH):
                            scalar.wait_ge(p_tr2, tcnt + 1)
                            scalar.copy(out=stage_pj[:, j % 2, ht, :],
                                        in_=ps_bf[tcnt % 2][0:128, 0:128]).then_inc(s_ev2, 1)
                            tcnt += 1

                @block.gpsimd
                def _(gpsimd):
                    for hb in range(NHB):
                        gpsimd.wait_ge(s_ev2, NC * (2 * hb + 2))
                        rinc(gpsimd.dma_start(out=sfcT_t[hb],
                                         in_=stage_fc[:, hb % 2, :, :]), d_sw, hb)
                    for j in range(NC):
                        gpsimd.wait_ge(s_ev2, NC * NH + NH * (j + 1))
                        rinc(gpsimd.dma_start(out=spjT_t[j],
                                         in_=stage_pj[:, j % 2, :, :]), d_sw, NHB + j)

            S2_DONE = 16 * (NH + NC)

            with contextlib.ExitStack() as sp1:
                sb1 = lambda n, s, d: sp1.enter_context(nc.sbuf_tensor(n, s, d))
                wt_sb = sb1("wt_sb", [128, NC, 2 * NG * 128], BF16)
                x_sb = sb1("x_sb", [128, 1, TT1, C], F32)
                xnb_sb = sb1("xnb_sb", [128, 1, TT1, C], BF16)
                xnT_sb = sb1("xnT_sb", [128, 3, NC, CH1], BF16)
                g_sb = sb1("g_sb", [128, 2, CH1], F32)
                v_sb = sb1("v_sb", [128, 2, CH1], F32)
                h_sb = sb1("h_sb", [128, 1, NG, CH1], F32)
                st6a = sb1("st6a", [128, TT1, 24], F32)
                mv1 = sb1("mv1", [128, TT1, 2], F32)
                veps1 = sb1("veps1", [128, TT1], F32)
                sd1 = sb1("sd1", [128, TT1], F32)
                rs1 = sb1("rs1", [128, TT1], F32)
                nmrs1 = sb1("nmrs1", [128, TT1], F32)
                ps_par = [sp1.enter_context(nc.psum_tensor(f"ps_par{i}", [128, 512], F32))
                          for i in range(4)]
                GP1 = 2 * NG

                @block.sync
                def _(sync):
                    sync.wait_ge(d_cast, 64)
                    rwait_all(sync, d_sw, NHB + NC)
                    for ct in range(NC):
                        sync.dma_start(out=wt_sb[:, ct, 0:NG * 128],
                                       in_=wbf_d[0:C, ct * 128:(ct + 1) * 128],
                                       transpose=True).then_inc(d_wt3, 16)
                        sync.dma_start(out=wt_sb[:, ct, NG * 128:2 * NG * 128],
                                       in_=wbf_d[C:D, ct * 128:(ct + 1) * 128],
                                       transpose=True).then_inc(d_wt3, 16)

                @block.sync
                def _(sync):
                    for k in range(NCH):
                        _w(sync, s_nm4, TT1 * k)
                        rinc(sync.dma_start(out=x_sb[:, 0, :, :],
                                       in_=x_ch1[k]), d_x4, k)
                        rwait(sync, d_xnw4, k)
                        _w(sync, p_mm4, GP1 * (k - 2))
                        for ct in range(NC):
                            sync.dma_start(
                                out=xnT_sb[:, k % 3, ct, :],
                                in_=xn_d[k * CH1:(k + 1) * CH1, ct * 128:(ct + 1) * 128],
                                transpose=True).then_inc(d_xnT4[k % 3], 16)

                @block.vector
                def _(vector):
                    vcnt = 0

                    def stats_seg(vector, k, vcnt):
                        rwait(vector, d_x4, k)
                        for tt in range(TT1):
                            vcnt = _bn(vector, st6a[:, tt, :], mv1[:, tt, :],
                                       x_sb[:, 0, tt, :], C, vs4, vcnt)
                        vector.wait_ge(vs4, vcnt)
                        _w(vector, s_sq4, k)
                        for tt in range(TT1):
                            ins = vector.tensor_scalar(
                                out=veps1[:, tt:tt + 1], in0=mv1[:, tt, 1:2],
                                scalar1=EPS, scalar2=None, op0=ALU.add)
                        ins.then_inc(v_lna, 1)
                        vector.wait_ge(s_sq4, k + 1)
                        _w(vector, s_nm4, TT1 * k)
                        vector.reciprocal(out=rs1[:, 0:TT1],
                                          in_=sd1[:, 0:TT1]).then_inc(vs4, 1)
                        vcnt += 1
                        vector.wait_ge(vs4, vcnt)
                        for tt in range(TT1):
                            ins = vector.scalar_tensor_tensor(
                                out=nmrs1[:, tt:tt + 1], in0=mv1[:, tt, 0:1],
                                scalar=-1.0, in1=rs1[:, tt:tt + 1],
                                op0=ALU.mult, op1=ALU.mult)
                        ins.then_inc(v_lnb, 1)
                        return vcnt

                    def scan_seg(vector, k):
                        for pair in range(NG):
                            gp = k * NG + pair
                            vector.wait_ge(s_ev4, k * GP1 + 2 * pair + 2)
                            rwait(vector, d_hw4, k - 1)
                            if k % RESET == 0:
                                init = 0.0
                            else:
                                vector.wait_ge(v_cr4, (k - 1) * NG + pair + 1)
                                init = carry[:, pair:pair + 1]
                            vector.tensor_tensor_scan(
                                out=h_sb[:, 0, pair, :],
                                data0=g_sb[:, gp % 2, :], data1=v_sb[:, gp % 2, :],
                                initial=init, op0=ALU.mult,
                                op1=ALU.add).then_inc(v_sc4, 1)
                            vector.wait_ge(v_sc4, gp + 1)
                            vector.tensor_copy(
                                out=carry[:, pair:pair + 1],
                                in_=h_sb[:, 0, pair, CH1 - 1:CH1]).then_inc(v_cr4, 1)

                    vcnt = stats_seg(vector, 0, vcnt)
                    for k in range(1, NCH):
                        vcnt = stats_seg(vector, k, vcnt)
                        scan_seg(vector, k - 1)
                    scan_seg(vector, NCH - 1)

                @block.scalar
                def _(scalar):
                    scalar.wait_ge(d_prep, 48)

                    def ln1_seg(scalar, k):
                        scalar.wait_ge(v_lna, k + 1)
                        scalar.sqrt(sd1[:, 0:TT1],
                                    veps1[:, 0:TT1]).then_inc(s_sq4, 1)
                        scalar.wait_ge(v_lnb, k + 1)
                        rwait(scalar, d_xnw4, k - 1)
                        for tt in range(TT1):
                            scalar.activation(
                                out=xnb_sb[:, 0, tt, :],
                                in_=x_sb[:, 0, tt, :], func=AF.Identity,
                                bias=nmrs1[:, tt:tt + 1],
                                scale=rs1[:, tt:tt + 1]).then_inc(s_nm4, 1)
                        scalar.wait_ge(s_nm4, TT1 * (k + 1))
                        rinc(scalar.dma_start(out=xn_ch1[k],
                                         in_=xnb_sb[:, 0, :, :]), d_xnw4, k)

                    def evac_seg(scalar, k):
                        for pair in range(NG):
                            for role in range(2):
                                gi = k * GP1 + 2 * pair + role
                                gp = k * NG + pair
                                scalar.wait_ge(p_mm4, gi + 1)
                                _w(scalar, v_sc4, gp - 1)
                                if role == 0:
                                    scalar.activation(
                                        out=g_sb[:, gp % 2, :],
                                        in_=ps_par[gi % 4][:, 0:CH1],
                                        func=AF.Sigmoid,
                                        bias=bT_sb[:, pair:pair + 1]).then_inc(s_ev4, 1)
                                else:
                                    scalar.activation(
                                        out=v_sb[:, gp % 2, :],
                                        in_=ps_par[gi % 4][:, 0:CH1],
                                        func=AF.Tanh,
                                        bias=bT_sb[:, NG + pair:NG + pair + 1]).then_inc(s_ev4, 1)

                    ln1_seg(scalar, 0)
                    for k in range(1, NCH):
                        ln1_seg(scalar, k)
                        evac_seg(scalar, k - 1)
                    evac_seg(scalar, NCH - 1)

                @block.tensor
                def _(tensor):
                    tensor.wait_ge(d_wt3, 32 * NC)
                    for k in range(NCH):
                        tensor.wait_ge(d_xnT4[k % 3], 16 * NC * (k // 3 + 1))
                        for pair in range(NG):
                            for role in range(2):
                                gi = k * GP1 + 2 * pair + role
                                _w(tensor, s_ev4, gi - 3)
                                col0 = (role * NG + pair) * 128
                                pp = ps_par[gi % 4]
                                for ct in range(NC):
                                    ins = tensor.matmul(
                                        pp[:, 0:CH1],
                                        wt_sb[:, ct, col0:col0 + 128],
                                        xnT_sb[:, k % 3, ct, :],
                                        start=(ct == 0), stop=(ct == NC - 1))
                                ins.then_inc(p_mm4, 1)

                @block.gpsimd
                def _(gpsimd):
                    for k in range(NCH):
                        gpsimd.wait_ge(v_sc4, NG * (k + 1))
                        rinc(gpsimd.dma_start(
                            out=h_t[k][:, 0:NG, :],
                            in_=h_sb[:, 0, :, :]), d_hw4, k)

            with contextlib.ExitStack() as sp7:
                sb7 = lambda n, s, d: sp7.enter_context(nc.sbuf_tensor(n, s, d))
                htin_sb = sb7("htin_sb", [128, 2, NG, CH1], F32)
                x2_sb = sb7("x2_sb", [128, 2, TT1, C], F32)
                xp_sb = sb7("xp_sb", [128, 2, C], F32)
                x1o_sb = sb7("x1o_sb", [128, 2, TT1, C], F32)
                st6b = sb7("st6b", [128, 24], F32)
                mv2 = sb7("mv2", [128, 2], F32)
                veps2 = sb7("veps2", [128, 1], F32)
                sd2 = sb7("sd2", [128, 1], F32)
                rs2 = sb7("rs2", [128, 1], F32)
                nmrs2 = sb7("nmrs2", [128, 1], F32)
                ps_hT = [sp7.enter_context(nc.psum_tensor(f"ps_hT{i}", [128, 2048], F32))
                         for i in range(2)]

                @block.sync
                def _(sync):
                    for k in range(NCH):
                        if k == 0:
                            rwait_all(sync, d_hw4, NCH)
                        _w(sync, p_tr7, NG * TT1 * (k - 1))
                        rinc(sync.dma_start(out=htin_sb[:, k % 2, :, :],
                                       in_=h_t[k]), d_hr7, k)

                @block.scalar
                def _(scalar):
                    for k in range(NCH):
                        if k == 0:
                            rwait_all(scalar, d_hw4, NCH)
                        _w(scalar, g_xp7, TT1 * (k - 1))
                        rinc(scalar.dma_start(out=x2_sb[:, k % 2, :, :],
                                         in_=x_ch1[k]), d_xr7, k)
                        for tt in range(TT1):
                            gt = k * TT1 + tt
                            scalar.wait_ge(vs7, 4 * gt + 3)
                            scalar.sqrt(sd2[:, 0:1], veps2[:, 0:1]).then_inc(s_sq7, 1)

                @block.tensor
                def _(tensor):
                    for k in range(NCH):
                        rwait(tensor, d_hr7, k)
                        for tt in range(TT1):
                            gt = k * TT1 + tt
                            _w(tensor, v_out7, gt - 1)
                            for ct in range(NG):
                                tensor.transpose(
                                    ps_hT[gt % 2][:, ct * 128:(ct + 1) * 128],
                                    htin_sb[:, k % 2, ct, tt * 128:(tt + 1) * 128],
                                    id32[:, :]).then_inc(p_tr7, 1)

                @block.vector
                def _(vector):
                    vcnt = 0
                    for k in range(NCH):
                        rwait(vector, d_x1w7, k - 2)
                        for tt in range(TT1):
                            gt = k * TT1 + tt
                            vector.wait_ge(p_tr7, NG * (gt + 1))
                            vcnt = _bn(vector, st6b[:, :], mv2[:, :],
                                       ps_hT[gt % 2][:, 0:C], C, vs7, vcnt)
                            vector.wait_ge(vs7, vcnt)
                            vector.tensor_scalar(
                                out=veps2[:, 0:1], in0=mv2[:, 1:2], scalar1=EPS,
                                scalar2=None, op0=ALU.add).then_inc(vs7, 1)
                            vcnt += 1
                            vector.wait_ge(s_sq7, gt + 1)
                            vector.reciprocal(out=rs2[:, 0:1],
                                              in_=sd2[:, 0:1]).then_inc(vs7, 1)
                            vcnt += 1
                            vector.wait_ge(vs7, vcnt)
                            vector.scalar_tensor_tensor(
                                out=nmrs2[:, 0:1], in0=mv2[:, 0:1], scalar=-1.0,
                                in1=rs2[:, 0:1], op0=ALU.mult,
                                op1=ALU.mult).then_inc(v_nm7, 1)
                            rwait(vector, d_xr7, k)
                            vector.wait_ge(v_nm7, gt + 1)
                            vector.tensor_scalar(
                                out=xp_sb[:, gt % 2, :], in0=x2_sb[:, k % 2, tt, :],
                                scalar1=nmrs2[:, 0:1], scalar2=None,
                                op0=ALU.add).then_inc(g_xp7, 1)
                            vector.wait_ge(g_xp7, gt + 1)
                            vector.scalar_tensor_tensor(
                                out=x1o_sb[:, k % 2, tt, :], in0=ps_hT[gt % 2][:, 0:C],
                                scalar=rs2[:, 0:1], in1=xp_sb[:, gt % 2, :],
                                op0=ALU.mult, op1=ALU.add).then_inc(v_out7, 1)

                @block.gpsimd
                def _(gpsimd):
                    for k in range(NCH):
                        gpsimd.wait_ge(v_out7, TT1 * (k + 1))
                        rinc(gpsimd.dma_start(out=x1_ch1[k],
                                         in_=x1o_sb[:, k % 2, :, :]), d_x1w7, k)

            with contextlib.ExitStack() as sp8:
                sb8 = lambda n, s, d: sp8.enter_context(nc.sbuf_tensor(n, s, d))
                x1a_sb = sb8("x1a_sb", [128, 2, C], F32)
                x1b_sb = sb8("x1b_sb", [128, 2, C], F32)
                xn1b_sb = sb8("xn1b_sb", [128, 2, C], BF16)
                xn1T_sb = sb8("xn1T_sb", [128, 2, NC, CH2], BF16)
                sfc_sb = sb8("sfc_sb", [128, 2, NC, 256], BF16)
                spj_sb = sb8("spj_sb", [128, 2, NH, 128], BF16)
                hfc_sb = sb8("hfc_sb", [128, NH, CH2], BF16)
                r_sb = sb8("r_sb", [128, 1, CH2], F32)
                y_sb = sb8("y_sb", [128, NC, CH2], BF16)
                outb_sb = sb8("outb_sb", [128, 1, C], F32)
                out2_sb = sb8("out2_sb", [128, 2, C], F32)
                st6c = sb8("st6c", [128, 24], F32)
                mv3 = sb8("mv3", [128, 2], F32)
                veps3 = sb8("veps3", [128, 2], F32)
                sd3 = sb8("sd3", [128, 2], F32)
                rs3 = sb8("rs3", [128, 2], F32)
                nmrs3 = sb8("nmrs3", [128, 2], F32)
                mv4 = sb8("mv4", [128, 2], F32)
                veps4 = sb8("veps4", [128, 1], F32)
                sd4 = sb8("sd4", [128, 1], F32)
                rstd4 = sb8("rstd4", [128, 1], F32)
                sceff = sb8("sceff", [128, 1], F32)
                bseff = sb8("bseff", [128, 1], F32)
                ps_fc = [sp8.enter_context(nc.psum_tensor(f"ps_fc{i}", [128, 512], F32))
                         for i in range(2)]
                ps_pj = [sp8.enter_context(nc.psum_tensor(f"ps_pj{i}", [128, 512], F32))
                         for i in range(2)]
                ps_yT = [sp8.enter_context(nc.psum_tensor(f"ps_yT{i}", [128, 2048], BF16))
                         for i in range(2)]
                xn1_rows = xn1_d.rearrange("(n p) c -> n p c", p=128)
                x1_rows128 = x1_d.rearrange("(n p) c -> n p c", p=128)
                NGQ = NQ * TT2
                VS8_CHUNK = 3 * TT2 + 5 * TT2

                @block.gpsimd
                def _(gpsimd):
                    def x1b_seg(gpsimd, q):
                        for tt in range(TT2):
                            gq = q * TT2 + tt
                            _w(gpsimd, v_o8, gq - 1)
                            rinc(gpsimd.dma_start(out=x1b_sb[:, gq % 2, :],
                                           in_=x1_rows128[gq]), d_x1b, gq)

                    for q in range(NQ):
                        for tt in range(TT2):
                            gq = q * TT2 + tt
                            if gq == 0:
                                rwait_all(gpsimd, d_x1w7, NCH)
                            _w(gpsimd, s_nm8, gq - 1)
                            rinc(gpsimd.dma_start(out=x1a_sb[:, gq % 2, :],
                                           in_=x1_rows128[gq]), d_x1a, gq)
                        if q >= 1:
                            x1b_seg(gpsimd, q - 1)
                    x1b_seg(gpsimd, NQ - 1)

                @block.vector
                def _(vector):
                    vcnt = [0]

                    def ln3_seg(vector, q):
                        for tt in range(TT2):
                            gq = q * TT2 + tt
                            rwait(vector, d_x1a, gq)
                            _w(vector, s_sq8, gq - 1)
                            vcnt[0] = _bn(vector, st6c[:, :], mv3[:, :],
                                          x1a_sb[:, gq % 2, :], C, vs8, vcnt[0])
                            vector.wait_ge(vs8, vcnt[0])
                            vector.tensor_scalar(
                                out=veps3[:, gq % 2:gq % 2 + 1], in0=mv3[:, 1:2],
                                scalar1=EPS, scalar2=None,
                                op0=ALU.add).then_inc(v_st8, 1)
                            vector.wait_ge(s_sq8, gq + 1)
                            _w(vector, s_nm8, gq - 1)
                            vector.reciprocal(
                                out=rs3[:, gq % 2:gq % 2 + 1],
                                in_=sd3[:, gq % 2:gq % 2 + 1]).then_inc(vs8, 1)
                            vcnt[0] += 1
                            vector.wait_ge(vs8, vcnt[0])
                            vector.scalar_tensor_tensor(
                                out=nmrs3[:, gq % 2:gq % 2 + 1], in0=mv3[:, 0:1],
                                scalar=-1.0, in1=rs3[:, gq % 2:gq % 2 + 1],
                                op0=ALU.mult, op1=ALU.mult).then_inc(v_nm3, 1)

                    def ln4_seg(vector, q):
                        for tt in range(TT2):
                            gq = q * TT2 + tt
                            vector.wait_ge(p_yt, NC * (gq + 1))
                            vcnt[0] = _bn(vector, st6c[:, :], mv4[:, :],
                                          ps_yT[gq % 2][:, 0:C], C, vs8, vcnt[0])
                            vector.wait_ge(vs8, vcnt[0])
                            vector.tensor_scalar(
                                out=veps4[:, 0:1], in0=mv4[:, 1:2],
                                scalar1=amsq_pj[:, 0:1], scalar2=EPS,
                                op0=ALU.mult, op1=ALU.add).then_inc(vs8, 1)
                            vcnt[0] += 1
                            vector.wait_ge(s_sq8b, gq + 1)
                            vector.reciprocal(out=rstd4[:, 0:1],
                                              in_=sd4[:, 0:1]).then_inc(vs8, 1)
                            vcnt[0] += 1
                            vector.wait_ge(vs8, vcnt[0])
                            _w(vector, g_xp8, gq)
                            vector.tensor_scalar(
                                out=sceff[:, 0:1], in0=am_pj[:, 0:1],
                                scalar1=rstd4[:, 0:1], scalar2=None,
                                op0=ALU.mult).then_inc(vs8, 1)
                            vcnt[0] += 1
                            vector.wait_ge(vs8, vcnt[0])
                            vector.scalar_tensor_tensor(
                                out=bseff[:, 0:1], in0=mv4[:, 0:1], scalar=-1.0,
                                in1=sceff[:, 0:1], op0=ALU.mult,
                                op1=ALU.mult).then_inc(v_fold8, 1)
                            vector.wait_ge(g_xp8, gq + 1)
                            rwait(vector, d_x1b, gq)
                            rwait(vector, d_out8, gq - 2)
                            vector.scalar_tensor_tensor(
                                out=out2_sb[:, gq % 2, :], in0=outb_sb[:, 0, :],
                                scalar=1.0, in1=x1b_sb[:, gq % 2, :],
                                op0=ALU.mult, op1=ALU.add).then_inc(v_o8, 1)

                    ln3_seg(vector, 0)
                    for q in range(1, NQ):
                        ln3_seg(vector, q)
                        ln4_seg(vector, q - 1)
                    ln4_seg(vector, NQ - 1)

                @block.scalar
                def _(scalar):
                    LN3P = 3 * TT2
                    LN4P = 5 * TT2

                    def ln3s_seg(scalar, q):
                        for tt in range(TT2):
                            gq = q * TT2 + tt
                            scalar.wait_ge(v_st8, gq + 1)
                            scalar.sqrt(sd3[:, gq % 2:gq % 2 + 1],
                                        veps3[:, gq % 2:gq % 2 + 1]).then_inc(s_sq8, 1)
                            scalar.wait_ge(v_nm3, gq + 1)
                            rwait(scalar, d_xn1w, gq - 2)
                            scalar.activation(
                                out=xn1b_sb[:, gq % 2, :], in_=x1a_sb[:, gq % 2, :],
                                func=AF.Identity, bias=nmrs3[:, gq % 2:gq % 2 + 1],
                                scale=rs3[:, gq % 2:gq % 2 + 1]).then_inc(s_nm8, 1)
                            scalar.wait_ge(s_nm8, gq + 1)
                            rinc(scalar.dma_start(out=xn1_rows[gq],
                                             in_=xn1b_sb[:, gq % 2, :]), d_xn1w, gq)

                    def comp_seg(scalar, q):
                        for ht in range(NH):
                            gh = q * NH + ht
                            scalar.wait_ge(p_fc, gh + 1)
                            _w(scalar, s_sqq, gh)
                            scalar.activation(
                                out=r_sb[:, 0, :], in_=ps_fc[gh % 2][:, 0:CH2],
                                func=AF.Relu, scale=am_fc[:, 0:1]).then_inc(s_rev, 1)
                            scalar.wait_ge(s_rev, gh + 1)
                            scalar.activation(
                                out=hfc_sb[:, ht, :], in_=r_sb[:, 0, :],
                                func=AF.Square).then_inc(s_sqq, 1)
                        for ct in range(NC):
                            gc = q * NC + ct
                            scalar.wait_ge(p_pj, gc + 1)
                            if ct == 0:
                                _w(scalar, p_yt, NC * TT2 * q)
                            scalar.copy(out=y_sb[:, ct, :],
                                        in_=ps_pj[gc % 2][:, 0:CH2]).then_inc(s_yev, 1)
                        for tt in range(TT2):
                            gq = q * TT2 + tt
                            target = LN3P * min(q + 2, NQ) + LN4P * q + 5 * tt + 3
                            scalar.wait_ge(vs8, target)
                            scalar.sqrt(sd4[:, 0:1], veps4[:, 0:1]).then_inc(s_sq8b, 1)
                            scalar.wait_ge(v_fold8, gq + 1)
                            _w(scalar, v_o8, gq)
                            scalar.activation(
                                out=outb_sb[:, 0, :], in_=ps_yT[gq % 2][:, 0:C],
                                func=AF.Identity, bias=bseff[:, 0:1],
                                scale=sceff[:, 0:1]).then_inc(g_xp8, 1)

                    ln3s_seg(scalar, 0)
                    for q in range(1, NQ):
                        ln3s_seg(scalar, q)
                        comp_seg(scalar, q - 1)
                    comp_seg(scalar, NQ - 1)

                @block.sync
                def _(sync):
                    for q in range(NQ):
                        if q == 0:
                            rwait_all(sync, d_sw, NHB + NC)
                        rwait_all(sync, d_xn1w, TT2 * (q + 1))
                        _w(sync, p_fc, NH * (q - 1))
                        for ct in range(NC):
                            sync.dma_start(
                                out=xn1T_sb[:, q % 2, ct, :],
                                in_=xn1_d[q * CH2:(q + 1) * CH2, ct * 128:(ct + 1) * 128],
                                transpose=True).then_inc(d_xn1T[q % 2], 16)
                        if q >= 1:
                            for tt in range(TT2):
                                gq = (q - 1) * TT2 + tt
                                sync.wait_ge(v_o8, gq + 1)
                                rinc(sync.dma_start(out=out_rows[gq],
                                                 in_=out2_sb[:, gq % 2, :]), d_out8, gq)
                        for hb in range(NHB):
                            ghb = q * NHB + hb
                            _w(sync, p_fc, 2 * ghb - 2)
                            rinc(sync.dma_start(
                                out=sfc_sb[:, ghb % 2, :, :],
                                in_=sfcT_t[hb]), d_sfc, ghb)
                        for ct in range(NC):
                            gc = q * NC + ct
                            _w(sync, p_pj, gc - 1)
                            rinc(sync.dma_start(
                                out=spj_sb[:, gc % 2, :, :],
                                in_=spjT_t[ct]), d_spr, gc)
                    for tt in range(TT2):
                        gq = (NQ - 1) * TT2 + tt
                        sync.wait_ge(v_o8, gq + 1)
                        rinc(sync.dma_start(out=out_rows[gq],
                                         in_=out2_sb[:, gq % 2, :]), d_out8, gq)
                    rwait_all(sync, d_out8, NT128)

                @block.tensor
                def _(tensor):
                    for q in range(NQ):
                        tensor.wait_ge(d_xn1T[q % 2], 16 * NC * (q // 2 + 1))
                        for ht in range(NH):
                            gh = q * NH + ht
                            ghb = q * NHB + ht // 2
                            rwait(tensor, d_sfc, ghb)
                            _w(tensor, s_rev, gh - 1)
                            for ct in range(NC):
                                ins = tensor.matmul(
                                    ps_fc[gh % 2][:, 0:CH2],
                                    sfc_sb[:, ghb % 2, ct,
                                           (ht % 2) * 128:(ht % 2 + 1) * 128],
                                    xn1T_sb[:, q % 2, ct, :],
                                    start=(ct == 0), stop=(ct == NC - 1))
                            ins.then_inc(p_fc, 1)
                        tensor.wait_ge(s_sqq, NH * (q + 1))
                        for ct in range(NC):
                            gc = q * NC + ct
                            rwait(tensor, d_spr, gc)
                            _w(tensor, s_yev, gc - 1)
                            for ht in range(NH):
                                ins = tensor.matmul(
                                    ps_pj[gc % 2][:, 0:CH2],
                                    spj_sb[:, gc % 2, ht, :],
                                    hfc_sb[:, ht, :],
                                    start=(ht == 0), stop=(ht == NH - 1))
                            ins.then_inc(p_pj, 1)
                        for tt in range(TT2):
                            gq = q * TT2 + tt
                            _w(tensor, g_xp8, gq - 1)
                            for ct in range(NC):
                                tensor.wait_ge(s_yev, q * NC + ct + 1)
                                tensor.transpose(
                                    ps_yT[gq % 2][:, ct * 128:(ct + 1) * 128],
                                    y_sb[:, ct, tt * 128:(tt + 1) * 128],
                                    idbf[:, :]).then_inc(p_yt, 1)

    return nc


def kernel(L0_attn_x, L0_attn_W, L0_attn_b, L0_ffn_w_fc, L0_ffn_w_proj):
    x = np.ascontiguousarray(np.asarray(L0_attn_x, dtype=np.float32))
    W = np.ascontiguousarray(np.asarray(L0_attn_W, dtype=np.float32))
    b = np.ascontiguousarray(np.asarray(L0_attn_b, dtype=np.float32))
    wfc = np.ascontiguousarray(np.asarray(L0_ffn_w_fc, dtype=np.float32))
    wpj = np.ascontiguousarray(np.asarray(L0_ffn_w_proj, dtype=np.float32))

    nc = build_program(FULL_CFG)
    ND = FULL_CFG["D"] // 128
    bT = np.ascontiguousarray(b.reshape(ND, 128).T)
    id32 = np.eye(128, dtype=np.float32)
    idbf = np.eye(128, dtype=ml_dtypes.bfloat16)
    in_maps = []
    for i in range(8):
        shard = np.ascontiguousarray(x[2 * i:2 * i + 2].reshape(4096, 2048))
        in_maps.append(dict(x=shard, W=W, bT=bT, wfc=wfc, wpj=wpj,
                            id32=id32, idbf=idbf))
    res = run_bass_kernel_spmd(nc, in_maps, core_ids=list(range(8)))
    outs = [np.asarray(r["out"]).reshape(2, 2048, 2048) for r in res.results]
    return np.concatenate(outs, axis=0)
